# version 04
# baseline (speedup 1.0000x reference)
import numpy as np
import ml_dtypes
import concourse.bass as bass
import concourse.mybir as mybir
from concourse.bass_utils import run_bass_kernel_spmd
from concourse.alu_op_type import AluOpType

F32 = mybir.dt.float32
BF16 = mybir.dt.bfloat16
AF = mybir.ActivationFunctionType

D_MODEL = 1024
D_INNER = 2048
D_STATE = 16
DT_RANK = 64
D_CONV = 4
NCH_M = D_MODEL // 128
NCH_I = D_INNER // 128
EPS = 1e-5


class Sem:
    __slots__ = ("h", "v")
    def __init__(self, h):
        self.h = h
        self.v = 0


def build_program(T, TB, debug=False):
    NB = T // TB
    assert TB == 512
    nc = bass.Bass("TRN2", target_bir_lowering=False)

    dr = lambda n, sh, dt=F32: nc.dram_tensor(n, sh, dt, kind="ExternalInput").ap()
    xT = dr("xT", [D_MODEL, T])
    W_inT = dr("W_inT", [D_MODEL, 2 * D_INNER])
    conv_w = dr("conv_wc", [128, NCH_I * D_CONV])
    conv_b = dr("conv_bc", [128, NCH_I])
    A_log = dr("A_log_r", [1, D_STATE])
    D_in = dr("D_c", [128, NCH_I])
    W_xT = dr("W_xT", [D_INNER, 96])
    W_dtT = dr("W_dtT", [DT_RANK, D_INNER])
    b_dt = dr("b_dtc", [128, NCH_I])
    W_outT = dr("W_outT", [D_INNER, D_MODEL])
    W_rT = dr("W_rT", [D_MODEL, D_MODEL])
    W_kT = dr("W_kT", [D_MODEL, D_MODEL])
    W_vT = dr("W_vT", [D_MODEL, D_MODEL])
    mu_r = dr("mu_rc", [128, NCH_M])
    mu_k = dr("mu_kc", [128, NCH_M])
    eye_in = dr("eye_in", [128, 128], BF16)

    out = nc.dram_tensor("out", [D_MODEL, T], F32, kind="ExternalOutput").ap()
    dbg = {}
    if debug:
        for nm, rows_ in (("d_ln", D_MODEL), ("d_xs", D_INNER), ("d_dt", D_INNER),
                          ("d_gy", D_INNER), ("d_x1", D_MODEL), ("d_ln2", D_MODEL)):
            dbg[nm] = nc.dram_tensor(nm, [rows_, T], F32, kind="ExternalOutput").ap()

    scr = lambda n, sh, dt=F32: nc.dram_tensor(n, sh, dt, kind="Internal").ap()
    W_inT_bf = scr("W_inT_bf", [D_MODEL, 2 * D_INNER], BF16)
    W_outT_bf = scr("W_outT_bf", [D_INNER, D_MODEL], BF16)
    W_rT_bf = scr("W_rT_bf", [D_MODEL, D_MODEL], BF16)
    W_kT_bf = scr("W_kT_bf", [D_MODEL, D_MODEL], BF16)
    W_vT_bf = scr("W_vT_bf", [D_MODEL, D_MODEL], BF16)
    W_dtT_bf = scr("W_dtT_bf", [DT_RANK, D_INNER], BF16)
    W_xT_bf = scr("W_xT_bf", [D_INNER, 96], BF16)
    bounce = [scr(f"bounce{p}", [4, TB]) for p in range(2)]
    bbc = [scr(f"bbc{p}", [32, TB], BF16) for p in range(2)]
    a_bounce = scr("a_bounce", [1, D_STATE])

    from contextlib import ExitStack
    _st = ExitStack()
    with _st:
        xt = _st.enter_context(nc.sbuf_tensor("xt", [128, NCH_M * TB], F32))
        xre = xt.ap()[:, 0:2 * TB]
        xsqa = _st.enter_context(nc.sbuf_tensor("xsqa", [128, TB], F32))
        xsqb = _st.enter_context(nc.sbuf_tensor("xsqb", [128, TB], F32))
        lnt = _st.enter_context(nc.sbuf_tensor("lnt", [128, NCH_M * TB], BF16))
        ln2t = _st.enter_context(nc.sbuf_tensor("ln2t", [128, NCH_M * TB], BF16))
        rows = _st.enter_context(nc.sbuf_tensor("rows", [1, 3 * TB], F32))
        mrep = _st.enter_context(nc.sbuf_tensor("mrep", [128, TB], BF16))
        rrep = _st.enter_context(nc.sbuf_tensor("rrep", [128, TB], BF16))
        xrt = _st.enter_context(nc.sbuf_tensor("xrt", [128, NCH_I * (TB + 3)], BF16))
        halo = _st.enter_context(nc.sbuf_tensor("halo", [128, NCH_I * 3], BF16))
        convacc = _st.enter_context(nc.sbuf_tensor("convacc", [128, 2 * TB], F32))
        szt = _st.enter_context(nc.sbuf_tensor("szt", [128, NCH_I * TB], BF16))
        dtt = _st.enter_context(nc.sbuf_tensor("dtt", [128, NCH_I * TB], BF16))
        xdbl = _st.enter_context(nc.sbuf_tensor("xdbl", [96, TB], BF16))
        dtr = None
        brep = _st.enter_context(nc.sbuf_tensor("brep", [128, D_STATE * TB], BF16))
        crep = _st.enter_context(nc.sbuf_tensor("crep", [128, D_STATE * TB], BF16))
        vch = _st.enter_context(nc.sbuf_tensor("vch", [128, TB], BF16))
        dxsb = _st.enter_context(nc.sbuf_tensor("dxsb", [128, TB], BF16))
        abuf = _st.enter_context(nc.sbuf_tensor("abuf", [128, 2 * TB], F32))
        ube = _st.enter_context(nc.sbuf_tensor("ube", [128, TB], BF16))
        ubo = _st.enter_context(nc.sbuf_tensor("ubo", [128, TB], BF16))
        hbe = _st.enter_context(nc.sbuf_tensor("hbe", [128, TB], BF16))
        hbo = _st.enter_context(nc.sbuf_tensor("hbo", [128, TB], BF16))
        chb4 = [_st.enter_context(nc.sbuf_tensor(f"chb{i}", [128, TB], BF16))
                for i in range(2)]
        ytmp = _st.enter_context(nc.sbuf_tensor("ytmp", [128, TB], F32))
        stce = _st.enter_context(nc.sbuf_tensor("stce", [128, NCH_I * 8], F32))
        stco = _st.enter_context(nc.sbuf_tensor("stco", [128, NCH_I * 8], F32))
        gat2 = [_st.enter_context(nc.sbuf_tensor(f"gat{i}", [128, NCH_I * TB], BF16))
                for i in range(2)]
        x1t = _st.enter_context(nc.sbuf_tensor("x1t", [128, NCH_M * TB], F32))
        ln2p = _st.enter_context(nc.sbuf_tensor("ln2p", [128, TB], BF16))
        carry = _st.enter_context(nc.sbuf_tensor("carry", [128, NCH_M], BF16))
        ffntmp = _st.enter_context(nc.sbuf_tensor("ffntmp", [128, TB], BF16))
        ffntmp2 = _st.enter_context(nc.sbuf_tensor("ffntmp2", [128, TB], BF16))
        x2t = _st.enter_context(nc.sbuf_tensor("x2t", [128, TB], F32))
        wslab = _st.enter_context(nc.sbuf_tensor("wslab", [128, 2 * 16 * 128], BF16))
        cwc = _st.enter_context(nc.sbuf_tensor("cwc", [128, NCH_I * D_CONV], F32))
        cbc = _st.enter_context(nc.sbuf_tensor("cbc", [128, NCH_I], F32))
        bdtc = _st.enter_context(nc.sbuf_tensor("bdtc", [128, NCH_I], F32))
        dc = _st.enter_context(nc.sbuf_tensor("dc", [128, NCH_I], F32))
        murc = _st.enter_context(nc.sbuf_tensor("murc", [128, NCH_M], F32))
        mukc = _st.enter_context(nc.sbuf_tensor("mukc", [128, NCH_M], F32))
        omrc = _st.enter_context(nc.sbuf_tensor("omrc", [128, NCH_M], F32))
        omkc = _st.enter_context(nc.sbuf_tensor("omkc", [128, NCH_M], F32))
        arow = _st.enter_context(nc.sbuf_tensor("arow", [1, D_STATE], F32))
        abc = _st.enter_context(nc.sbuf_tensor("abc", [128, D_STATE], F32))
        eyeb = _st.enter_context(nc.sbuf_tensor("eyeb", [128, 128], BF16))
        onesf = _st.enter_context(nc.sbuf_tensor("onesf", [128, 1], F32))
        rt = dtt.ap()[:, 0:NCH_M * TB]
        kat = dtt.ap()[:, NCH_M * TB:2 * NCH_M * TB]
        ps_a = _st.enter_context(nc.psum_tensor("ps_a", [128, 512], F32))
        ps_b = _st.enter_context(nc.psum_tensor("ps_b", [128, 512], F32))
        ps_c = _st.enter_context(nc.psum_tensor("ps_c", [128, 512], F32))
        ps_d = _st.enter_context(nc.psum_tensor("ps_d", [128, 512], F32))
        ps_y0 = _st.enter_context(nc.psum_tensor("ps_y0", [128, 512], F32))
        ps_y1 = _st.enter_context(nc.psum_tensor("ps_y1", [128, 512], F32))
        ps_s1 = _st.enter_context(nc.psum_tensor("ps_s1", [1, 512], F32))
        ps_s2 = _st.enter_context(nc.psum_tensor("ps_s2", [1, 512], F32))
        _q_w = _st.enter_context(nc.semaphore("q_w"))
        _q_w2 = _st.enter_context(nc.semaphore("q_w2"))
        _q_w3 = _st.enter_context(nc.semaphore("q_w3"))
        _q_x = _st.enter_context(nc.semaphore("q_x"))
        _q_g = _st.enter_context(nc.semaphore("q_g"))
        _q_o = _st.enter_context(nc.semaphore("q_o"))
        _q_o2 = _st.enter_context(nc.semaphore("q_o2"))
        _s_T = _st.enter_context(nc.semaphore("s_T"))
        _s_V = _st.enter_context(nc.semaphore("s_V"))
        _s_S = _st.enter_context(nc.semaphore("s_S"))
        _s_G = _st.enter_context(nc.semaphore("s_G"))
        block = _st.enter_context(nc.Block())
        q_w, q_x, q_g, q_o = Sem(_q_w), Sem(_q_x), Sem(_q_g), Sem(_q_o)
        q_w2, q_w3, q_o2 = Sem(_q_w2), Sem(_q_w3), Sem(_q_o2)
        q_ws = [q_w, q_w2, q_w3]
        q_os = [q_o, q_o2]
        s_T, s_V, s_S, s_G = Sem(_s_T), Sem(_s_V), Sem(_s_S), Sem(_s_G)

        class _Pending:
            __slots__ = ("incs",)
            def __init__(self):
                self.incs = []
            def then_inc(self, sem, n):
                self.incs.append((sem, n))
                return self

        def _nm(x):
            try:
                return x.tensor.name
            except AttributeError:
                return None

        _IOMAP = {
            "tensor_tensor": lambda a, kw: ((a[1], a[2]), (a[0],)),
            "tensor_scalar": lambda a, kw: ((a[1], a[2], a[3]), (a[0],)),
            "scalar_tensor_tensor": lambda a, kw: ((a[1], a[2], a[3]), (a[0],)),
            "tensor_tensor_scan": lambda a, kw: ((a[1], a[2], a[3]), (a[0],)),
            "tensor_copy": lambda a, kw: ((a[1],), (a[0],)),
            "memset": lambda a, kw: ((), (a[0],)),
            "activation": lambda a, kw: (
                (kw.get("in_"), kw.get("bias"), kw.get("scale")),
                (kw.get("out"),)),
            "matmul": lambda a, kw: ((a[1], a[2]), ()),
        }

        class _Rec:
            def __init__(self, sem=None):
                self.ops = []
                self.sem = sem
                self.waited = 0
                self.lw = {}
                self.lr = {}

            def _auto(self, meth, a, kw):
                if self.sem is None:
                    return
                if meth == "wait_ge":
                    if a and a[0] is self.sem.h and a[1] > self.waited:
                        self.waited = a[1]
                    return
                io = _IOMAP.get(meth)
                if io is None:
                    return
                rd, wr = io(a, kw)
                need = 0
                for x in rd:
                    n = _nm(x)
                    if n:
                        need = max(need, self.lw.get(n, 0))
                for x in wr:
                    n = _nm(x)
                    if n:
                        need = max(need, self.lw.get(n, 0), self.lr.get(n, 0))
                if need > self.waited:
                    self.ops.append(("wait_ge", (self.sem.h, need), {}, _Pending()))
                    self.waited = need
                cnt = self.sem.v + 1
                for x in rd:
                    n = _nm(x)
                    if n:
                        self.lr[n] = cnt
                for x in wr:
                    n = _nm(x)
                    if n:
                        self.lw[n] = cnt

            def __getattr__(self, meth):
                def call(*a, **kw):
                    self._auto(meth, a, kw)
                    p = _Pending()
                    self.ops.append((meth, a, kw, p))
                    return p
                return call

            def replay(self, eng):
                for meth, a, kw, p in self.ops:
                    instr = getattr(eng, meth)(*a, **kw)
                    for sem, n in p.incs:
                        instr.then_inc(sem, n)

        sy = _Rec()
        g = _Rec(s_G)
        t, v, s = _Rec(s_T), _Rec(s_V), _Rec(s_S)
        PS = [ps_a, ps_b, ps_c, ps_d]

        def dma(eng, q, **kw):
            eng.dma_start(**kw).then_inc(q.h, 16)
            q.v += 16

        def vi(instr):
            instr.then_inc(s_V.h, 1)
            s_V.v += 1

        def si(instr):
            instr.then_inc(s_S.h, 1)
            s_S.v += 1

        def ti(instr):
            instr.then_inc(s_T.h, 1)
            s_T.v += 1

        def gi(instr):
            instr.then_inc(s_G.h, 1)
            s_G.v += 1

        dma(sy, q_x, out=cwc[:], in_=conv_w[:])
        dma(sy, q_x, out=cbc[:], in_=conv_b[:])
        dma(sy, q_x, out=bdtc[:], in_=b_dt[:])
        dma(sy, q_x, out=dc[:], in_=D_in[:])
        dma(sy, q_x, out=murc[:], in_=mu_r[:])
        dma(sy, q_x, out=mukc[:], in_=mu_k[:])
        dma(sy, q_x, out=arow[:], in_=A_log[:])
        dma(sy, q_x, out=eyeb[:], in_=eye_in[:])
        s.wait_ge(q_x.h, q_x.v)
        si(s.activation(out=arow[:], in_=arow[:], func=AF.Exp))
        v.wait_ge(s_S.h, s_S.v)
        vi(v.tensor_scalar(arow[:], arow[:], -1.0, None, AluOpType.mult))
        g.wait_ge(s_V.h, s_V.v)
        dma(g, q_g, out=a_bounce[:], in_=arow[:])
        g.wait_ge(q_g.h, q_g.v)
        dma(g, q_g, out=abc[:], in_=a_bounce.broadcast_to((128, D_STATE)))

        v.wait_ge(q_x.h, q_x.v)
        vi(v.memset(onesf[:], 1.0))
        vi(v.memset(stce[:], 0.0))
        vi(v.memset(stco[:], 0.0))
        vi(v.memset(halo[:], 0.0))
        vi(v.memset(carry[:], 0.0))
        vi(v.tensor_scalar(omrc[:], murc[:], -1.0, 1.0, AluOpType.mult, AluOpType.add))
        vi(v.tensor_scalar(omkc[:], mukc[:], -1.0, 1.0, AluOpType.mult, AluOpType.add))

        def cast_weight(src, dst, R, C):
            for r0 in range(0, R, 128):
                sy.wait_ge(q_g.h, q_g.v)
                dma(sy, q_w, out=xt[:, :C], in_=src[r0:r0 + 128, :])
                v.wait_ge(q_w.h, q_w.v)
                vi(v.tensor_copy(gat2[0][:, :C], xt[:, :C]))
                g.wait_ge(s_V.h, s_V.v)
                dma(g, q_g, out=dst[r0:r0 + 128, :], in_=gat2[0][:, :C])

        cast_weight(W_inT, W_inT_bf, D_MODEL, 2 * D_INNER)
        cast_weight(W_outT, W_outT_bf, D_INNER, D_MODEL)
        cast_weight(W_rT, W_rT_bf, D_MODEL, D_MODEL)
        cast_weight(W_kT, W_kT_bf, D_MODEL, D_MODEL)
        cast_weight(W_vT, W_vT_bf, D_MODEL, D_MODEL)
        cast_weight(W_xT, W_xT_bf, D_INNER, 96)
        sy.wait_ge(q_g.h, q_g.v)
        dma(sy, q_w, out=xt[0:64, :D_INNER], in_=W_dtT[:, :])
        v.wait_ge(q_w.h, q_w.v)
        vi(v.tensor_copy(gat2[0][0:64, :D_INNER], xt[0:64, :D_INNER]))
        g.wait_ge(s_V.h, s_V.v)
        dma(g, q_g, out=W_dtT_bf[:, :], in_=gat2[0][0:64, :D_INNER])
        g.wait_ge(q_g.h, q_g.v)

        SLABW = 16 * 128
        slab_guard = [0, 0]
        slab_cnt = [0, 0]
        slab_rot = [0]

        def slab_load(src_bf, R, m0):
            sl = slab_rot[0]
            slab_rot[0] = (sl + 1) % 2
            nk = R // 128
            sy.wait_ge(s_T.h, slab_guard[sl])
            dma(sy, q_ws[sl],
                out=wslab[:, sl * SLABW: sl * SLABW + nk * 128]
                .rearrange("p (k c) -> p k c", c=128),
                in_=src_bf[:, m0 * 128:(m0 + 1) * 128]
                .rearrange("(k p) c -> p k c", p=128))
            slab_cnt[sl] = q_ws[sl].v
            return sl, q_ws[sl].v

        def slab_seq(loader, ms, body):
            nxt = loader(ms[0])
            for i, m in enumerate(ms):
                sl, cnt = nxt
                if i + 1 < len(ms):
                    nxt = loader(ms[i + 1])
                t.wait_ge(q_ws[sl].h, cnt)
                body(sl, m)

        def slab_load_dt():
            sl = slab_rot[0]
            slab_rot[0] = (sl + 1) % 2
            sy.wait_ge(s_T.h, slab_guard[sl])
            dma(sy, q_ws[sl], out=wslab[0:64, sl * SLABW: sl * SLABW + D_INNER],
                in_=W_dtT_bf[:, :])
            slab_cnt[sl] = q_ws[sl].v
            return sl, q_ws[sl].v

        def xs_c(c):
            return xrt[:, c * (TB + 3) + 3:(c + 1) * (TB + 3)]

        scan_g = {}
        ch_g = {}
        mm_g = {}
        ytmp_g = {}
        gate_g = {}
        vchfree = {}
        silu_g = {}
        zsilu_g = {}
        dt_g = {}
        lnap_g = {}
        x1_g = {}
        mix_g = {}
        dtr_g = {}
        brep_q = {}
        gatdone_g = {}
        rk_g = {}
        wv_g = {}
        winev_g = {}
        out_cnt = {0: 0, 1: 0}
        ln_ctr = [0]
        conv_prev = [0]

        def ln_stats(src_t):
            par = ln_ctr[0] & 1
            ln_ctr[0] += 1
            t.wait_ge(s_V.h, s_V.v)
            for c in range(NCH_M):
                sl = src_t[:, c * TB:(c + 1) * TB]
                xsq = xsqa if (c & 1) == 0 else xsqb
                s.wait_ge(s_T.h, s_T.v)
                si(s.activation(out=xsq[:], in_=sl, func=AF.Square))
                t.wait_ge(s_S.h, s_S.v)
                ti(t.matmul(ps_s1[:, :], onesf[:], sl, start=(c == 0), stop=(c == NCH_M - 1)))
                ti(t.matmul(ps_s2[:, :], onesf[:], xsq[:], start=(c == 0), stop=(c == NCH_M - 1)))
            v.wait_ge(s_T.h, s_T.v)
            r_mean = rows[0:1, 0:TB]
            r_q = rows[0:1, TB:2 * TB]
            r_var = rows[0:1, 2 * TB:3 * TB]
            vi(v.tensor_scalar(r_mean, ps_s1[0:1, :], 1.0 / D_MODEL, None, AluOpType.mult))
            vi(v.tensor_scalar(r_q, ps_s2[0:1, :], 1.0 / D_MODEL, None, AluOpType.mult))
            vi(v.tensor_tensor(r_var, r_mean, r_mean, AluOpType.mult))
            vi(v.tensor_tensor(r_var, r_q, r_var, AluOpType.subtract))
            vi(v.tensor_scalar(r_var, r_var, EPS, None, AluOpType.add))
            s.wait_ge(s_V.h, s_V.v)
            si(s.activation(out=r_var, in_=r_var, func=AF.Ln))
            si(s.activation(out=r_var, in_=r_var, func=AF.Exp, scale=-0.5))
            g.wait_ge(s_V.h, s_V.v)
            g.wait_ge(s_S.h, s_S.v)
            dma(g, q_g, out=bounce[par][0:1, :], in_=r_mean)
            dma(g, q_g, out=bounce[par][1:2, :], in_=r_var)
            g.wait_ge(q_g.h, q_g.v)
            dma(g, q_g, out=mrep[:], in_=bounce[par][0:1, :].broadcast_to((128, TB)))
            dma(g, q_g, out=rrep[:], in_=bounce[par][1:2, :].broadcast_to((128, TB)))

        def ln_apply(src_t, dst_t):
            v.wait_ge(q_g.h, q_g.v)
            for c in range(NCH_M):
                sl = src_t[:, c * TB:(c + 1) * TB]
                dl = dst_t[:, c * TB:(c + 1) * TB]
                acc = convacc[:, (c & 1) * TB:((c & 1) + 1) * TB]
                vi(v.tensor_tensor(acc, sl, mrep[:], AluOpType.subtract))
                vi(v.tensor_tensor(dl, acc, rrep[:], AluOpType.mult))

        def fr_load_xt(j):
            def f():
                sy.wait_ge(s_V.h, s_V.v)
                sy.wait_ge(s_S.h, s_S.v)
                sy.wait_ge(s_T.h, s_T.v)
                t0j = j * TB
                for c in range(NCH_M):
                    dma(sy, q_x, out=xt[:, c * TB:(c + 1) * TB],
                        in_=xT[c * 128:(c + 1) * 128, t0j:t0j + TB])
            return f

        def fr_ln1_stats(j):
            def f():
                s.wait_ge(q_x.h, q_x.v)
                t.wait_ge(q_x.h, q_x.v)
                ln_stats(xt)
            return f

        def fr_ln1_apply(j):
            def f():
                ln_apply(xt, lnt)
                lnap_g[("1", j)] = s_V.v
                if debug:
                    g.wait_ge(s_V.h, s_V.v)
                    for c in range(NCH_M):
                        dma(g, q_g, out=dbg["d_ln"][c * 128:(c + 1) * 128,
                                                    j * TB:(j + 1) * TB],
                            in_=lnt[:, c * TB:(c + 1) * TB])
            return f

        def fr_win_xr(j, ms, k_cur):
            def f():
                t.wait_ge(s_V.h, lnap_g[("1", j)])
                t.wait_ge(s_S.h, dt_g.get((j - 1, NCH_I - 1), 0))
                def body(sl, m):
                    ps = PS[2 + (m & 1)]
                    t.wait_ge(s_V.h, winev_g.get((j, m - 2), 0))
                    if k_cur is not None:
                        t.wait_ge(s_V.h, vchfree.get((k_cur, m), 0))
                    for kk in range(NCH_M):
                        ti(t.matmul(ps[:, :],
                                    wslab[:, sl * SLABW + kk * 128: sl * SLABW + (kk + 1) * 128],
                                    lnt[:, kk * TB:(kk + 1) * TB],
                                    start=(kk == 0), stop=(kk == NCH_M - 1)))
                    slab_guard[sl] = s_T.v
                    v.wait_ge(s_T.h, s_T.v)
                    vi(v.tensor_copy(xrt[:, m * (TB + 3) + 3:(m + 1) * (TB + 3)], ps[:, :]))
                    winev_g[(j, m)] = s_V.v
                slab_seq(lambda m: slab_load(W_inT_bf, D_MODEL, m), ms, body)
            return f

        def fr_win_z(j, js, k_cur):
            def f():
                t.wait_ge(s_V.h, lnap_g[("1", j)])
                def body(sl, jz):
                    t.wait_ge(s_S.h, zsilu_g.get((j, jz - 1),
                                                 zsilu_g.get((j - 1, NCH_I - 1), 0)))
                    if k_cur is not None:
                        t.wait_ge(s_V.h, gate_g.get((k_cur, jz), 0))
                    for kk in range(NCH_M):
                        ti(t.matmul(ps_y1[:, :],
                                    wslab[:, sl * SLABW + kk * 128: sl * SLABW + (kk + 1) * 128],
                                    lnt[:, kk * TB:(kk + 1) * TB],
                                    start=(kk == 0), stop=(kk == NCH_M - 1)))
                    slab_guard[sl] = s_T.v
                    s.wait_ge(s_T.h, s_T.v)
                    si(s.activation(out=szt[:, jz * TB:(jz + 1) * TB], in_=ps_y1[:, :],
                                    func=AF.Silu))
                    zsilu_g[(j, jz)] = s_S.v
                slab_seq(lambda jz: slab_load(W_inT_bf, D_MODEL, NCH_I + jz), js, body)
            return f

        def fr_conv(j, cs):
            def f():
                for c in cs:
                    xr_ch = xrt[:, c * (TB + 3):(c + 1) * (TB + 3)]
                    acc0 = convacc[:, 0:TB]
                    acc1 = convacc[:, TB:2 * TB]
                    v.wait_ge(s_S.h, conv_prev[0])
                    vi(v.tensor_copy(xr_ch[:, 0:3], halo[:, c * 3:(c + 1) * 3]))
                    vi(v.tensor_scalar(acc0, xr_ch[:, 0:TB], cwc[:, c * 4:c * 4 + 1],
                                       None, AluOpType.mult))
                    vi(v.scalar_tensor_tensor(acc1, xr_ch[:, 1:1 + TB],
                                              cwc[:, c * 4 + 1:c * 4 + 2], acc0,
                                              AluOpType.mult, AluOpType.add))
                    vi(v.scalar_tensor_tensor(acc0, xr_ch[:, 2:2 + TB],
                                              cwc[:, c * 4 + 2:c * 4 + 3], acc1,
                                              AluOpType.mult, AluOpType.add))
                    vi(v.scalar_tensor_tensor(acc1, xr_ch[:, 3:3 + TB],
                                              cwc[:, c * 4 + 3:c * 4 + 4], acc0,
                                              AluOpType.mult, AluOpType.add))
                    vi(v.tensor_copy(halo[:, c * 3:(c + 1) * 3], xr_ch[:, TB:TB + 3]))
                    s.wait_ge(s_V.h, s_V.v)
                    si(s.activation(out=xr_ch[:, 3:3 + TB], in_=acc1, func=AF.Silu,
                                    bias=cbc[:, c:c + 1]))
                    silu_g[(j, c)] = s_S.v
                    conv_prev[0] = s_S.v
            return f

        def fr_wx_wdt(j):
            def f():
                slx = slab_rot[0]
                slab_rot[0] = (slx + 1) % 2
                sy.wait_ge(s_T.h, slab_guard[slx])
                dma(sy, q_ws[slx],
                    out=wslab[:, slx * SLABW: slx * SLABW + NCH_I * 96]
                    .rearrange("p (k c) -> p k c", c=96),
                    in_=W_xT_bf.rearrange("(k p) c -> p k c", p=128))
                slab_cnt[slx] = q_ws[slx].v
                t.wait_ge(s_S.h, silu_g[(j, NCH_I - 1)])
                t.wait_ge(s_S.h, dt_g.get((j - 1, NCH_I - 1), 0))
                t.wait_ge(s_V.h, s_V.v)
                t.wait_ge(q_ws[slx].h, slab_cnt[slx])
                for kk in range(NCH_I):
                    ti(t.matmul(ps_c[0:96, :],
                                wslab[:, slx * SLABW + kk * 96: slx * SLABW + (kk + 1) * 96],
                                xs_c(kk),
                                start=(kk == 0), stop=(kk == NCH_I - 1)))
                slab_guard[slx] = s_T.v
                v.wait_ge(s_T.h, s_T.v)
                v.wait_ge(q_g.h, q_g.v)
                vi(v.tensor_copy(xdbl[:, :], ps_c[0:96, :]))
                dtr_g[j] = s_V.v
                par = j & 1
                g.wait_ge(s_V.h, s_V.v)
                g.wait_ge(q_g.h, q_g.v)
                dma(g, q_g, out=bbc[par][:, :], in_=xdbl[64:96, :])
                g.wait_ge(q_g.h, q_g.v)
                for si_ in range(D_STATE):
                    dma(g, q_g, out=brep[:, si_ * TB:(si_ + 1) * TB],
                        in_=bbc[par][si_:si_ + 1, :].broadcast_to((128, TB)))
                    dma(g, q_g, out=crep[:, si_ * TB:(si_ + 1) * TB],
                        in_=bbc[par][16 + si_:17 + si_, :].broadcast_to((128, TB)))
                brep_q[j] = q_g.v
                sl, _cnt = slab_load_dt()
                t.wait_ge(s_V.h, s_V.v)
                t.wait_ge(q_ws[sl].h, _cnt)
                for m in range(NCH_I):
                    ps = PS[2 + (m & 1)]
                    xsq = xsqb if (m & 1) == 0 else xsqa
                    if m >= 2:
                        t.wait_ge(s_S.h, dt_g[(j, m - 2)])
                    ti(t.matmul(ps[:, :], wslab[0:64, sl * SLABW + m * 128:
                                                sl * SLABW + (m + 1) * 128],
                                xdbl[0:64, :], start=True, stop=True))
                    s.wait_ge(s_T.h, s_T.v)
                    si(s.activation(out=xsq[:], in_=ps[:, :], func=AF.Exp,
                                    bias=bdtc[:, m:m + 1]))
                    si(s.activation(out=dtt[:, m * TB:(m + 1) * TB], in_=xsq[:],
                                    func=AF.Ln, bias=1.0))
                    dt_g[(j, m)] = s_S.v
                slab_guard[sl] = s_T.v
            return f

        def bk_wout(b, ms):
            def f():
                t0b = b * TB
                sy.wait_ge(s_V.h, s_V.v)
                for m in ms:
                    dma(sy, q_x, out=xre[:, (m & 1) * TB:((m & 1) + 1) * TB],
                        in_=xT[m * 128:(m + 1) * 128, t0b:t0b + TB])
                xq = q_x.v
                t.wait_ge(s_V.h, gatdone_g[b])
                t.wait_ge(s_V.h, wv_g.get(b - 1, 0))
                def body(sl, m):
                    ps = PS[m & 1]
                    t.wait_ge(s_V.h, x1_g.get((b, m - 2), 0))
                    for kk in range(NCH_I):
                        ti(t.matmul(ps[:, :],
                                    wslab[:, sl * SLABW + kk * 128: sl * SLABW + (kk + 1) * 128],
                                    gat2[b & 1][:, kk * TB:(kk + 1) * TB],
                                    start=(kk == 0), stop=(kk == NCH_I - 1)))
                    slab_guard[sl] = s_T.v
                    v.wait_ge(s_T.h, s_T.v)
                    v.wait_ge(q_x.h, xq)
                    vi(v.tensor_tensor(x1t[:, m * TB:(m + 1) * TB], ps[:, :],
                                       xre[:, (m & 1) * TB:((m & 1) + 1) * TB],
                                       AluOpType.add))
                    x1_g[(b, m)] = s_V.v
                slab_seq(lambda m: slab_load(W_outT_bf, D_INNER, m), ms, body)
                if debug and ms[-1] == NCH_M - 1:
                    g.wait_ge(s_V.h, s_V.v)
                    for c in range(NCH_M):
                        dma(g, q_g, out=dbg["d_x1"][c * 128:(c + 1) * 128,
                                                    b * TB:(b + 1) * TB],
                            in_=x1t[:, c * TB:(c + 1) * TB])
            return f

        def bk_ln2_stats(b):
            def f():
                ln_stats(x1t)
            return f

        def bk_ln2_apply(b):
            def f():
                ln_apply(x1t, ln2t)
                lnap_g[("2", b)] = s_V.v
                if debug:
                    g.wait_ge(s_V.h, s_V.v)
                    for c in range(NCH_M):
                        dma(g, q_g, out=dbg["d_ln2"][c * 128:(c + 1) * 128,
                                                     b * TB:(b + 1) * TB],
                            in_=ln2t[:, c * TB:(c + 1) * TB])
            return f

        def bk_ffn_mix(b, cs):
            xft = gat2[b & 1].ap()[:, 0:NCH_M * TB]
            xkt = gat2[b & 1].ap()[:, NCH_M * TB:2 * NCH_M * TB]
            def f():
                for c in cs:
                    l2 = ln2t[:, c * TB:(c + 1) * TB]
                    vi(v.tensor_copy(ln2p[:, 0:1], carry[:, c:c + 1]))
                    vi(v.tensor_copy(ln2p[:, 1:TB], l2[:, 0:TB - 1]))
                    vi(v.tensor_copy(carry[:, c:c + 1], l2[:, TB - 1:TB]))
                    xf = xft[:, c * TB:(c + 1) * TB]
                    xk = xkt[:, c * TB:(c + 1) * TB]
                    vi(v.tensor_scalar(xf, l2, murc[:, c:c + 1], None, AluOpType.mult))
                    vi(v.tensor_scalar(xk, l2, mukc[:, c:c + 1], None, AluOpType.mult))
                    vi(v.scalar_tensor_tensor(ffntmp[:], ln2p[:], omrc[:, c:c + 1], xf,
                                              AluOpType.mult, AluOpType.add))
                    vi(v.scalar_tensor_tensor(ffntmp2[:], ln2p[:], omkc[:, c:c + 1], xk,
                                              AluOpType.mult, AluOpType.add))
                    vi(v.tensor_copy(xf, ffntmp[:]))
                    vi(v.tensor_copy(xk, ffntmp2[:]))
                mix_g[b] = s_V.v
            return f

        def bk_wrk(b, which, ms):
            wt = W_rT_bf if which == "r" else W_kT_bf
            off = 0 if which == "r" else NCH_M * TB
            def f():
                src = gat2[b & 1].ap()[:, off:off + NCH_M * TB]
                t.wait_ge(s_V.h, mix_g[b])
                def body(sl, m):
                    ps = PS[m & 1]
                    if which == "r":
                        t.wait_ge(s_V.h, x1_g[(b, 6 + (m & 1))])
                        t.wait_ge(s_S.h, rk_g.get((b, "r", m - 2), 0))
                    else:
                        prev = rk_g.get((b, "k", m - 2), rk_g.get((b, "r", 6 + (m & 1)), 0))
                        t.wait_ge(s_S.h, prev)
                    for kk in range(NCH_M):
                        ti(t.matmul(ps[:, :],
                                    wslab[:, sl * SLABW + kk * 128: sl * SLABW + (kk + 1) * 128],
                                    src[:, kk * TB:(kk + 1) * TB],
                                    start=(kk == 0), stop=(kk == NCH_M - 1)))
                    slab_guard[sl] = s_T.v
                    s.wait_ge(s_T.h, s_T.v)
                    if which == "r":
                        si(s.activation(out=rt[:, m * TB:(m + 1) * TB], in_=ps[:, :],
                                        func=AF.Sigmoid))
                    else:
                        xsq = xsqa if (m & 1) == 0 else xsqb
                        si(s.activation(out=xsq[:], in_=ps[:, :], func=AF.Relu))
                        si(s.activation(out=kat[:, m * TB:(m + 1) * TB], in_=xsq[:],
                                        func=AF.Square))
                    rk_g[(b, which, m)] = s_S.v
                slab_seq(lambda m: slab_load(wt, D_MODEL, m), ms, body)
            return f

        def bk_wv(b, ms):
            t0b = b * TB
            def f():
                t.wait_ge(s_S.h, rk_g[(b, "k", NCH_M - 1)])
                def body(sl, m):
                    ps = PS[m & 1]
                    t.wait_ge(s_S.h, rk_g.get((b, "k", 6 + (m & 1)), 0))
                    t.wait_ge(s_V.h, wv_g.get((b, m - 2), 0))
                    for kk in range(NCH_M):
                        ti(t.matmul(ps[:, :],
                                    wslab[:, sl * SLABW + kk * 128: sl * SLABW + (kk + 1) * 128],
                                    kat[:, kk * TB:(kk + 1) * TB],
                                    start=(kk == 0), stop=(kk == NCH_M - 1)))
                    slab_guard[sl] = s_T.v
                    v.wait_ge(s_T.h, s_T.v)
                    v.wait_ge(q_os[m & 1].h, out_cnt[m & 1])
                    x2b = x2t[:, 0:TB] if (m & 1) == 0 else ytmp[:, 0:TB]
                    vi(v.tensor_tensor(ffntmp[:], ps[:, :], rt[:, m * TB:(m + 1) * TB],
                                       AluOpType.mult))
                    vi(v.tensor_tensor(x2b, ffntmp[:], x1t[:, m * TB:(m + 1) * TB],
                                       AluOpType.add))
                    wv_g[(b, m)] = s_V.v
                    sy.wait_ge(s_V.h, s_V.v)
                    dma(sy, q_os[m & 1], out=out[m * 128:(m + 1) * 128, t0b:t0b + TB],
                        in_=x2b)
                    out_cnt[m & 1] = q_os[m & 1].v
                slab_seq(lambda m: slab_load(W_vT_bf, D_MODEL, m), ms, body)
                if ms[-1] == NCH_M - 1:
                    wv_g[b] = s_V.v
            return f

        UB = [ube, ubo]
        HB = [hbe, hbo]

        def scan_section(k, c):
            c_abs = k * NCH_I + c
            dt_ch = dtt[:, c * TB:(c + 1) * TB]
            v.wait_ge(s_S.h, dt_g[(k, c)])
            v.wait_ge(s_S.h, silu_g[(k, c)])
            if c == 0:
                v.wait_ge(q_g.h, brep_q[k])
                g.wait_ge(q_g.h, brep_q[k])
                v.wait_ge(q_os[1].h, out_cnt[1])
            vi(v.tensor_tensor(vch[:], dt_ch, xs_c(c), AluOpType.mult))
            vi(v.tensor_scalar(dxsb[:], xs_c(c), dc[:, c:c + 1], None, AluOpType.mult))
            vchfree[(k, c)] = s_V.v
            vi(v.tensor_tensor(UB[0][:], vch[:], brep[:, 0:TB], AluOpType.mult))

            def tail(j):
                kj = c_abs * D_STATE + j
                g.wait_ge(s_T.h, mm_g.get(kj - 2, 0))
                g.wait_ge(s_V.h, scan_g[kj])
                chb = chb4[j & 1]
                gi(g.tensor_tensor(chb[:], HB[j & 1][:], crep[:, j * TB:(j + 1) * TB],
                                   AluOpType.mult))
                ch_g[kj] = s_G.v
                stx = stce if (j & 1) == 0 else stco
                stcol = stx[:, c * 8 + (j >> 1): c * 8 + (j >> 1) + 1]
                vi(v.tensor_scalar(stcol, HB[j & 1][:, TB - 1:TB], 1.0, None,
                                   AluOpType.mult))
                if j == 0:
                    t.wait_ge(s_V.h, ytmp_g.get((k, c - 1) if c else (k - 1, NCH_I - 1), 0))
                t.wait_ge(s_G.h, ch_g[kj])
                ti(t.matmul(ps_y0[:, :], eyeb[:], chb[:],
                            start=(j == 0), stop=(j == D_STATE - 1)))
                mm_g[kj] = s_T.v

            for si_ in range(D_STATE):
                kj = c_abs * D_STATE + si_
                ab = abuf[:, (si_ & 1) * TB:((si_ & 1) + 1) * TB]
                s.wait_ge(s_V.h, scan_g.get(kj - 2, 0))
                si(s.activation(out=ab, in_=dt_ch, func=AF.Exp,
                                scale=abc[:, si_:si_ + 1]))
                ab_cnt = s_S.v
                if si_ + 1 < D_STATE:
                    vi(v.tensor_tensor(UB[(si_ + 1) & 1][:], vch[:],
                                       brep[:, (si_ + 1) * TB:(si_ + 2) * TB],
                                       AluOpType.mult))
                stx = stce if (si_ & 1) == 0 else stco
                stcol = stx[:, c * 8 + (si_ >> 1): c * 8 + (si_ >> 1) + 1]
                v.wait_ge(s_S.h, ab_cnt)
                v.wait_ge(s_G.h, ch_g.get(kj - 2, 0))
                scan_g[kj] = s_V.v + 1
                vi(v.tensor_tensor_scan(HB[si_ & 1][:], ab, UB[si_ & 1][:], stcol,
                                        AluOpType.mult, AluOpType.add))
                if si_ > 0:
                    tail(si_ - 1)
            tail(D_STATE - 1)
            v.wait_ge(s_T.h, mm_g[c_abs * D_STATE + D_STATE - 1])
            v.wait_ge(s_S.h, zsilu_g[(k, c)])
            vi(v.tensor_tensor(ytmp[:], ps_y0[:, :], dxsb[:], AluOpType.add))
            ytmp_g[(k, c)] = s_V.v
            vi(v.tensor_tensor(gat2[k & 1][:, c * TB:(c + 1) * TB], ytmp[:],
                               szt[:, c * TB:(c + 1) * TB], AluOpType.mult))
            gate_g[(k, c)] = s_V.v
            if c == NCH_I - 1:
                gatdone_g[k] = s_V.v
            if debug:
                g.wait_ge(s_V.h, s_V.v)
                t0k = k * TB
                dma(g, q_g, out=dbg["d_xs"][c * 128:(c + 1) * 128, t0k:t0k + TB],
                    in_=xs_c(c))
                dma(g, q_g, out=dbg["d_dt"][c * 128:(c + 1) * 128, t0k:t0k + TB],
                    in_=dt_ch)
                dma(g, q_g, out=dbg["d_gy"][c * 128:(c + 1) * 128, t0k:t0k + TB],
                    in_=gat2[k & 1][:, c * TB:(c + 1) * TB])

        def front_agenda(j, k_cur, ag, tailp):
            ag[4].append(fr_load_xt(j))
            ag[5].append(fr_ln1_stats(j))
            ag[6].append(fr_ln1_apply(j))
            ag[7].append(fr_win_xr(j, [0, 1, 2, 3], k_cur))
            ag[8].append(fr_win_xr(j, [4, 5, 6, 7], k_cur))
            ag[9].append(fr_win_xr(j, [8, 9], k_cur))
            for m in range(10, NCH_I):
                ag[m].append(fr_win_xr(j, [m], k_cur))
            ag[8].append(fr_win_z(j, [0, 1], k_cur))
            ag[9].append(fr_win_z(j, [2, 3], k_cur))
            ag[10].append(fr_win_z(j, [4, 5], k_cur))
            ag[11].append(fr_win_z(j, [6, 7], k_cur))
            ag[12].append(fr_win_z(j, [8, 9], k_cur))
            ag[13].append(fr_win_z(j, [10, 11], k_cur))
            ag[14].append(fr_win_z(j, [12, 13], k_cur))
            ag[15].append(fr_win_z(j, [14], k_cur))
            tailp.append(fr_win_z(j, [15], k_cur))
            for cp in range(0, 9):
                ag[cp + 7].append(fr_conv(j, [cp]))
            for cp in range(9, 15):
                ag[cp + 1].append(fr_conv(j, [cp]))
            tailp.append(fr_conv(j, [15]))
            tailp.append(fr_wx_wdt(j))

        fr_load_xt(0)()
        fr_ln1_stats(0)()
        fr_ln1_apply(0)()
        for m0 in range(0, NCH_I, 2):
            fr_win_xr(0, [m0, m0 + 1], None)()
        for c0 in range(0, NCH_I, 2):
            fr_conv(0, [c0, c0 + 1])()
        for j0 in range(0, NCH_I, 2):
            fr_win_z(0, [j0, j0 + 1], None)()
        fr_wx_wdt(0)()

        for k in range(NB):
            ag = {cc: [] for cc in range(NCH_I)}
            tailp = []
            b = k - 1
            if b >= 0:
                ag[0].append(bk_wout(b, [0, 1]))
                ag[1].append(bk_wout(b, [2, 3]))
                ag[2].append(bk_wout(b, [4, 5]))
                ag[3].append(bk_wout(b, [6, 7]))
                ag[4].append(bk_ln2_stats(b))
                ag[5].append(bk_ln2_apply(b))
                ag[6].append(bk_ffn_mix(b, [0, 1, 2, 3]))
                ag[7].append(bk_ffn_mix(b, [4, 5, 6, 7]))
                ag[8].append(bk_wrk(b, "r", [0, 1]))
                ag[9].append(bk_wrk(b, "r", [2, 3]))
                ag[10].append(bk_wrk(b, "r", [4, 5]))
                ag[11].append(bk_wrk(b, "r", [6, 7]))
                ag[12].append(bk_wrk(b, "k", [0, 1]))
                ag[13].append(bk_wrk(b, "k", [2, 3]))
                ag[14].append(bk_wrk(b, "k", [4, 5]))
                ag[15].append(bk_wrk(b, "k", [6, 7]))
                tailp.append(bk_wv(b, [0, 1]))
                tailp.append(bk_wv(b, [2, 3]))
                tailp.append(bk_wv(b, [4, 5]))
                tailp.append(bk_wv(b, [6, 7]))
            j = k + 1
            if j < NB:
                front_agenda(j, k, ag, tailp)

            for c in range(NCH_I):
                scan_section(k, c)
                for fn in ag[c]:
                    fn()
            for fn in tailp:
                fn()

        b = NB - 1
        for m0 in range(0, NCH_M, 2):
            bk_wout(b, [m0, m0 + 1])()
        bk_ln2_stats(b)()
        bk_ln2_apply(b)()
        bk_ffn_mix(b, list(range(NCH_M)))()
        for m0 in range(0, NCH_M, 2):
            bk_wrk(b, "r", [m0, m0 + 1])()
        for m0 in range(0, NCH_M, 2):
            bk_wrk(b, "k", [m0, m0 + 1])()
        for m0 in range(0, NCH_M, 2):
            bk_wv(b, [m0, m0 + 1])()

        g.wait_ge(q_o.h, q_o.v)
        g.wait_ge(q_o2.h, q_o2.v)
        if debug:
            g.wait_ge(q_g.h, q_g.v)

        @block.sync
        def _(e):
            sy.replay(e)

        @block.gpsimd
        def _(e):
            g.replay(e)

        @block.tensor
        def _(e):
            t.replay(e)

        @block.vector
        def _(e):
            v.replay(e)

        @block.scalar
        def _(e):
            s.replay(e)

    return nc


_PROG_CACHE = {}


def _get_prog(T, TB, debug=False):
    key = (T, TB, debug)
    if key not in _PROG_CACHE:
        _PROG_CACHE[key] = build_program(T, TB, debug)
    return _PROG_CACHE[key]


def make_in_map(b, T, L0_attn_x, L0_attn_W_in, L0_attn_conv_w, L0_attn_conv_b,
                L0_attn_A_log, L0_attn_D, L0_attn_W_x, L0_attn_W_dt, L0_attn_b_dt,
                L0_attn_W_out, L0_ffn_W_r, L0_ffn_W_k, L0_ffn_W_v, L0_ffn_mu_r,
                L0_ffn_mu_k):
    C = np.ascontiguousarray
    f = np.float32
    _bf16 = ml_dtypes.bfloat16
    return {
        "xT": C(np.asarray(L0_attn_x)[b, :T, :].T.astype(f)),
        "W_inT": C(np.asarray(L0_attn_W_in).T.astype(f)),
        "conv_wc": C(np.asarray(L0_attn_conv_w).reshape(NCH_I, 128, 4)
                     .transpose(1, 0, 2).reshape(128, NCH_I * 4).astype(f)),
        "conv_bc": C(np.asarray(L0_attn_conv_b).reshape(NCH_I, 128).T.astype(f)),
        "A_log_r": C(np.asarray(L0_attn_A_log).reshape(1, D_STATE).astype(f)),
        "D_c": C(np.asarray(L0_attn_D).reshape(NCH_I, 128).T.astype(f)),
        "W_xT": C(np.asarray(L0_attn_W_x).T.astype(f)),
        "W_dtT": C(np.asarray(L0_attn_W_dt).T.astype(f)),
        "b_dtc": C(np.asarray(L0_attn_b_dt).reshape(NCH_I, 128).T.astype(f)),
        "W_outT": C(np.asarray(L0_attn_W_out).T.astype(f)),
        "W_rT": C(np.asarray(L0_ffn_W_r).T.astype(f)),
        "W_kT": C(np.asarray(L0_ffn_W_k).T.astype(f)),
        "W_vT": C(np.asarray(L0_ffn_W_v).T.astype(f)),
        "mu_rc": C(np.asarray(L0_ffn_mu_r).reshape(NCH_M, 128).T.astype(f)),
        "mu_kc": C(np.asarray(L0_ffn_mu_k).reshape(NCH_M, 128).T.astype(f)),
        "eye_in": np.eye(128, dtype=_bf16),
    }


def kernel(**inputs):
    T = 8192
    nc = _get_prog(T, 512)
    in_maps = [make_in_map(c >> 1, T, **inputs) for c in range(8)]
    res = run_bass_kernel_spmd(nc, in_maps, core_ids=list(range(8)))
    B = 4
    outs = []
    for b in range(B):
        o = res.results[2 * b]["out"]
        outs.append(np.ascontiguousarray(o.T)[None])
    return np.concatenate(outs, axis=0)


# version 05
# speedup vs baseline: 1.1676x; 1.1676x over previous
import numpy as np
import ml_dtypes
import concourse.bass as bass
import concourse.mybir as mybir
from concourse.bass_utils import run_bass_kernel_spmd
from concourse.alu_op_type import AluOpType

F32 = mybir.dt.float32
BF16 = mybir.dt.bfloat16
AF = mybir.ActivationFunctionType

D_MODEL = 1024
D_INNER = 2048
D_STATE = 16
DT_RANK = 64
D_CONV = 4
NCH_M = D_MODEL // 128
NCH_I = D_INNER // 128
EPS = 1e-5


class Sem:
    __slots__ = ("h", "v")
    def __init__(self, h):
        self.h = h
        self.v = 0


def build_program(T, TB, debug=False):
    NB = T // TB
    assert TB == 512
    nc = bass.Bass("TRN2", target_bir_lowering=False)

    dr = lambda n, sh, dt=F32: nc.dram_tensor(n, sh, dt, kind="ExternalInput").ap()
    xT = dr("xT", [D_MODEL, T])
    W_inT = dr("W_inT", [D_MODEL, 2 * D_INNER])
    conv_w = dr("conv_wc", [128, NCH_I * D_CONV])
    conv_b = dr("conv_bc", [128, NCH_I])
    A_log = dr("A_log_r", [1, D_STATE])
    D_in = dr("D_c", [128, NCH_I])
    W_xT = dr("W_xT", [D_INNER, 96])
    W_dtT = dr("W_dtT", [DT_RANK, D_INNER])
    b_dt = dr("b_dtc", [128, NCH_I])
    W_outT = dr("W_outT", [D_INNER, D_MODEL])
    W_rT = dr("W_rT", [D_MODEL, D_MODEL])
    W_kT = dr("W_kT", [D_MODEL, D_MODEL])
    W_vT = dr("W_vT", [D_MODEL, D_MODEL])
    mu_r = dr("mu_rc", [128, NCH_M])
    mu_k = dr("mu_kc", [128, NCH_M])
    eye_in = dr("eye_in", [128, 128], BF16)

    out = nc.dram_tensor("out", [D_MODEL, T], F32, kind="ExternalOutput").ap()
    dbg = {}
    if debug:
        for nm, rows_ in (("d_ln", D_MODEL), ("d_xs", D_INNER), ("d_dt", D_INNER),
                          ("d_gy", D_INNER), ("d_x1", D_MODEL), ("d_ln2", D_MODEL)):
            dbg[nm] = nc.dram_tensor(nm, [rows_, T], F32, kind="ExternalOutput").ap()

    scr = lambda n, sh, dt=F32: nc.dram_tensor(n, sh, dt, kind="Internal").ap()
    W_inT_bf = scr("W_inT_bf", [D_MODEL, 2 * D_INNER], BF16)
    W_outT_bf = scr("W_outT_bf", [D_INNER, D_MODEL], BF16)
    W_rT_bf = scr("W_rT_bf", [D_MODEL, D_MODEL], BF16)
    W_kT_bf = scr("W_kT_bf", [D_MODEL, D_MODEL], BF16)
    W_vT_bf = scr("W_vT_bf", [D_MODEL, D_MODEL], BF16)
    W_dtT_bf = scr("W_dtT_bf", [DT_RANK, D_INNER], BF16)
    W_xT_bf = scr("W_xT_bf", [D_INNER, 96], BF16)
    bounce = [scr(f"bounce{p}", [4, TB]) for p in range(2)]
    bbc = [scr(f"bbc{p}", [32, TB], BF16) for p in range(2)]
    a_bounce = scr("a_bounce", [1, D_STATE])

    from contextlib import ExitStack
    _st = ExitStack()
    with _st:
        xt = _st.enter_context(nc.sbuf_tensor("xt", [128, NCH_M * TB], F32))
        xre = xt.ap()[:, 0:2 * TB]
        xsqa = _st.enter_context(nc.sbuf_tensor("xsqa", [128, TB], F32))
        xsqb = _st.enter_context(nc.sbuf_tensor("xsqb", [128, TB], F32))
        lnt = _st.enter_context(nc.sbuf_tensor("lnt", [128, NCH_M * TB], BF16))
        ln2t = _st.enter_context(nc.sbuf_tensor("ln2t", [128, NCH_M * TB], BF16))
        rows = _st.enter_context(nc.sbuf_tensor("rows", [1, 3 * TB], F32))
        mrep = _st.enter_context(nc.sbuf_tensor("mrep", [128, TB], BF16))
        rrep = _st.enter_context(nc.sbuf_tensor("rrep", [128, TB], BF16))
        xrt = _st.enter_context(nc.sbuf_tensor("xrt", [128, NCH_I * (TB + 3)], BF16))
        halo = _st.enter_context(nc.sbuf_tensor("halo", [128, NCH_I * 3], BF16))
        convacc = _st.enter_context(nc.sbuf_tensor("convacc", [128, 2 * TB], F32))
        szt = _st.enter_context(nc.sbuf_tensor("szt", [128, NCH_I * TB], BF16))
        dtt = _st.enter_context(nc.sbuf_tensor("dtt", [128, NCH_I * TB], BF16))
        xdbl = _st.enter_context(nc.sbuf_tensor("xdbl", [96, TB], BF16))
        dtr = None
        brep = _st.enter_context(nc.sbuf_tensor("brep", [128, D_STATE * TB], BF16))
        crep = _st.enter_context(nc.sbuf_tensor("crep", [128, D_STATE * TB], BF16))
        vch = _st.enter_context(nc.sbuf_tensor("vch", [128, TB], BF16))
        dxsb = _st.enter_context(nc.sbuf_tensor("dxsb", [128, TB], BF16))
        abuf = _st.enter_context(nc.sbuf_tensor("abuf", [128, 2 * TB], F32))
        ube = _st.enter_context(nc.sbuf_tensor("ube", [128, TB], BF16))
        ubo = _st.enter_context(nc.sbuf_tensor("ubo", [128, TB], BF16))
        hbe = _st.enter_context(nc.sbuf_tensor("hbe", [128, TB], BF16))
        hbo = _st.enter_context(nc.sbuf_tensor("hbo", [128, TB], BF16))
        chb4 = [_st.enter_context(nc.sbuf_tensor(f"chb{i}", [128, TB], BF16))
                for i in range(3)]
        ytmp = _st.enter_context(nc.sbuf_tensor("ytmp", [128, TB], F32))
        stce = _st.enter_context(nc.sbuf_tensor("stce", [128, NCH_I * 8], F32))
        stco = _st.enter_context(nc.sbuf_tensor("stco", [128, NCH_I * 8], F32))
        gat2 = [_st.enter_context(nc.sbuf_tensor(f"gat{i}", [128, NCH_I * TB], BF16))
                for i in range(2)]
        x1t = _st.enter_context(nc.sbuf_tensor("x1t", [128, NCH_M * TB], F32))
        ln2p = _st.enter_context(nc.sbuf_tensor("ln2p", [128, TB], BF16))
        carry = _st.enter_context(nc.sbuf_tensor("carry", [128, NCH_M], BF16))
        ffntmp = _st.enter_context(nc.sbuf_tensor("ffntmp", [128, TB], BF16))
        x2t = _st.enter_context(nc.sbuf_tensor("x2t", [128, TB], F32))
        wslab = _st.enter_context(nc.sbuf_tensor("wslab", [128, 2 * 16 * 128], BF16))
        cwc = _st.enter_context(nc.sbuf_tensor("cwc", [128, NCH_I * D_CONV], F32))
        cbc = _st.enter_context(nc.sbuf_tensor("cbc", [128, NCH_I], F32))
        bdtc = _st.enter_context(nc.sbuf_tensor("bdtc", [128, NCH_I], F32))
        dc = _st.enter_context(nc.sbuf_tensor("dc", [128, NCH_I], F32))
        murc = _st.enter_context(nc.sbuf_tensor("murc", [128, NCH_M], F32))
        mukc = _st.enter_context(nc.sbuf_tensor("mukc", [128, NCH_M], F32))
        omrc = _st.enter_context(nc.sbuf_tensor("omrc", [128, NCH_M], F32))
        omkc = _st.enter_context(nc.sbuf_tensor("omkc", [128, NCH_M], F32))
        arow = _st.enter_context(nc.sbuf_tensor("arow", [1, D_STATE], F32))
        abc = _st.enter_context(nc.sbuf_tensor("abc", [128, D_STATE], F32))
        eyeb = _st.enter_context(nc.sbuf_tensor("eyeb", [128, 128], BF16))
        onesf = _st.enter_context(nc.sbuf_tensor("onesf", [128, 1], F32))
        rt = ln2t
        def kat_of(b):
            return gat2[b & 1].ap()[:, 0:NCH_M * TB]
        ps_a = _st.enter_context(nc.psum_tensor("ps_a", [128, 512], F32))
        ps_b = _st.enter_context(nc.psum_tensor("ps_b", [128, 512], F32))
        ps_c = _st.enter_context(nc.psum_tensor("ps_c", [128, 512], F32))
        ps_d = _st.enter_context(nc.psum_tensor("ps_d", [128, 512], F32))
        ps_y0 = _st.enter_context(nc.psum_tensor("ps_y0", [128, 512], F32))
        ps_y1 = _st.enter_context(nc.psum_tensor("ps_y1", [128, 512], F32))
        ps_s1 = _st.enter_context(nc.psum_tensor("ps_s1", [1, 512], F32))
        ps_s2 = _st.enter_context(nc.psum_tensor("ps_s2", [1, 512], F32))
        _q_w = _st.enter_context(nc.semaphore("q_w"))
        _q_w2 = _st.enter_context(nc.semaphore("q_w2"))
        _q_w3 = _st.enter_context(nc.semaphore("q_w3"))
        _q_x = _st.enter_context(nc.semaphore("q_x"))
        _q_g = _st.enter_context(nc.semaphore("q_g"))
        _q_o = _st.enter_context(nc.semaphore("q_o"))
        _q_o2 = _st.enter_context(nc.semaphore("q_o2"))
        _s_T = _st.enter_context(nc.semaphore("s_T"))
        _s_V = _st.enter_context(nc.semaphore("s_V"))
        _s_S = _st.enter_context(nc.semaphore("s_S"))
        _s_G = _st.enter_context(nc.semaphore("s_G"))
        block = _st.enter_context(nc.Block())
        q_w, q_x, q_g, q_o = Sem(_q_w), Sem(_q_x), Sem(_q_g), Sem(_q_o)
        q_w2, q_w3, q_o2 = Sem(_q_w2), Sem(_q_w3), Sem(_q_o2)
        q_ws = [q_w, q_w2, q_w3]
        q_os = [q_o, q_o2]
        s_T, s_V, s_S, s_G = Sem(_s_T), Sem(_s_V), Sem(_s_S), Sem(_s_G)

        class _Pending:
            __slots__ = ("incs",)
            def __init__(self):
                self.incs = []
            def then_inc(self, sem, n):
                self.incs.append((sem, n))
                return self

        def _nm(x):
            try:
                return x.tensor.name
            except AttributeError:
                return None

        _IOMAP = {
            "tensor_tensor": lambda a, kw: ((a[1], a[2]), (a[0],)),
            "tensor_scalar": lambda a, kw: ((a[1], a[2], a[3]), (a[0],)),
            "scalar_tensor_tensor": lambda a, kw: ((a[1], a[2], a[3]), (a[0],)),
            "tensor_tensor_scan": lambda a, kw: ((a[1], a[2], a[3]), (a[0],)),
            "tensor_copy": lambda a, kw: ((a[1],), (a[0],)),
            "memset": lambda a, kw: ((), (a[0],)),
            "activation": lambda a, kw: (
                (kw.get("in_"), kw.get("bias"), kw.get("scale")),
                (kw.get("out"),)),
            "matmul": lambda a, kw: ((a[1], a[2]), ()),
        }

        class _Rec:
            def __init__(self, sem=None):
                self.ops = []
                self.sem = sem
                self.waited = 0
                self.lw = {}
                self.lr = {}

            def _auto(self, meth, a, kw):
                if self.sem is None:
                    return
                if meth == "wait_ge":
                    if a and a[0] is self.sem.h and a[1] > self.waited:
                        self.waited = a[1]
                    return
                io = _IOMAP.get(meth)
                if io is None:
                    return
                rd, wr = io(a, kw)
                need = 0
                for x in rd:
                    n = _nm(x)
                    if n:
                        need = max(need, self.lw.get(n, 0))
                for x in wr:
                    n = _nm(x)
                    if n:
                        need = max(need, self.lw.get(n, 0), self.lr.get(n, 0))
                if need > self.waited:
                    self.ops.append(("wait_ge", (self.sem.h, need), {}, _Pending()))
                    self.waited = need
                cnt = self.sem.v + 1
                for x in rd:
                    n = _nm(x)
                    if n:
                        self.lr[n] = cnt
                for x in wr:
                    n = _nm(x)
                    if n:
                        self.lw[n] = cnt

            def __getattr__(self, meth):
                def call(*a, **kw):
                    self._auto(meth, a, kw)
                    p = _Pending()
                    self.ops.append((meth, a, kw, p))
                    return p
                return call

            def replay(self, eng):
                for meth, a, kw, p in self.ops:
                    instr = getattr(eng, meth)(*a, **kw)
                    for sem, n in p.incs:
                        instr.then_inc(sem, n)

        sy = _Rec()
        g = _Rec(s_G)
        t, v, s = _Rec(s_T), _Rec(s_V), _Rec(s_S)
        PS = [ps_a, ps_b, ps_c, ps_d]

        def dma(eng, q, **kw):
            eng.dma_start(**kw).then_inc(q.h, 16)
            q.v += 16

        def vi(instr):
            instr.then_inc(s_V.h, 1)
            s_V.v += 1

        def si(instr):
            instr.then_inc(s_S.h, 1)
            s_S.v += 1

        def ti(instr):
            instr.then_inc(s_T.h, 1)
            s_T.v += 1

        def gi(instr):
            instr.then_inc(s_G.h, 1)
            s_G.v += 1

        dma(sy, q_x, out=cwc[:], in_=conv_w[:])
        dma(sy, q_x, out=cbc[:], in_=conv_b[:])
        dma(sy, q_x, out=bdtc[:], in_=b_dt[:])
        dma(sy, q_x, out=dc[:], in_=D_in[:])
        dma(sy, q_x, out=murc[:], in_=mu_r[:])
        dma(sy, q_x, out=mukc[:], in_=mu_k[:])
        dma(sy, q_x, out=arow[:], in_=A_log[:])
        dma(sy, q_x, out=eyeb[:], in_=eye_in[:])
        s.wait_ge(q_x.h, q_x.v)
        si(s.activation(out=arow[:], in_=arow[:], func=AF.Exp))
        v.wait_ge(s_S.h, s_S.v)
        vi(v.tensor_scalar(arow[:], arow[:], -1.0, None, AluOpType.mult))
        g.wait_ge(s_V.h, s_V.v)
        dma(g, q_g, out=a_bounce[:], in_=arow[:])
        g.wait_ge(q_g.h, q_g.v)
        dma(g, q_g, out=abc[:], in_=a_bounce.broadcast_to((128, D_STATE)))

        v.wait_ge(q_x.h, q_x.v)
        vi(v.memset(onesf[:], 1.0))
        vi(v.memset(stce[:], 0.0))
        vi(v.memset(stco[:], 0.0))
        vi(v.memset(halo[:], 0.0))
        vi(v.memset(carry[:], 0.0))
        vi(v.tensor_scalar(omrc[:], murc[:], -1.0, 1.0, AluOpType.mult, AluOpType.add))
        vi(v.tensor_scalar(omkc[:], mukc[:], -1.0, 1.0, AluOpType.mult, AluOpType.add))

        def cast_weight(src, dst, R, C):
            for r0 in range(0, R, 128):
                sy.wait_ge(q_g.h, q_g.v)
                dma(sy, q_w, out=xt[:, :C], in_=src[r0:r0 + 128, :])
                v.wait_ge(q_w.h, q_w.v)
                vi(v.tensor_copy(gat2[0][:, :C], xt[:, :C]))
                g.wait_ge(s_V.h, s_V.v)
                dma(g, q_g, out=dst[r0:r0 + 128, :], in_=gat2[0][:, :C])

        cast_weight(W_inT, W_inT_bf, D_MODEL, 2 * D_INNER)
        cast_weight(W_outT, W_outT_bf, D_INNER, D_MODEL)
        cast_weight(W_rT, W_rT_bf, D_MODEL, D_MODEL)
        cast_weight(W_kT, W_kT_bf, D_MODEL, D_MODEL)
        cast_weight(W_vT, W_vT_bf, D_MODEL, D_MODEL)
        cast_weight(W_xT, W_xT_bf, D_INNER, 96)
        sy.wait_ge(q_g.h, q_g.v)
        dma(sy, q_w, out=xt[0:64, :D_INNER], in_=W_dtT[:, :])
        v.wait_ge(q_w.h, q_w.v)
        vi(v.tensor_copy(gat2[0][0:64, :D_INNER], xt[0:64, :D_INNER]))
        g.wait_ge(s_V.h, s_V.v)
        dma(g, q_g, out=W_dtT_bf[:, :], in_=gat2[0][0:64, :D_INNER])
        g.wait_ge(q_g.h, q_g.v)

        SLABW = 16 * 128
        slab_guard = [0, 0]
        slab_cnt = [0, 0]
        slab_rot = [0]

        def slab_load(src_bf, R, m0):
            sl = slab_rot[0]
            slab_rot[0] = (sl + 1) % 2
            nk = R // 128
            sy.wait_ge(s_T.h, slab_guard[sl])
            dma(sy, q_ws[sl],
                out=wslab[:, sl * SLABW: sl * SLABW + nk * 128]
                .rearrange("p (k c) -> p k c", c=128),
                in_=src_bf[:, m0 * 128:(m0 + 1) * 128]
                .rearrange("(k p) c -> p k c", p=128))
            slab_cnt[sl] = q_ws[sl].v
            return sl, q_ws[sl].v

        def slab_seq(loader, ms, body):
            nxt = loader(ms[0])
            for i, m in enumerate(ms):
                sl, cnt = nxt
                if i + 1 < len(ms):
                    nxt = loader(ms[i + 1])
                t.wait_ge(q_ws[sl].h, cnt)
                body(sl, m)

        def slab_load_dt():
            sl = slab_rot[0]
            slab_rot[0] = (sl + 1) % 2
            sy.wait_ge(s_T.h, slab_guard[sl])
            dma(sy, q_ws[sl], out=wslab[0:64, sl * SLABW: sl * SLABW + D_INNER],
                in_=W_dtT_bf[:, :])
            slab_cnt[sl] = q_ws[sl].v
            return sl, q_ws[sl].v

        def xs_c(c):
            return xrt[:, c * (TB + 3) + 3:(c + 1) * (TB + 3)]

        scan_g = {}
        ch_g = {}
        mm_g = {}
        ytmp_g = {}
        gate_g = {}
        vchfree = {}
        silu_g = {}
        zsilu_g = {}
        dt_g = {}
        lnap_g = {}
        x1_g = {}
        mix_g = {}
        dtr_g = {}
        brep_q = {}
        gatdone_g = {}
        rk_g = {}
        wv_g = {}
        wvmm_g = {}
        winev_g = {}
        out_cnt = {0: 0, 1: 0}
        ln_ctr = [0]
        conv_prev = [0]

        def ln_stats(src_t):
            par = ln_ctr[0] & 1
            ln_ctr[0] += 1
            t.wait_ge(s_V.h, s_V.v)
            for c in range(NCH_M):
                sl = src_t[:, c * TB:(c + 1) * TB]
                xsq = xsqa if (c & 1) == 0 else xsqb
                s.wait_ge(s_T.h, s_T.v)
                si(s.activation(out=xsq[:], in_=sl, func=AF.Square))
                t.wait_ge(s_S.h, s_S.v)
                ti(t.matmul(ps_s1[:, :], onesf[:], sl, start=(c == 0), stop=(c == NCH_M - 1)))
                ti(t.matmul(ps_s2[:, :], onesf[:], xsq[:], start=(c == 0), stop=(c == NCH_M - 1)))
            v.wait_ge(s_T.h, s_T.v)
            r_mean = rows[0:1, 0:TB]
            r_q = rows[0:1, TB:2 * TB]
            r_var = rows[0:1, 2 * TB:3 * TB]
            vi(v.tensor_scalar(r_mean, ps_s1[0:1, :], 1.0 / D_MODEL, None, AluOpType.mult))
            vi(v.tensor_scalar(r_q, ps_s2[0:1, :], 1.0 / D_MODEL, None, AluOpType.mult))
            vi(v.tensor_tensor(r_var, r_mean, r_mean, AluOpType.mult))
            vi(v.tensor_tensor(r_var, r_q, r_var, AluOpType.subtract))
            vi(v.tensor_scalar(r_var, r_var, EPS, None, AluOpType.add))
            s.wait_ge(s_V.h, s_V.v)
            si(s.activation(out=r_var, in_=r_var, func=AF.Ln))
            si(s.activation(out=r_var, in_=r_var, func=AF.Exp, scale=-0.5))
            g.wait_ge(s_V.h, s_V.v)
            g.wait_ge(s_S.h, s_S.v)
            dma(g, q_g, out=bounce[par][0:1, :], in_=r_mean)
            dma(g, q_g, out=bounce[par][1:2, :], in_=r_var)
            g.wait_ge(q_g.h, q_g.v)
            dma(g, q_g, out=mrep[:], in_=bounce[par][0:1, :].broadcast_to((128, TB)))
            dma(g, q_g, out=rrep[:], in_=bounce[par][1:2, :].broadcast_to((128, TB)))

        def ln_apply(src_t, dst_t):
            v.wait_ge(q_g.h, q_g.v)
            for c in range(NCH_M):
                sl = src_t[:, c * TB:(c + 1) * TB]
                dl = dst_t[:, c * TB:(c + 1) * TB]
                acc = convacc[:, (c & 1) * TB:((c & 1) + 1) * TB]
                vi(v.tensor_tensor(acc, sl, mrep[:], AluOpType.subtract))
                vi(v.tensor_tensor(dl, acc, rrep[:], AluOpType.mult))

        def fr_load_xt(j):
            def f():
                sy.wait_ge(s_V.h, s_V.v)
                sy.wait_ge(s_S.h, s_S.v)
                sy.wait_ge(s_T.h, s_T.v)
                t0j = j * TB
                for c in range(NCH_M):
                    dma(sy, q_x, out=xt[:, c * TB:(c + 1) * TB],
                        in_=xT[c * 128:(c + 1) * 128, t0j:t0j + TB])
            return f

        def fr_ln1_stats(j):
            def f():
                s.wait_ge(q_x.h, q_x.v)
                t.wait_ge(q_x.h, q_x.v)
                ln_stats(xt)
            return f

        def fr_ln1_apply(j):
            def f():
                ln_apply(xt, lnt)
                lnap_g[("1", j)] = s_V.v
                if debug:
                    g.wait_ge(s_V.h, s_V.v)
                    for c in range(NCH_M):
                        dma(g, q_g, out=dbg["d_ln"][c * 128:(c + 1) * 128,
                                                    j * TB:(j + 1) * TB],
                            in_=lnt[:, c * TB:(c + 1) * TB])
            return f

        def fr_win_xr(j, ms, k_cur):
            def f():
                t.wait_ge(s_V.h, lnap_g[("1", j)])
                t.wait_ge(s_S.h, dt_g.get((j - 1, NCH_I - 1), 0))
                def body(sl, m):
                    ps = PS[2 + (m & 1)]
                    t.wait_ge(s_V.h, winev_g.get((j, m - 2), 0))
                    if k_cur is not None:
                        t.wait_ge(s_V.h, vchfree.get((k_cur, m), 0))
                    for kk in range(NCH_M):
                        ti(t.matmul(ps[:, :],
                                    wslab[:, sl * SLABW + kk * 128: sl * SLABW + (kk + 1) * 128],
                                    lnt[:, kk * TB:(kk + 1) * TB],
                                    start=(kk == 0), stop=(kk == NCH_M - 1)))
                    slab_guard[sl] = s_T.v
                    v.wait_ge(s_T.h, s_T.v)
                    vi(v.tensor_copy(xrt[:, m * (TB + 3) + 3:(m + 1) * (TB + 3)], ps[:, :]))
                    winev_g[(j, m)] = s_V.v
                slab_seq(lambda m: slab_load(W_inT_bf, D_MODEL, m), ms, body)
            return f

        def fr_win_z(j, js, k_cur):
            def f():
                t.wait_ge(s_V.h, lnap_g[("1", j)])
                def body(sl, jz):
                    t.wait_ge(s_S.h, zsilu_g.get((j, jz - 1),
                                                 zsilu_g.get((j - 1, NCH_I - 1), 0)))
                    if k_cur is not None:
                        t.wait_ge(s_V.h, gate_g.get((k_cur, jz), 0))
                    for kk in range(NCH_M):
                        ti(t.matmul(ps_y1[:, :],
                                    wslab[:, sl * SLABW + kk * 128: sl * SLABW + (kk + 1) * 128],
                                    lnt[:, kk * TB:(kk + 1) * TB],
                                    start=(kk == 0), stop=(kk == NCH_M - 1)))
                    slab_guard[sl] = s_T.v
                    s.wait_ge(s_T.h, s_T.v)
                    si(s.activation(out=szt[:, jz * TB:(jz + 1) * TB], in_=ps_y1[:, :],
                                    func=AF.Silu))
                    zsilu_g[(j, jz)] = s_S.v
                slab_seq(lambda jz: slab_load(W_inT_bf, D_MODEL, NCH_I + jz), js, body)
            return f

        def fr_conv(j, cs):
            def f():
                for c in cs:
                    xr_ch = xrt[:, c * (TB + 3):(c + 1) * (TB + 3)]
                    acc0 = convacc[:, 0:TB]
                    acc1 = convacc[:, TB:2 * TB]
                    v.wait_ge(s_S.h, conv_prev[0])
                    vi(v.tensor_copy(xr_ch[:, 0:3], halo[:, c * 3:(c + 1) * 3]))
                    vi(v.tensor_scalar(acc0, xr_ch[:, 0:TB], cwc[:, c * 4:c * 4 + 1],
                                       None, AluOpType.mult))
                    vi(v.scalar_tensor_tensor(acc1, xr_ch[:, 1:1 + TB],
                                              cwc[:, c * 4 + 1:c * 4 + 2], acc0,
                                              AluOpType.mult, AluOpType.add))
                    vi(v.scalar_tensor_tensor(acc0, xr_ch[:, 2:2 + TB],
                                              cwc[:, c * 4 + 2:c * 4 + 3], acc1,
                                              AluOpType.mult, AluOpType.add))
                    vi(v.scalar_tensor_tensor(acc1, xr_ch[:, 3:3 + TB],
                                              cwc[:, c * 4 + 3:c * 4 + 4], acc0,
                                              AluOpType.mult, AluOpType.add))
                    vi(v.tensor_copy(halo[:, c * 3:(c + 1) * 3], xr_ch[:, TB:TB + 3]))
                    s.wait_ge(s_V.h, s_V.v)
                    si(s.activation(out=xr_ch[:, 3:3 + TB], in_=acc1, func=AF.Silu,
                                    bias=cbc[:, c:c + 1]))
                    silu_g[(j, c)] = s_S.v
                    conv_prev[0] = s_S.v
            return f

        def fr_wx_wdt(j):
            def f():
                slx = slab_rot[0]
                slab_rot[0] = (slx + 1) % 2
                sy.wait_ge(s_T.h, slab_guard[slx])
                dma(sy, q_ws[slx],
                    out=wslab[:, slx * SLABW: slx * SLABW + NCH_I * 96]
                    .rearrange("p (k c) -> p k c", c=96),
                    in_=W_xT_bf.rearrange("(k p) c -> p k c", p=128))
                slab_cnt[slx] = q_ws[slx].v
                t.wait_ge(s_S.h, silu_g[(j, NCH_I - 1)])
                t.wait_ge(s_S.h, dt_g.get((j - 1, NCH_I - 1), 0))
                t.wait_ge(s_V.h, s_V.v)
                t.wait_ge(q_ws[slx].h, slab_cnt[slx])
                for kk in range(NCH_I):
                    ti(t.matmul(ps_c[0:96, :],
                                wslab[:, slx * SLABW + kk * 96: slx * SLABW + (kk + 1) * 96],
                                xs_c(kk),
                                start=(kk == 0), stop=(kk == NCH_I - 1)))
                slab_guard[slx] = s_T.v
                v.wait_ge(s_T.h, s_T.v)
                v.wait_ge(q_g.h, q_g.v)
                vi(v.tensor_copy(xdbl[:, :], ps_c[0:96, :]))
                dtr_g[j] = s_V.v
                par = j & 1
                g.wait_ge(s_V.h, s_V.v)
                g.wait_ge(q_g.h, q_g.v)
                dma(g, q_g, out=bbc[par][:, :], in_=xdbl[64:96, :])
                g.wait_ge(q_g.h, q_g.v)
                for si_ in range(D_STATE):
                    dma(g, q_g, out=brep[:, si_ * TB:(si_ + 1) * TB],
                        in_=bbc[par][si_:si_ + 1, :].broadcast_to((128, TB)))
                    dma(g, q_g, out=crep[:, si_ * TB:(si_ + 1) * TB],
                        in_=bbc[par][16 + si_:17 + si_, :].broadcast_to((128, TB)))
                brep_q[j] = q_g.v
                sl, _cnt = slab_load_dt()
                t.wait_ge(s_V.h, s_V.v)
                t.wait_ge(q_ws[sl].h, _cnt)
                for m in range(NCH_I):
                    ps = PS[2 + (m & 1)]
                    xsq = xsqb if (m & 1) == 0 else xsqa
                    if m >= 2:
                        t.wait_ge(s_S.h, dt_g[(j, m - 2)])
                    ti(t.matmul(ps[:, :], wslab[0:64, sl * SLABW + m * 128:
                                                sl * SLABW + (m + 1) * 128],
                                xdbl[0:64, :], start=True, stop=True))
                    s.wait_ge(s_T.h, s_T.v)
                    si(s.activation(out=xsq[:], in_=ps[:, :], func=AF.Exp,
                                    bias=bdtc[:, m:m + 1]))
                    si(s.activation(out=dtt[:, m * TB:(m + 1) * TB], in_=xsq[:],
                                    func=AF.Ln, bias=1.0))
                    dt_g[(j, m)] = s_S.v
                slab_guard[sl] = s_T.v
            return f

        def bk_wout(b, ms):
            def f():
                t0b = b * TB
                sy.wait_ge(s_V.h, s_V.v)
                for m in ms:
                    dma(sy, q_x, out=xre[:, (m & 1) * TB:((m & 1) + 1) * TB],
                        in_=xT[m * 128:(m + 1) * 128, t0b:t0b + TB])
                xq = q_x.v
                t.wait_ge(s_V.h, gatdone_g[b])
                t.wait_ge(s_V.h, wv_g.get(b - 1, 0))
                def body(sl, m):
                    ps = PS[m & 1]
                    t.wait_ge(s_V.h, x1_g.get((b, m - 2), 0))
                    for kk in range(NCH_I):
                        ti(t.matmul(ps[:, :],
                                    wslab[:, sl * SLABW + kk * 128: sl * SLABW + (kk + 1) * 128],
                                    gat2[b & 1][:, kk * TB:(kk + 1) * TB],
                                    start=(kk == 0), stop=(kk == NCH_I - 1)))
                    slab_guard[sl] = s_T.v
                    v.wait_ge(s_T.h, s_T.v)
                    v.wait_ge(q_x.h, xq)
                    vi(v.tensor_tensor(x1t[:, m * TB:(m + 1) * TB], ps[:, :],
                                       xre[:, (m & 1) * TB:((m & 1) + 1) * TB],
                                       AluOpType.add))
                    x1_g[(b, m)] = s_V.v
                slab_seq(lambda m: slab_load(W_outT_bf, D_INNER, m), ms, body)
                if debug and ms[-1] == NCH_M - 1:
                    g.wait_ge(s_V.h, s_V.v)
                    for c in range(NCH_M):
                        dma(g, q_g, out=dbg["d_x1"][c * 128:(c + 1) * 128,
                                                    b * TB:(b + 1) * TB],
                            in_=x1t[:, c * TB:(c + 1) * TB])
            return f

        def bk_ln2_stats(b):
            def f():
                ln_stats(x1t)
            return f

        def bk_ln2_apply(b):
            def f():
                ln_apply(x1t, ln2t)
                lnap_g[("2", b)] = s_V.v
                if debug:
                    g.wait_ge(s_V.h, s_V.v)
                    for c in range(NCH_M):
                        dma(g, q_g, out=dbg["d_ln2"][c * 128:(c + 1) * 128,
                                                     b * TB:(b + 1) * TB],
                            in_=ln2t[:, c * TB:(c + 1) * TB])
            return f

        def bk_ffn_mix(b, cs):
            xft = gat2[b & 1].ap()[:, 0:NCH_M * TB]
            xkt = gat2[b & 1].ap()[:, NCH_M * TB:2 * NCH_M * TB]
            def f():
                for c in cs:
                    l2 = ln2t[:, c * TB:(c + 1) * TB]
                    vi(v.tensor_copy(ln2p[:, 0:1], carry[:, c:c + 1]))
                    vi(v.tensor_copy(ln2p[:, 1:TB], l2[:, 0:TB - 1]))
                    vi(v.tensor_copy(carry[:, c:c + 1], l2[:, TB - 1:TB]))
                    xf = xft[:, c * TB:(c + 1) * TB]
                    xk = xkt[:, c * TB:(c + 1) * TB]
                    vi(v.tensor_scalar(xf, l2, murc[:, c:c + 1], None, AluOpType.mult))
                    vi(v.scalar_tensor_tensor(ffntmp[:], ln2p[:], omrc[:, c:c + 1], xf,
                                              AluOpType.mult, AluOpType.add))
                    vi(v.tensor_copy(xf, ffntmp[:]))
                    vi(v.tensor_scalar(xk, l2, mukc[:, c:c + 1], None, AluOpType.mult))
                    vi(v.scalar_tensor_tensor(ffntmp[:], ln2p[:], omkc[:, c:c + 1], xk,
                                              AluOpType.mult, AluOpType.add))
                    vi(v.tensor_copy(xk, ffntmp[:]))
                mix_g[b] = s_V.v
            return f

        def bk_wrk(b, which, ms):
            wt = W_rT_bf if which == "r" else W_kT_bf
            off = 0 if which == "r" else NCH_M * TB
            def f():
                src = gat2[b & 1].ap()[:, off:off + NCH_M * TB]
                t.wait_ge(s_V.h, mix_g[b])
                def body(sl, m):
                    ps = PS[m & 1]
                    if which == "r":
                        t.wait_ge(s_V.h, x1_g[(b, 6 + (m & 1))])
                        t.wait_ge(s_S.h, rk_g.get((b, "r", m - 2), 0))
                    else:
                        prev = rk_g.get((b, "k", m - 2), rk_g.get((b, "r", 6 + (m & 1)), 0))
                        t.wait_ge(s_S.h, prev)
                    for kk in range(NCH_M):
                        ti(t.matmul(ps[:, :],
                                    wslab[:, sl * SLABW + kk * 128: sl * SLABW + (kk + 1) * 128],
                                    src[:, kk * TB:(kk + 1) * TB],
                                    start=(kk == 0), stop=(kk == NCH_M - 1)))
                    slab_guard[sl] = s_T.v
                    s.wait_ge(s_T.h, s_T.v)
                    if which == "r":
                        si(s.activation(out=rt[:, m * TB:(m + 1) * TB], in_=ps[:, :],
                                        func=AF.Sigmoid))
                    else:
                        xsq = xsqa if (m & 1) == 0 else xsqb
                        si(s.activation(out=xsq[:], in_=ps[:, :], func=AF.Relu))
                        si(s.activation(out=kat_of(b)[:, m * TB:(m + 1) * TB],
                                        in_=xsq[:], func=AF.Square))
                    rk_g[(b, which, m)] = s_S.v
                slab_seq(lambda m: slab_load(wt, D_MODEL, m), ms, body)
            return f

        def bk_wv(b, ms):
            t0b = b * TB
            def f():
                t.wait_ge(s_S.h, rk_g[(b, "k", NCH_M - 1)])
                def body(sl, m):
                    ps = PS[m & 1]
                    t.wait_ge(s_S.h, rk_g.get((b, "k", 6 + (m & 1)), 0))
                    t.wait_ge(s_V.h, wv_g.get((b, m - 2), 0))
                    for kk in range(NCH_M):
                        ti(t.matmul(ps[:, :],
                                    wslab[:, sl * SLABW + kk * 128: sl * SLABW + (kk + 1) * 128],
                                    kat_of(b)[:, kk * TB:(kk + 1) * TB],
                                    start=(kk == 0), stop=(kk == NCH_M - 1)))
                    slab_guard[sl] = s_T.v
                    v.wait_ge(s_T.h, s_T.v)
                    v.wait_ge(q_os[m & 1].h, out_cnt[m & 1])
                    x2b = x2t[:, 0:TB] if (m & 1) == 0 else ytmp[:, 0:TB]
                    vi(v.tensor_tensor(ffntmp[:], ps[:, :], rt[:, m * TB:(m + 1) * TB],
                                       AluOpType.mult))
                    vi(v.tensor_tensor(x2b, ffntmp[:], x1t[:, m * TB:(m + 1) * TB],
                                       AluOpType.add))
                    wv_g[(b, m)] = s_V.v
                    sy.wait_ge(s_V.h, s_V.v)
                    dma(sy, q_os[m & 1], out=out[m * 128:(m + 1) * 128, t0b:t0b + TB],
                        in_=x2b)
                    out_cnt[m & 1] = q_os[m & 1].v
                slab_seq(lambda m: slab_load(W_vT_bf, D_MODEL, m), ms, body)
                if ms[-1] == NCH_M - 1:
                    wv_g[b] = s_V.v
                    wvmm_g[b] = s_T.v
            return f

        UB = [ube, ubo]
        HB = [hbe, hbo]

        def scan_section(k, c):
            c_abs = k * NCH_I + c
            dt_ch = dtt[:, c * TB:(c + 1) * TB]
            v.wait_ge(s_S.h, dt_g[(k, c)])
            v.wait_ge(s_S.h, silu_g[(k, c)])
            if c == 0:
                v.wait_ge(q_g.h, brep_q[k])
                v.wait_ge(q_os[1].h, out_cnt[1])
                v.wait_ge(s_T.h, wvmm_g.get(k - 2, 0))
            vi(v.tensor_tensor(vch[:], dt_ch, xs_c(c), AluOpType.mult))
            vi(v.tensor_scalar(dxsb[:], xs_c(c), dc[:, c:c + 1], None, AluOpType.mult))
            vchfree[(k, c)] = s_V.v
            vi(v.tensor_tensor(UB[0][:], vch[:], brep[:, 0:TB], AluOpType.mult))

            def tail(j):
                kj = c_abs * D_STATE + j
                v.wait_ge(s_T.h, mm_g.get(kj - 3, 0))
                chb = chb4[j % 3]
                vi(v.tensor_tensor(chb[:], HB[j & 1][:], crep[:, j * TB:(j + 1) * TB],
                                   AluOpType.mult))
                ch_g[kj] = s_V.v
                stx = stce if (j & 1) == 0 else stco
                stcol = stx[:, c * 8 + (j >> 1): c * 8 + (j >> 1) + 1]
                vi(v.tensor_scalar(stcol, HB[j & 1][:, TB - 1:TB], 1.0, None,
                                   AluOpType.mult))
                if j == 0:
                    t.wait_ge(s_V.h, ytmp_g.get((k, c - 1) if c else (k - 1, NCH_I - 1), 0))
                t.wait_ge(s_V.h, ch_g[kj])
                ti(t.matmul(ps_y0[:, :], eyeb[:], chb[:],
                            start=(j == 0), stop=(j == D_STATE - 1)))
                mm_g[kj] = s_T.v

            for si_ in range(D_STATE):
                kj = c_abs * D_STATE + si_
                ab = abuf[:, (si_ & 1) * TB:((si_ & 1) + 1) * TB]
                s.wait_ge(s_V.h, scan_g.get(kj - 2, 0))
                si(s.activation(out=ab, in_=dt_ch, func=AF.Exp,
                                scale=abc[:, si_:si_ + 1]))
                ab_cnt = s_S.v
                if si_ + 1 < D_STATE:
                    vi(v.tensor_tensor(UB[(si_ + 1) & 1][:], vch[:],
                                       brep[:, (si_ + 1) * TB:(si_ + 2) * TB],
                                       AluOpType.mult))
                stx = stce if (si_ & 1) == 0 else stco
                stcol = stx[:, c * 8 + (si_ >> 1): c * 8 + (si_ >> 1) + 1]
                v.wait_ge(s_S.h, ab_cnt)
                scan_g[kj] = s_V.v + 1
                vi(v.tensor_tensor_scan(HB[si_ & 1][:], ab, UB[si_ & 1][:], stcol,
                                        AluOpType.mult, AluOpType.add))
                if si_ > 0:
                    tail(si_ - 1)
            tail(D_STATE - 1)
            v.wait_ge(s_T.h, mm_g[c_abs * D_STATE + D_STATE - 1])
            v.wait_ge(s_S.h, zsilu_g[(k, c)])
            vi(v.tensor_tensor(ytmp[:], ps_y0[:, :], dxsb[:], AluOpType.add))
            ytmp_g[(k, c)] = s_V.v
            vi(v.tensor_tensor(gat2[k & 1][:, c * TB:(c + 1) * TB], ytmp[:],
                               szt[:, c * TB:(c + 1) * TB], AluOpType.mult))
            gate_g[(k, c)] = s_V.v
            if c == NCH_I - 1:
                gatdone_g[k] = s_V.v
            if debug:
                g.wait_ge(s_V.h, s_V.v)
                t0k = k * TB
                dma(g, q_g, out=dbg["d_xs"][c * 128:(c + 1) * 128, t0k:t0k + TB],
                    in_=xs_c(c))
                dma(g, q_g, out=dbg["d_dt"][c * 128:(c + 1) * 128, t0k:t0k + TB],
                    in_=dt_ch)
                dma(g, q_g, out=dbg["d_gy"][c * 128:(c + 1) * 128, t0k:t0k + TB],
                    in_=gat2[k & 1][:, c * TB:(c + 1) * TB])

        def front_agenda(j, k_cur, ag, tailp):
            ag[4].append(fr_load_xt(j))
            ag[5].append(fr_ln1_stats(j))
            ag[6].append(fr_ln1_apply(j))
            ag[7].append(fr_win_xr(j, [0, 1, 2, 3], k_cur))
            ag[8].append(fr_win_xr(j, [4, 5, 6, 7], k_cur))
            ag[9].append(fr_win_xr(j, [8, 9], k_cur))
            ag[11].append(fr_win_xr(j, [10, 11], k_cur))
            ag[13].append(fr_win_xr(j, [12, 13], k_cur))
            ag[14].append(fr_win_xr(j, [14], k_cur))
            ag[15].append(fr_win_xr(j, [15], k_cur))
            ag[9].append(fr_conv(j, [0, 1, 2, 3]))
            ag[9].append(fr_win_z(j, [0, 1, 2, 3], k_cur))
            ag[12].append(fr_win_z(j, [4, 5, 6, 7], k_cur))
            ag[13].append(fr_conv(j, [4, 5, 6, 7, 8, 9]))
            ag[15].append(fr_conv(j, [10, 11, 12, 13]))
            ag[15].append(fr_win_z(j, [8, 9, 10, 11, 12, 13, 14], k_cur))
            tailp.append(fr_conv(j, [14, 15]))
            tailp.append(fr_win_z(j, [15], k_cur))
            tailp.append(fr_wx_wdt(j))

        fr_load_xt(0)()
        fr_ln1_stats(0)()
        fr_ln1_apply(0)()
        for m0 in range(0, NCH_I, 2):
            fr_win_xr(0, [m0, m0 + 1], None)()
        for c0 in range(0, NCH_I, 2):
            fr_conv(0, [c0, c0 + 1])()
        for j0 in range(0, NCH_I, 2):
            fr_win_z(0, [j0, j0 + 1], None)()
        fr_wx_wdt(0)()

        for k in range(NB):
            ag = {cc: [] for cc in range(NCH_I)}
            tailp = []
            b = k - 1
            if b >= 0:
                ag[0].append(bk_wout(b, [0, 1]))
                ag[1].append(bk_wout(b, [2, 3]))
                ag[2].append(bk_wout(b, [4, 5]))
                ag[3].append(bk_wout(b, [6, 7]))
                ag[4].append(bk_ln2_stats(b))
                ag[5].append(bk_ln2_apply(b))
                ag[6].append(bk_ffn_mix(b, [0, 1, 2, 3]))
                ag[7].append(bk_ffn_mix(b, [4, 5, 6, 7]))
                ag[8].append(bk_wrk(b, "r", [0, 1, 2, 3]))
                ag[10].append(bk_wrk(b, "r", [4, 5, 6, 7]))
                ag[12].append(bk_wrk(b, "k", [0, 1, 2, 3]))
                ag[13].append(bk_wrk(b, "k", [4, 5, 6, 7]))
                tailp.append(bk_wv(b, [0, 1]))
                tailp.append(bk_wv(b, [2, 3]))
                tailp.append(bk_wv(b, [4, 5]))
                tailp.append(bk_wv(b, [6, 7]))
            j = k + 1
            if j < NB:
                front_agenda(j, k, ag, tailp)

            for c in range(NCH_I):
                scan_section(k, c)
                for fn in ag[c]:
                    fn()
            for fn in tailp:
                fn()

        b = NB - 1
        for m0 in range(0, NCH_M, 2):
            bk_wout(b, [m0, m0 + 1])()
        bk_ln2_stats(b)()
        bk_ln2_apply(b)()
        bk_ffn_mix(b, list(range(NCH_M)))()
        for m0 in range(0, NCH_M, 2):
            bk_wrk(b, "r", [m0, m0 + 1])()
        for m0 in range(0, NCH_M, 2):
            bk_wrk(b, "k", [m0, m0 + 1])()
        for m0 in range(0, NCH_M, 2):
            bk_wv(b, [m0, m0 + 1])()

        g.wait_ge(q_o.h, q_o.v)
        g.wait_ge(q_o2.h, q_o2.v)
        if debug:
            g.wait_ge(q_g.h, q_g.v)

        @block.sync
        def _(e):
            sy.replay(e)

        @block.gpsimd
        def _(e):
            g.replay(e)

        @block.tensor
        def _(e):
            t.replay(e)

        @block.vector
        def _(e):
            v.replay(e)

        @block.scalar
        def _(e):
            s.replay(e)

    return nc


_PROG_CACHE = {}


def _get_prog(T, TB, debug=False):
    key = (T, TB, debug)
    if key not in _PROG_CACHE:
        _PROG_CACHE[key] = build_program(T, TB, debug)
    return _PROG_CACHE[key]


def make_in_map(b, T, L0_attn_x, L0_attn_W_in, L0_attn_conv_w, L0_attn_conv_b,
                L0_attn_A_log, L0_attn_D, L0_attn_W_x, L0_attn_W_dt, L0_attn_b_dt,
                L0_attn_W_out, L0_ffn_W_r, L0_ffn_W_k, L0_ffn_W_v, L0_ffn_mu_r,
                L0_ffn_mu_k):
    C = np.ascontiguousarray
    f = np.float32
    _bf16 = ml_dtypes.bfloat16
    return {
        "xT": C(np.asarray(L0_attn_x)[b, :T, :].T.astype(f)),
        "W_inT": C(np.asarray(L0_attn_W_in).T.astype(f)),
        "conv_wc": C(np.asarray(L0_attn_conv_w).reshape(NCH_I, 128, 4)
                     .transpose(1, 0, 2).reshape(128, NCH_I * 4).astype(f)),
        "conv_bc": C(np.asarray(L0_attn_conv_b).reshape(NCH_I, 128).T.astype(f)),
        "A_log_r": C(np.asarray(L0_attn_A_log).reshape(1, D_STATE).astype(f)),
        "D_c": C(np.asarray(L0_attn_D).reshape(NCH_I, 128).T.astype(f)),
        "W_xT": C(np.asarray(L0_attn_W_x).T.astype(f)),
        "W_dtT": C(np.asarray(L0_attn_W_dt).T.astype(f)),
        "b_dtc": C(np.asarray(L0_attn_b_dt).reshape(NCH_I, 128).T.astype(f)),
        "W_outT": C(np.asarray(L0_attn_W_out).T.astype(f)),
        "W_rT": C(np.asarray(L0_ffn_W_r).T.astype(f)),
        "W_kT": C(np.asarray(L0_ffn_W_k).T.astype(f)),
        "W_vT": C(np.asarray(L0_ffn_W_v).T.astype(f)),
        "mu_rc": C(np.asarray(L0_ffn_mu_r).reshape(NCH_M, 128).T.astype(f)),
        "mu_kc": C(np.asarray(L0_ffn_mu_k).reshape(NCH_M, 128).T.astype(f)),
        "eye_in": np.eye(128, dtype=_bf16),
    }


def kernel(**inputs):
    T = 8192
    nc = _get_prog(T, 512)
    in_maps = [make_in_map(c >> 1, T, **inputs) for c in range(8)]
    res = run_bass_kernel_spmd(nc, in_maps, core_ids=list(range(8)))
    B = 4
    outs = []
    for b in range(B):
        o = res.results[2 * b]["out"]
        outs.append(np.ascontiguousarray(o.T)[None])
    return np.concatenate(outs, axis=0)


# version 06
# speedup vs baseline: 1.2370x; 1.0594x over previous
import numpy as np
import ml_dtypes
import concourse.bass as bass
import concourse.mybir as mybir
from concourse.bass_utils import run_bass_kernel_spmd
from concourse.alu_op_type import AluOpType

F32 = mybir.dt.float32
BF16 = mybir.dt.bfloat16
AF = mybir.ActivationFunctionType

D_MODEL = 1024
D_INNER = 2048
D_STATE = 16
DT_RANK = 64
D_CONV = 4
NCH_M = D_MODEL // 128
NCH_I = D_INNER // 128
EPS = 1e-5


class Sem:
    __slots__ = ("h", "v")
    def __init__(self, h):
        self.h = h
        self.v = 0


def build_program(T, TB, debug=False):
    NB = T // TB
    assert TB == 512
    nc = bass.Bass("TRN2", target_bir_lowering=False)

    dr = lambda n, sh, dt=F32: nc.dram_tensor(n, sh, dt, kind="ExternalInput").ap()
    xT = dr("xT", [D_MODEL, T])
    W_inT = dr("W_inT", [D_MODEL, 2 * D_INNER])
    conv_w = dr("conv_wc", [128, NCH_I * D_CONV])
    conv_b = dr("conv_bc", [128, NCH_I])
    A_log = dr("A_log_r", [1, D_STATE])
    D_in = dr("D_c", [128, NCH_I])
    W_xT = dr("W_xT", [D_INNER, 96])
    W_dtT = dr("W_dtT", [DT_RANK, D_INNER])
    b_dt = dr("b_dtc", [128, NCH_I])
    W_outT = dr("W_outT", [D_INNER, D_MODEL])
    W_rT = dr("W_rT", [D_MODEL, D_MODEL])
    W_kT = dr("W_kT", [D_MODEL, D_MODEL])
    W_vT = dr("W_vT", [D_MODEL, D_MODEL])
    mu_r = dr("mu_rc", [128, NCH_M])
    mu_k = dr("mu_kc", [128, NCH_M])
    eye_in = dr("eye_in", [128, 128], BF16)

    out = nc.dram_tensor("out", [D_MODEL, T], F32, kind="ExternalOutput").ap()
    dbg = {}
    if debug:
        for nm, rows_ in (("d_ln", D_MODEL), ("d_xs", D_INNER), ("d_dt", D_INNER),
                          ("d_gy", D_INNER), ("d_x1", D_MODEL), ("d_ln2", D_MODEL)):
            dbg[nm] = nc.dram_tensor(nm, [rows_, T], F32, kind="ExternalOutput").ap()

    scr = lambda n, sh, dt=F32: nc.dram_tensor(n, sh, dt, kind="Internal").ap()
    W_inT_bf = scr("W_inT_bf", [D_MODEL, 2 * D_INNER], BF16)
    W_outT_bf = scr("W_outT_bf", [D_INNER, D_MODEL], BF16)
    W_rT_bf = scr("W_rT_bf", [D_MODEL, D_MODEL], BF16)
    W_kT_bf = scr("W_kT_bf", [D_MODEL, D_MODEL], BF16)
    W_vT_bf = scr("W_vT_bf", [D_MODEL, D_MODEL], BF16)
    W_dtT_bf = scr("W_dtT_bf", [DT_RANK, D_INNER], BF16)
    W_xT_bf = scr("W_xT_bf", [D_INNER, 96], BF16)
    bounce = [scr(f"bounce{p}", [4, TB]) for p in range(2)]
    bbc = [scr(f"bbc{p}", [32, TB], BF16) for p in range(2)]
    a_bounce = scr("a_bounce", [1, D_STATE])

    from contextlib import ExitStack
    _st = ExitStack()
    with _st:
        xt = _st.enter_context(nc.sbuf_tensor("xt", [128, NCH_M * TB], F32))
        xre = xt.ap()[:, 0:2 * TB]
        xsqa = _st.enter_context(nc.sbuf_tensor("xsqa", [128, TB], F32))
        xsqb = _st.enter_context(nc.sbuf_tensor("xsqb", [128, TB], F32))
        lnt = _st.enter_context(nc.sbuf_tensor("lnt", [128, NCH_M * TB], BF16))
        ln2t = _st.enter_context(nc.sbuf_tensor("ln2t", [128, NCH_M * TB], BF16))
        rows = _st.enter_context(nc.sbuf_tensor("rows", [1, 3 * TB], F32))
        mrep = _st.enter_context(nc.sbuf_tensor("mrep", [128, TB], BF16))
        rrep = _st.enter_context(nc.sbuf_tensor("rrep", [128, TB], BF16))
        xrt = _st.enter_context(nc.sbuf_tensor("xrt", [128, NCH_I * (TB + 3)], BF16))
        halo = _st.enter_context(nc.sbuf_tensor("halo", [128, NCH_I * 3], BF16))
        convacc = _st.enter_context(nc.sbuf_tensor("convacc", [128, 2 * TB], F32))
        szt = _st.enter_context(nc.sbuf_tensor("szt", [128, NCH_I * TB], BF16))
        dtt = _st.enter_context(nc.sbuf_tensor("dtt", [128, NCH_I * TB], BF16))
        xdbl = _st.enter_context(nc.sbuf_tensor("xdbl", [96, TB], BF16))
        dtr = None
        brep = _st.enter_context(nc.sbuf_tensor("brep", [128, D_STATE * TB], BF16))
        crep = _st.enter_context(nc.sbuf_tensor("crep", [128, D_STATE * TB], BF16))
        vch = _st.enter_context(nc.sbuf_tensor("vch", [128, TB], BF16))
        dxsb = _st.enter_context(nc.sbuf_tensor("dxsb", [128, TB], BF16))
        abuf = _st.enter_context(nc.sbuf_tensor("abuf", [128, 2 * TB], F32))
        ube = _st.enter_context(nc.sbuf_tensor("ube", [128, TB], BF16))
        ubo = _st.enter_context(nc.sbuf_tensor("ubo", [128, TB], BF16))
        hbe = _st.enter_context(nc.sbuf_tensor("hbe", [128, TB], BF16))
        hbo = _st.enter_context(nc.sbuf_tensor("hbo", [128, TB], BF16))
        chb4 = [_st.enter_context(nc.sbuf_tensor(f"chb{i}", [128, TB], BF16))
                for i in range(3)]
        ytmp = _st.enter_context(nc.sbuf_tensor("ytmp", [128, TB], F32))
        stce = _st.enter_context(nc.sbuf_tensor("stce", [128, NCH_I * 8], F32))
        stco = _st.enter_context(nc.sbuf_tensor("stco", [128, NCH_I * 8], F32))
        gat2 = [_st.enter_context(nc.sbuf_tensor(f"gat{i}", [128, NCH_I * TB], BF16))
                for i in range(2)]
        x1t = _st.enter_context(nc.sbuf_tensor("x1t", [128, NCH_M * TB], F32))
        ln2p = _st.enter_context(nc.sbuf_tensor("ln2p", [128, TB], BF16))
        carry = _st.enter_context(nc.sbuf_tensor("carry", [128, NCH_M], BF16))
        ffntmp = _st.enter_context(nc.sbuf_tensor("ffntmp", [128, TB], BF16))
        x2t = _st.enter_context(nc.sbuf_tensor("x2t", [128, TB], F32))
        wslab = _st.enter_context(nc.sbuf_tensor("wslab", [128, 2 * 16 * 128], BF16))
        cwc = _st.enter_context(nc.sbuf_tensor("cwc", [128, NCH_I * D_CONV], F32))
        cbc = _st.enter_context(nc.sbuf_tensor("cbc", [128, NCH_I], F32))
        bdtc = _st.enter_context(nc.sbuf_tensor("bdtc", [128, NCH_I], F32))
        dc = _st.enter_context(nc.sbuf_tensor("dc", [128, NCH_I], F32))
        murc = _st.enter_context(nc.sbuf_tensor("murc", [128, NCH_M], F32))
        mukc = _st.enter_context(nc.sbuf_tensor("mukc", [128, NCH_M], F32))
        omrc = _st.enter_context(nc.sbuf_tensor("omrc", [128, NCH_M], F32))
        omkc = _st.enter_context(nc.sbuf_tensor("omkc", [128, NCH_M], F32))
        arow = _st.enter_context(nc.sbuf_tensor("arow", [1, D_STATE], F32))
        abc = _st.enter_context(nc.sbuf_tensor("abc", [128, D_STATE], F32))
        eyeb = _st.enter_context(nc.sbuf_tensor("eyeb", [128, 128], BF16))
        onesf = _st.enter_context(nc.sbuf_tensor("onesf", [128, 1], F32))
        rt = ln2t
        def kat_of(b):
            return gat2[b & 1].ap()[:, 0:NCH_M * TB]
        ps_a = _st.enter_context(nc.psum_tensor("ps_a", [128, 512], F32))
        ps_b = _st.enter_context(nc.psum_tensor("ps_b", [128, 512], F32))
        ps_c = _st.enter_context(nc.psum_tensor("ps_c", [128, 512], F32))
        ps_d = _st.enter_context(nc.psum_tensor("ps_d", [128, 512], F32))
        ps_y0 = _st.enter_context(nc.psum_tensor("ps_y0", [128, 512], F32))
        ps_y1 = _st.enter_context(nc.psum_tensor("ps_y1", [128, 512], F32))
        ps_s1 = _st.enter_context(nc.psum_tensor("ps_s1", [1, 512], F32))
        ps_s2 = _st.enter_context(nc.psum_tensor("ps_s2", [1, 512], F32))
        _q_w = _st.enter_context(nc.semaphore("q_w"))
        _q_w2 = _st.enter_context(nc.semaphore("q_w2"))
        _q_w3 = _st.enter_context(nc.semaphore("q_w3"))
        _q_x = _st.enter_context(nc.semaphore("q_x"))
        _q_g = _st.enter_context(nc.semaphore("q_g"))
        _q_o = _st.enter_context(nc.semaphore("q_o"))
        _q_o2 = _st.enter_context(nc.semaphore("q_o2"))
        _s_T = _st.enter_context(nc.semaphore("s_T"))
        _s_V = _st.enter_context(nc.semaphore("s_V"))
        _s_S = _st.enter_context(nc.semaphore("s_S"))
        _s_G = _st.enter_context(nc.semaphore("s_G"))
        block = _st.enter_context(nc.Block())
        q_w, q_x, q_g, q_o = Sem(_q_w), Sem(_q_x), Sem(_q_g), Sem(_q_o)
        q_w2, q_w3, q_o2 = Sem(_q_w2), Sem(_q_w3), Sem(_q_o2)
        q_ws = [q_w, q_w2]
        q_b = q_w3
        q_os = [q_o, q_o2]
        s_T, s_V, s_S, s_G = Sem(_s_T), Sem(_s_V), Sem(_s_S), Sem(_s_G)

        class _Pending:
            __slots__ = ("incs",)
            def __init__(self):
                self.incs = []
            def then_inc(self, sem, n):
                self.incs.append((sem, n))
                return self

        def _nm(x):
            try:
                return x.tensor.name
            except AttributeError:
                return None

        _IOMAP = {
            "tensor_tensor": lambda a, kw: ((a[1], a[2]), (a[0],)),
            "tensor_scalar": lambda a, kw: ((a[1], a[2], a[3]), (a[0],)),
            "scalar_tensor_tensor": lambda a, kw: ((a[1], a[2], a[3]), (a[0],)),
            "tensor_tensor_scan": lambda a, kw: ((a[1], a[2], a[3]), (a[0],)),
            "tensor_copy": lambda a, kw: ((a[1],), (a[0],)),
            "memset": lambda a, kw: ((), (a[0],)),
            "activation": lambda a, kw: (
                (kw.get("in_"), kw.get("bias"), kw.get("scale")),
                (kw.get("out"),)),
            "matmul": lambda a, kw: ((a[1], a[2]), ()),
        }

        class _Rec:
            def __init__(self, sem=None):
                self.ops = []
                self.sem = sem
                self.waited = 0
                self.lw = {}
                self.lr = {}

            def _auto(self, meth, a, kw):
                if self.sem is None:
                    return
                if meth == "wait_ge":
                    if a and a[0] is self.sem.h and a[1] > self.waited:
                        self.waited = a[1]
                    return
                io = _IOMAP.get(meth)
                if io is None:
                    return
                rd, wr = io(a, kw)
                need = 0
                for x in rd:
                    n = _nm(x)
                    if n:
                        need = max(need, self.lw.get(n, 0))
                for x in wr:
                    n = _nm(x)
                    if n:
                        need = max(need, self.lw.get(n, 0), self.lr.get(n, 0))
                if need > self.waited:
                    self.ops.append(("wait_ge", (self.sem.h, need), {}, _Pending()))
                    self.waited = need
                cnt = self.sem.v + 1
                for x in rd:
                    n = _nm(x)
                    if n:
                        self.lr[n] = cnt
                for x in wr:
                    n = _nm(x)
                    if n:
                        self.lw[n] = cnt

            def __getattr__(self, meth):
                def call(*a, **kw):
                    self._auto(meth, a, kw)
                    p = _Pending()
                    self.ops.append((meth, a, kw, p))
                    return p
                return call

            def replay(self, eng):
                for meth, a, kw, p in self.ops:
                    instr = getattr(eng, meth)(*a, **kw)
                    for sem, n in p.incs:
                        instr.then_inc(sem, n)

        sy = _Rec()
        g = _Rec(s_G)
        t, v, s = _Rec(s_T), _Rec(s_V), _Rec(s_S)
        PS = [ps_a, ps_b, ps_c, ps_d]

        def dma(eng, q, **kw):
            eng.dma_start(**kw).then_inc(q.h, 16)
            q.v += 16

        def vi(instr):
            instr.then_inc(s_V.h, 1)
            s_V.v += 1

        def si(instr):
            instr.then_inc(s_S.h, 1)
            s_S.v += 1

        def ti(instr):
            instr.then_inc(s_T.h, 1)
            s_T.v += 1

        def gi(instr):
            instr.then_inc(s_G.h, 1)
            s_G.v += 1

        dma(sy, q_x, out=cwc[:], in_=conv_w[:])
        dma(sy, q_x, out=cbc[:], in_=conv_b[:])
        dma(sy, q_x, out=bdtc[:], in_=b_dt[:])
        dma(sy, q_x, out=dc[:], in_=D_in[:])
        dma(sy, q_x, out=murc[:], in_=mu_r[:])
        dma(sy, q_x, out=mukc[:], in_=mu_k[:])
        dma(sy, q_x, out=arow[:], in_=A_log[:])
        dma(sy, q_x, out=eyeb[:], in_=eye_in[:])
        s.wait_ge(q_x.h, q_x.v)
        si(s.activation(out=arow[:], in_=arow[:], func=AF.Exp))
        v.wait_ge(s_S.h, s_S.v)
        vi(v.tensor_scalar(arow[:], arow[:], -1.0, None, AluOpType.mult))
        g.wait_ge(s_V.h, s_V.v)
        dma(g, q_g, out=a_bounce[:], in_=arow[:])
        g.wait_ge(q_g.h, q_g.v)
        dma(g, q_g, out=abc[:], in_=a_bounce.broadcast_to((128, D_STATE)))

        v.wait_ge(q_x.h, q_x.v)
        vi(v.memset(onesf[:], 1.0))
        vi(v.memset(stce[:], 0.0))
        vi(v.memset(stco[:], 0.0))
        vi(v.memset(halo[:], 0.0))
        vi(v.memset(carry[:], 0.0))
        vi(v.tensor_scalar(omrc[:], murc[:], -1.0, 1.0, AluOpType.mult, AluOpType.add))
        vi(v.tensor_scalar(omkc[:], mukc[:], -1.0, 1.0, AluOpType.mult, AluOpType.add))

        def cast_weight(src, dst, R, C):
            for r0 in range(0, R, 128):
                sy.wait_ge(q_g.h, q_g.v)
                dma(sy, q_w, out=xt[:, :C], in_=src[r0:r0 + 128, :])
                v.wait_ge(q_w.h, q_w.v)
                vi(v.tensor_copy(gat2[0][:, :C], xt[:, :C]))
                g.wait_ge(s_V.h, s_V.v)
                dma(g, q_g, out=dst[r0:r0 + 128, :], in_=gat2[0][:, :C])

        cast_weight(W_inT, W_inT_bf, D_MODEL, 2 * D_INNER)
        cast_weight(W_outT, W_outT_bf, D_INNER, D_MODEL)
        cast_weight(W_rT, W_rT_bf, D_MODEL, D_MODEL)
        cast_weight(W_kT, W_kT_bf, D_MODEL, D_MODEL)
        cast_weight(W_vT, W_vT_bf, D_MODEL, D_MODEL)
        cast_weight(W_xT, W_xT_bf, D_INNER, 96)
        sy.wait_ge(q_g.h, q_g.v)
        dma(sy, q_w, out=xt[0:64, :D_INNER], in_=W_dtT[:, :])
        v.wait_ge(q_w.h, q_w.v)
        vi(v.tensor_copy(gat2[0][0:64, :D_INNER], xt[0:64, :D_INNER]))
        g.wait_ge(s_V.h, s_V.v)
        dma(g, q_g, out=W_dtT_bf[:, :], in_=gat2[0][0:64, :D_INNER])
        g.wait_ge(q_g.h, q_g.v)

        SLABW = 16 * 128
        slab_guard = [0, 0]
        slab_cnt = [0, 0]
        slab_rot = [0]

        def slab_load(src_bf, R, m0):
            sl = slab_rot[0]
            slab_rot[0] = (sl + 1) % 2
            nk = R // 128
            sy.wait_ge(s_T.h, slab_guard[sl])
            dma(sy, q_ws[sl],
                out=wslab[:, sl * SLABW: sl * SLABW + nk * 128]
                .rearrange("p (k c) -> p k c", c=128),
                in_=src_bf[:, m0 * 128:(m0 + 1) * 128]
                .rearrange("(k p) c -> p k c", p=128))
            slab_cnt[sl] = q_ws[sl].v
            return sl, q_ws[sl].v

        def slab_seq(loader, ms, body):
            nxt = loader(ms[0])
            for i, m in enumerate(ms):
                sl, cnt = nxt
                if i + 1 < len(ms):
                    nxt = loader(ms[i + 1])
                t.wait_ge(q_ws[sl].h, cnt)
                body(sl, m)

        def slab_load_dt():
            sl = slab_rot[0]
            slab_rot[0] = (sl + 1) % 2
            sy.wait_ge(s_T.h, slab_guard[sl])
            dma(sy, q_ws[sl], out=wslab[0:64, sl * SLABW: sl * SLABW + D_INNER],
                in_=W_dtT_bf[:, :])
            slab_cnt[sl] = q_ws[sl].v
            return sl, q_ws[sl].v

        def xs_c(c):
            return xrt[:, c * (TB + 3) + 3:(c + 1) * (TB + 3)]

        scan_g = {}
        ch_g = {}
        mm_g = {}
        ytmp_g = {}
        gate_g = {}
        vchfree = {}
        silu_g = {}
        zsilu_g = {}
        dt_g = {}
        lnap_g = {}
        x1_g = {}
        mix_g = {}
        dtr_g = {}
        brep_q = {}
        gatdone_g = {}
        rk_g = {}
        wv_g = {}
        wvmm_g = {}
        winmm_g = {}
        woutmm_g = {}
        xre_q = {}
        brep_si = {}
        winev_g = {}
        out_cnt = {0: 0, 1: 0}
        ln_ctr = [0]
        conv_prev = [0]

        def ln_stats(src_t):
            par = ln_ctr[0] & 1
            ln_ctr[0] += 1
            t.wait_ge(s_V.h, s_V.v)
            for c in range(NCH_M):
                sl = src_t[:, c * TB:(c + 1) * TB]
                xsq = xsqa if (c & 1) == 0 else xsqb
                s.wait_ge(s_T.h, s_T.v)
                si(s.activation(out=xsq[:], in_=sl, func=AF.Square))
                t.wait_ge(s_S.h, s_S.v)
                ti(t.matmul(ps_s1[:, :], onesf[:], sl, start=(c == 0), stop=(c == NCH_M - 1)))
                ti(t.matmul(ps_s2[:, :], onesf[:], xsq[:], start=(c == 0), stop=(c == NCH_M - 1)))
            v.wait_ge(s_T.h, s_T.v)
            r_mean = rows[0:1, 0:TB]
            r_q = rows[0:1, TB:2 * TB]
            r_var = rows[0:1, 2 * TB:3 * TB]
            vi(v.tensor_scalar(r_mean, ps_s1[0:1, :], 1.0 / D_MODEL, None, AluOpType.mult))
            vi(v.tensor_scalar(r_q, ps_s2[0:1, :], 1.0 / D_MODEL, None, AluOpType.mult))
            vi(v.tensor_tensor(r_var, r_mean, r_mean, AluOpType.mult))
            vi(v.tensor_tensor(r_var, r_q, r_var, AluOpType.subtract))
            vi(v.tensor_scalar(r_var, r_var, EPS, None, AluOpType.add))
            s.wait_ge(s_V.h, s_V.v)
            si(s.activation(out=r_var, in_=r_var, func=AF.Ln))
            si(s.activation(out=r_var, in_=r_var, func=AF.Exp, scale=-0.5))
            g.wait_ge(s_V.h, s_V.v)
            g.wait_ge(s_S.h, s_S.v)
            dma(g, q_g, out=bounce[par][0:1, :], in_=r_mean)
            dma(g, q_g, out=bounce[par][1:2, :], in_=r_var)
            g.wait_ge(q_g.h, q_g.v)
            dma(g, q_g, out=mrep[:], in_=bounce[par][0:1, :].broadcast_to((128, TB)))
            dma(g, q_g, out=rrep[:], in_=bounce[par][1:2, :].broadcast_to((128, TB)))

        def ln_apply(src_t, dst_t):
            v.wait_ge(q_g.h, q_g.v)
            for c in range(NCH_M):
                sl = src_t[:, c * TB:(c + 1) * TB]
                dl = dst_t[:, c * TB:(c + 1) * TB]
                acc = convacc[:, (c & 1) * TB:((c & 1) + 1) * TB]
                vi(v.tensor_tensor(acc, sl, mrep[:], AluOpType.subtract))
                vi(v.tensor_tensor(dl, acc, rrep[:], AluOpType.mult))

        def fr_load_xt(j):
            def f():
                sy.wait_ge(s_V.h, s_V.v)
                sy.wait_ge(s_S.h, s_S.v)
                sy.wait_ge(s_T.h, s_T.v)
                t0j = j * TB
                for c in range(NCH_M):
                    dma(sy, q_x, out=xt[:, c * TB:(c + 1) * TB],
                        in_=xT[c * 128:(c + 1) * 128, t0j:t0j + TB])
            return f

        def fr_ln1_stats(j):
            def f():
                s.wait_ge(q_x.h, q_x.v)
                t.wait_ge(q_x.h, q_x.v)
                ln_stats(xt)
            return f

        def fr_ln1_apply(j):
            def f():
                if debug:
                    v.wait_ge(q_g.h, q_g.v)
                ln_apply(xt, lnt)
                lnap_g[("1", j)] = s_V.v
                if debug:
                    g.wait_ge(s_V.h, s_V.v)
                    for c in range(NCH_M):
                        dma(g, q_g, out=dbg["d_ln"][c * 128:(c + 1) * 128,
                                                    j * TB:(j + 1) * TB],
                            in_=lnt[:, c * TB:(c + 1) * TB])
            return f

        def fr_win_xr_mm(j, ms, k_cur):
            def f():
                t.wait_ge(s_V.h, lnap_g[("1", j)])
                t.wait_ge(s_S.h, dt_g.get((j - 1, NCH_I - 1), 0))
                def body(sl, m):
                    ps = PS[2 + (m & 1)]
                    t.wait_ge(s_V.h, winev_g.get((j, m - 2), 0))
                    if k_cur is not None:
                        t.wait_ge(s_V.h, vchfree.get((k_cur, m), 0))
                    for kk in range(NCH_M):
                        ti(t.matmul(ps[:, :],
                                    wslab[:, sl * SLABW + kk * 128: sl * SLABW + (kk + 1) * 128],
                                    lnt[:, kk * TB:(kk + 1) * TB],
                                    start=(kk == 0), stop=(kk == NCH_M - 1)))
                    slab_guard[sl] = s_T.v
                    winmm_g[(j, m)] = s_T.v
                slab_seq(lambda m: slab_load(W_inT_bf, D_MODEL, m), ms, body)
            return f

        def fr_win_xr_ev(j, ms, k_cur):
            def f():
                if debug:
                    v.wait_ge(q_g.h, q_g.v)
                for m in ms:
                    ps = PS[2 + (m & 1)]
                    v.wait_ge(s_T.h, winmm_g[(j, m)])
                    vi(v.tensor_copy(xrt[:, m * (TB + 3) + 3:(m + 1) * (TB + 3)], ps[:, :]))
                    winev_g[(j, m)] = s_V.v
            return f

        def fr_win_z(j, js, k_cur):
            def f():
                t.wait_ge(s_V.h, lnap_g[("1", j)])
                def body(sl, jz):
                    t.wait_ge(s_S.h, zsilu_g.get((j, jz - 1),
                                                 zsilu_g.get((j - 1, NCH_I - 1), 0)))
                    if k_cur is not None:
                        t.wait_ge(s_V.h, gate_g.get((k_cur, jz), 0))
                    for kk in range(NCH_M):
                        ti(t.matmul(ps_y1[:, :],
                                    wslab[:, sl * SLABW + kk * 128: sl * SLABW + (kk + 1) * 128],
                                    lnt[:, kk * TB:(kk + 1) * TB],
                                    start=(kk == 0), stop=(kk == NCH_M - 1)))
                    slab_guard[sl] = s_T.v
                    s.wait_ge(s_T.h, s_T.v)
                    si(s.activation(out=szt[:, jz * TB:(jz + 1) * TB], in_=ps_y1[:, :],
                                    func=AF.Silu))
                    zsilu_g[(j, jz)] = s_S.v
                slab_seq(lambda jz: slab_load(W_inT_bf, D_MODEL, NCH_I + jz), js, body)
            return f

        def fr_conv(j, cs):
            def f():
                for c in cs:
                    xr_ch = xrt[:, c * (TB + 3):(c + 1) * (TB + 3)]
                    acc0 = convacc[:, 0:TB]
                    acc1 = convacc[:, TB:2 * TB]
                    v.wait_ge(s_S.h, conv_prev[0])
                    vi(v.tensor_copy(xr_ch[:, 0:3], halo[:, c * 3:(c + 1) * 3]))
                    vi(v.tensor_scalar(acc0, xr_ch[:, 0:TB], cwc[:, c * 4:c * 4 + 1],
                                       None, AluOpType.mult))
                    vi(v.scalar_tensor_tensor(acc1, xr_ch[:, 1:1 + TB],
                                              cwc[:, c * 4 + 1:c * 4 + 2], acc0,
                                              AluOpType.mult, AluOpType.add))
                    vi(v.scalar_tensor_tensor(acc0, xr_ch[:, 2:2 + TB],
                                              cwc[:, c * 4 + 2:c * 4 + 3], acc1,
                                              AluOpType.mult, AluOpType.add))
                    vi(v.scalar_tensor_tensor(acc1, xr_ch[:, 3:3 + TB],
                                              cwc[:, c * 4 + 3:c * 4 + 4], acc0,
                                              AluOpType.mult, AluOpType.add))
                    vi(v.tensor_copy(halo[:, c * 3:(c + 1) * 3], xr_ch[:, TB:TB + 3]))
                    s.wait_ge(s_V.h, s_V.v)
                    si(s.activation(out=xr_ch[:, 3:3 + TB], in_=acc1, func=AF.Silu,
                                    bias=cbc[:, c:c + 1]))
                    silu_g[(j, c)] = s_S.v
                    conv_prev[0] = s_S.v
            return f

        def fr_wx_wdt(j):
            def f():
                slx = slab_rot[0]
                slab_rot[0] = (slx + 1) % 2
                sy.wait_ge(s_T.h, slab_guard[slx])
                dma(sy, q_ws[slx],
                    out=wslab[:, slx * SLABW: slx * SLABW + NCH_I * 96]
                    .rearrange("p (k c) -> p k c", c=96),
                    in_=W_xT_bf.rearrange("(k p) c -> p k c", p=128))
                slab_cnt[slx] = q_ws[slx].v
                t.wait_ge(s_S.h, silu_g[(j, NCH_I - 1)])
                t.wait_ge(s_S.h, dt_g.get((j - 1, NCH_I - 1), 0))
                t.wait_ge(s_V.h, s_V.v)
                t.wait_ge(q_ws[slx].h, slab_cnt[slx])
                for kk in range(NCH_I):
                    ti(t.matmul(ps_c[0:96, :],
                                wslab[:, slx * SLABW + kk * 96: slx * SLABW + (kk + 1) * 96],
                                xs_c(kk),
                                start=(kk == 0), stop=(kk == NCH_I - 1)))
                slab_guard[slx] = s_T.v
                v.wait_ge(s_T.h, s_T.v)
                v.wait_ge(q_g.h, q_g.v)
                vi(v.tensor_copy(xdbl[:, :], ps_c[0:96, :]))
                dtr_g[j] = s_V.v
                par = j & 1
                g.wait_ge(s_V.h, s_V.v)
                g.wait_ge(q_g.h, q_g.v)
                dma(g, q_g, out=bbc[par][:, :], in_=xdbl[64:96, :])
                g.wait_ge(q_g.h, q_g.v)
                for si_ in range(D_STATE):
                    dma(g, q_b, out=brep[:, si_ * TB:(si_ + 1) * TB],
                        in_=bbc[par][si_:si_ + 1, :].broadcast_to((128, TB)))
                    dma(g, q_b, out=crep[:, si_ * TB:(si_ + 1) * TB],
                        in_=bbc[par][16 + si_:17 + si_, :].broadcast_to((128, TB)))
                brep_q[j] = q_b.v
                sl, _cnt = slab_load_dt()
                if debug:
                    s.wait_ge(q_g.h, q_g.v)
                t.wait_ge(s_V.h, s_V.v)
                t.wait_ge(q_ws[sl].h, _cnt)
                for m in range(NCH_I):
                    ps = PS[2 + (m & 1)]
                    xsq = xsqb if (m & 1) == 0 else xsqa
                    if m >= 2:
                        t.wait_ge(s_S.h, dt_g[(j, m - 2)])
                    ti(t.matmul(ps[:, :], wslab[0:64, sl * SLABW + m * 128:
                                                sl * SLABW + (m + 1) * 128],
                                xdbl[0:64, :], start=True, stop=True))
                    s.wait_ge(s_T.h, s_T.v)
                    si(s.activation(out=xsq[:], in_=ps[:, :], func=AF.Exp,
                                    bias=bdtc[:, m:m + 1]))
                    si(s.activation(out=dtt[:, m * TB:(m + 1) * TB], in_=xsq[:],
                                    func=AF.Ln, bias=1.0))
                    dt_g[(j, m)] = s_S.v
                slab_guard[sl] = s_T.v
            return f

        def bk_wout_mm(b, ms):
            def f():
                t0b = b * TB
                sy.wait_ge(s_V.h, s_V.v)
                for m in ms:
                    dma(sy, q_x, out=xre[:, (m & 1) * TB:((m & 1) + 1) * TB],
                        in_=xT[m * 128:(m + 1) * 128, t0b:t0b + TB])
                xre_q[ms[0]] = q_x.v
                t.wait_ge(s_V.h, gatdone_g[b])
                t.wait_ge(s_V.h, wv_g.get(b - 1, 0))
                def body(sl, m):
                    ps = PS[m & 1]
                    t.wait_ge(s_V.h, x1_g.get((b, m - 2), 0))
                    for kk in range(NCH_I):
                        ti(t.matmul(ps[:, :],
                                    wslab[:, sl * SLABW + kk * 128: sl * SLABW + (kk + 1) * 128],
                                    gat2[b & 1][:, kk * TB:(kk + 1) * TB],
                                    start=(kk == 0), stop=(kk == NCH_I - 1)))
                    slab_guard[sl] = s_T.v
                    woutmm_g[(b, m)] = s_T.v
                slab_seq(lambda m: slab_load(W_outT_bf, D_INNER, m), ms, body)
            return f

        def bk_wout_x1(b, ms):
            def f():
                if debug:
                    v.wait_ge(q_g.h, q_g.v)
                for m in ms:
                    ps = PS[m & 1]
                    v.wait_ge(s_T.h, woutmm_g[(b, m)])
                    v.wait_ge(q_x.h, xre_q[ms[0]])
                    vi(v.tensor_tensor(x1t[:, m * TB:(m + 1) * TB], ps[:, :],
                                       xre[:, (m & 1) * TB:((m & 1) + 1) * TB],
                                       AluOpType.add))
                    x1_g[(b, m)] = s_V.v
                if debug and ms[-1] == NCH_M - 1:
                    g.wait_ge(s_V.h, s_V.v)
                    for c in range(NCH_M):
                        dma(g, q_g, out=dbg["d_x1"][c * 128:(c + 1) * 128,
                                                    b * TB:(b + 1) * TB],
                            in_=x1t[:, c * TB:(c + 1) * TB])
            return f

        def bk_ln2_stats(b):
            def f():
                ln_stats(x1t)
            return f

        def bk_ln2_apply(b):
            def f():
                ln_apply(x1t, ln2t)
                lnap_g[("2", b)] = s_V.v
                if debug:
                    g.wait_ge(s_V.h, s_V.v)
                    for c in range(NCH_M):
                        dma(g, q_g, out=dbg["d_ln2"][c * 128:(c + 1) * 128,
                                                     b * TB:(b + 1) * TB],
                            in_=ln2t[:, c * TB:(c + 1) * TB])
            return f

        def bk_ffn_mix(b, cs):
            xft = gat2[b & 1].ap()[:, 0:NCH_M * TB]
            xkt = gat2[b & 1].ap()[:, NCH_M * TB:2 * NCH_M * TB]
            def f():
                for c in cs:
                    l2 = ln2t[:, c * TB:(c + 1) * TB]
                    vi(v.tensor_copy(ln2p[:, 0:1], carry[:, c:c + 1]))
                    vi(v.tensor_copy(ln2p[:, 1:TB], l2[:, 0:TB - 1]))
                    vi(v.tensor_copy(carry[:, c:c + 1], l2[:, TB - 1:TB]))
                    xf = xft[:, c * TB:(c + 1) * TB]
                    xk = xkt[:, c * TB:(c + 1) * TB]
                    vi(v.tensor_scalar(xf, l2, murc[:, c:c + 1], None, AluOpType.mult))
                    vi(v.scalar_tensor_tensor(ffntmp[:], ln2p[:], omrc[:, c:c + 1], xf,
                                              AluOpType.mult, AluOpType.add))
                    vi(v.tensor_copy(xf, ffntmp[:]))
                    vi(v.tensor_scalar(xk, l2, mukc[:, c:c + 1], None, AluOpType.mult))
                    vi(v.scalar_tensor_tensor(ffntmp[:], ln2p[:], omkc[:, c:c + 1], xk,
                                              AluOpType.mult, AluOpType.add))
                    vi(v.tensor_copy(xk, ffntmp[:]))
                mix_g[b] = s_V.v
            return f

        def bk_wrk(b, which, ms):
            wt = W_rT_bf if which == "r" else W_kT_bf
            off = 0 if which == "r" else NCH_M * TB
            def f():
                src = gat2[b & 1].ap()[:, off:off + NCH_M * TB]
                if debug:
                    s.wait_ge(q_g.h, q_g.v)
                t.wait_ge(s_V.h, mix_g[b])
                def body(sl, m):
                    ps = PS[m & 1]
                    if which == "r":
                        t.wait_ge(s_V.h, x1_g[(b, 6 + (m & 1))])
                        t.wait_ge(s_S.h, rk_g.get((b, "r", m - 2), 0))
                    else:
                        prev = rk_g.get((b, "k", m - 2), rk_g.get((b, "r", 6 + (m & 1)), 0))
                        t.wait_ge(s_S.h, prev)
                    for kk in range(NCH_M):
                        ti(t.matmul(ps[:, :],
                                    wslab[:, sl * SLABW + kk * 128: sl * SLABW + (kk + 1) * 128],
                                    src[:, kk * TB:(kk + 1) * TB],
                                    start=(kk == 0), stop=(kk == NCH_M - 1)))
                    slab_guard[sl] = s_T.v
                    s.wait_ge(s_T.h, s_T.v)
                    if which == "r":
                        si(s.activation(out=rt[:, m * TB:(m + 1) * TB], in_=ps[:, :],
                                        func=AF.Sigmoid))
                    else:
                        xsq = xsqa if (m & 1) == 0 else xsqb
                        si(s.activation(out=xsq[:], in_=ps[:, :], func=AF.Relu))
                        si(s.activation(out=kat_of(b)[:, m * TB:(m + 1) * TB],
                                        in_=xsq[:], func=AF.Square))
                    rk_g[(b, which, m)] = s_S.v
                slab_seq(lambda m: slab_load(wt, D_MODEL, m), ms, body)
            return f

        def bk_wv(b, ms):
            t0b = b * TB
            def f():
                t.wait_ge(s_S.h, rk_g[(b, "k", NCH_M - 1)])
                t.wait_ge(s_S.h, dt_g.get((b + 1, NCH_I - 1), 0))
                t.wait_ge(s_V.h, s_V.v)
                def body(sl, m):
                    ps = PS[m & 3]
                    t.wait_ge(s_V.h, wv_g.get((b, m - 4), 0))
                    for kk in range(NCH_M):
                        ti(t.matmul(ps[:, :],
                                    wslab[:, sl * SLABW + kk * 128: sl * SLABW + (kk + 1) * 128],
                                    kat_of(b)[:, kk * TB:(kk + 1) * TB],
                                    start=(kk == 0), stop=(kk == NCH_M - 1)))
                    slab_guard[sl] = s_T.v
                    v.wait_ge(s_T.h, s_T.v)
                    v.wait_ge(q_os[m & 1].h, out_cnt[m & 1])
                    x2b = x2t[:, 0:TB] if (m & 1) == 0 else ytmp[:, 0:TB]
                    vi(v.tensor_tensor(ffntmp[:], PS[m & 3][:, :],
                                       rt[:, m * TB:(m + 1) * TB], AluOpType.mult))
                    vi(v.tensor_tensor(x2b, ffntmp[:], x1t[:, m * TB:(m + 1) * TB],
                                       AluOpType.add))
                    wv_g[(b, m)] = s_V.v
                    sy.wait_ge(s_V.h, s_V.v)
                    dma(sy, q_os[m & 1], out=out[m * 128:(m + 1) * 128, t0b:t0b + TB],
                        in_=x2b)
                    out_cnt[m & 1] = q_os[m & 1].v
                slab_seq(lambda m: slab_load(W_vT_bf, D_MODEL, m), ms, body)
                if ms[-1] == NCH_M - 1:
                    wv_g[b] = s_V.v
                    wvmm_g[b] = s_T.v
            return f

        UB = [ube, ubo]
        HB = [hbe, hbo]

        def scan_section(k, c):
            c_abs = k * NCH_I + c
            dt_ch = dtt[:, c * TB:(c + 1) * TB]
            v.wait_ge(s_S.h, dt_g[(k, c)])
            v.wait_ge(s_S.h, silu_g[(k, c)])
            if c == 0:
                v.wait_ge(q_os[1].h, out_cnt[1])
                v.wait_ge(s_T.h, wvmm_g.get(k - 2, 0))
            vi(v.tensor_tensor(vch[:], dt_ch, xs_c(c), AluOpType.mult))
            vi(v.tensor_scalar(dxsb[:], xs_c(c), dc[:, c:c + 1], None, AluOpType.mult))
            vchfree[(k, c)] = s_V.v
            if c == 0:
                v.wait_ge(q_b.h, brep_q[k])
            vi(v.tensor_tensor(UB[0][:], vch[:], brep[:, 0:TB], AluOpType.mult))

            def tail(j):
                kj = c_abs * D_STATE + j
                v.wait_ge(s_T.h, mm_g.get(kj - 3, 0))
                chb = chb4[j % 3]
                vi(v.tensor_tensor(chb[:], HB[j & 1][:], crep[:, j * TB:(j + 1) * TB],
                                   AluOpType.mult))
                ch_g[kj] = s_V.v
                stx = stce if (j & 1) == 0 else stco
                stcol = stx[:, c * 8 + (j >> 1): c * 8 + (j >> 1) + 1]
                vi(v.tensor_scalar(stcol, HB[j & 1][:, TB - 1:TB], 1.0, None,
                                   AluOpType.mult))
                if j == 0:
                    t.wait_ge(s_V.h, ytmp_g.get((k, c - 1) if c else (k - 1, NCH_I - 1), 0))
                t.wait_ge(s_V.h, ch_g[kj])
                ti(t.matmul(ps_y0[:, :], eyeb[:], chb[:],
                            start=(j == 0), stop=(j == D_STATE - 1)))
                mm_g[kj] = s_T.v

            for si_ in range(D_STATE):
                kj = c_abs * D_STATE + si_
                ab = abuf[:, (si_ & 1) * TB:((si_ & 1) + 1) * TB]
                s.wait_ge(s_V.h, scan_g.get(kj - 2, 0))
                si(s.activation(out=ab, in_=dt_ch, func=AF.Exp,
                                scale=abc[:, si_:si_ + 1]))
                ab_cnt = s_S.v
                if si_ + 1 < D_STATE:
                    vi(v.tensor_tensor(UB[(si_ + 1) & 1][:], vch[:],
                                       brep[:, (si_ + 1) * TB:(si_ + 2) * TB],
                                       AluOpType.mult))
                stx = stce if (si_ & 1) == 0 else stco
                stcol = stx[:, c * 8 + (si_ >> 1): c * 8 + (si_ >> 1) + 1]
                v.wait_ge(s_S.h, ab_cnt)
                scan_g[kj] = s_V.v + 1
                vi(v.tensor_tensor_scan(HB[si_ & 1][:], ab, UB[si_ & 1][:], stcol,
                                        AluOpType.mult, AluOpType.add))
                if si_ > 0:
                    tail(si_ - 1)
            tail(D_STATE - 1)
            v.wait_ge(s_T.h, mm_g[c_abs * D_STATE + D_STATE - 1])
            v.wait_ge(s_S.h, zsilu_g[(k, c)])
            vi(v.tensor_tensor(ytmp[:], ps_y0[:, :], dxsb[:], AluOpType.add))
            ytmp_g[(k, c)] = s_V.v
            vi(v.tensor_tensor(gat2[k & 1][:, c * TB:(c + 1) * TB], ytmp[:],
                               szt[:, c * TB:(c + 1) * TB], AluOpType.mult))
            gate_g[(k, c)] = s_V.v
            if c == NCH_I - 1:
                gatdone_g[k] = s_V.v
            if debug:
                g.wait_ge(s_V.h, s_V.v)
                t0k = k * TB
                dma(g, q_g, out=dbg["d_xs"][c * 128:(c + 1) * 128, t0k:t0k + TB],
                    in_=xs_c(c))
                dma(g, q_g, out=dbg["d_dt"][c * 128:(c + 1) * 128, t0k:t0k + TB],
                    in_=dt_ch)
                dma(g, q_g, out=dbg["d_gy"][c * 128:(c + 1) * 128, t0k:t0k + TB],
                    in_=gat2[k & 1][:, c * TB:(c + 1) * TB])

        def front_agenda(j, k_cur, ag, tailp):
            ag[4].append(fr_load_xt(j))
            ag[5].append(fr_ln1_stats(j))
            ag[6].append(fr_ln1_apply(j))
            ag[7].append(fr_win_xr_mm(j, [0, 1], k_cur))
            ag[8].append(fr_win_xr_ev(j, [0, 1], k_cur))
            ag[8].append(fr_win_xr_mm(j, [2, 3], k_cur))
            ag[9].append(fr_win_xr_ev(j, [2, 3], k_cur))
            ag[9].append(fr_win_xr_mm(j, [4, 5], k_cur))
            ag[10].append(fr_win_xr_ev(j, [4, 5], k_cur))
            ag[10].append(fr_win_xr_mm(j, [6, 7], k_cur))
            ag[11].append(fr_win_xr_ev(j, [6, 7], k_cur))
            ag[11].append(fr_win_xr_mm(j, [8, 9], k_cur))
            ag[12].append(fr_win_xr_ev(j, [8, 9], k_cur))
            ag[12].append(fr_win_xr_mm(j, [10, 11], k_cur))
            ag[13].append(fr_win_xr_ev(j, [10, 11], k_cur))
            ag[13].append(fr_win_xr_mm(j, [12, 13], k_cur))
            ag[14].append(fr_win_xr_ev(j, [12, 13], k_cur))
            ag[14].append(fr_win_xr_mm(j, [14], k_cur))
            ag[15].append(fr_win_xr_ev(j, [14], k_cur))
            ag[15].append(fr_win_xr_mm(j, [15], k_cur))
            ag[9].append(fr_conv(j, [0, 1, 2, 3]))
            ag[9].append(fr_win_z(j, [0, 1, 2, 3], k_cur))
            ag[12].append(fr_win_z(j, [4, 5, 6, 7], k_cur))
            ag[13].append(fr_conv(j, [4, 5, 6, 7, 8, 9]))
            ag[15].append(fr_conv(j, [10, 11, 12, 13]))
            ag[15].append(fr_win_z(j, [8, 9, 10, 11, 12, 13, 14], k_cur))
            tailp.append(fr_win_xr_ev(j, [15], k_cur))
            tailp.append(fr_conv(j, [14, 15]))
            tailp.append(fr_win_z(j, [15], k_cur))
            tailp.append(fr_wx_wdt(j))

        fr_load_xt(0)()
        fr_ln1_stats(0)()
        fr_ln1_apply(0)()
        for m0 in range(0, NCH_I, 2):
            fr_win_xr_mm(0, [m0, m0 + 1], None)()
            fr_win_xr_ev(0, [m0, m0 + 1], None)()
        for c0 in range(0, NCH_I, 2):
            fr_conv(0, [c0, c0 + 1])()
        for j0 in range(0, NCH_I, 2):
            fr_win_z(0, [j0, j0 + 1], None)()
        fr_wx_wdt(0)()

        for k in range(NB):
            ag = {cc: [] for cc in range(NCH_I)}
            tailp = []
            b = k - 1
            if b >= 0:
                ag[0].append(bk_wout_mm(b, [0, 1]))
                ag[1].append(bk_wout_x1(b, [0, 1]))
                ag[1].append(bk_wout_mm(b, [2, 3]))
                ag[2].append(bk_wout_x1(b, [2, 3]))
                ag[2].append(bk_wout_mm(b, [4, 5]))
                ag[3].append(bk_wout_x1(b, [4, 5]))
                ag[3].append(bk_wout_mm(b, [6, 7]))
                ag[4].append(bk_wout_x1(b, [6, 7]))
                ag[4].append(bk_ln2_stats(b))
                ag[5].append(bk_ln2_apply(b))
                ag[6].append(bk_ffn_mix(b, [0, 1, 2, 3]))
                ag[7].append(bk_ffn_mix(b, [4, 5, 6, 7]))
                ag[8].append(bk_wrk(b, "r", [0, 1, 2, 3]))
                ag[10].append(bk_wrk(b, "r", [4, 5, 6, 7]))
                ag[12].append(bk_wrk(b, "k", [0, 1, 2, 3]))
                ag[13].append(bk_wrk(b, "k", [4, 5, 6, 7]))
                wvp = [bk_wv(b, [0, 1]), bk_wv(b, [2, 3]),
                       bk_wv(b, [4, 5]), bk_wv(b, [6, 7])]
            else:
                wvp = []
            j = k + 1
            if j < NB:
                front_agenda(j, k, ag, tailp)
            if tailp and tailp[-1].__name__ == "f":
                pass
            wx = [fn for fn in tailp if fn.__qualname__.startswith("build_program.<locals>.fr_wx_wdt")]
            pre = [fn for fn in tailp if fn not in wx]
            tailp = pre + wvp + wx

            for c in range(NCH_I):
                scan_section(k, c)
                for fn in ag[c]:
                    fn()
            for fn in tailp:
                fn()

        b = NB - 1
        for m0 in range(0, NCH_M, 2):
            bk_wout_mm(b, [m0, m0 + 1])()
            bk_wout_x1(b, [m0, m0 + 1])()
        bk_ln2_stats(b)()
        bk_ln2_apply(b)()
        bk_ffn_mix(b, list(range(NCH_M)))()
        for m0 in range(0, NCH_M, 2):
            bk_wrk(b, "r", [m0, m0 + 1])()
        for m0 in range(0, NCH_M, 2):
            bk_wrk(b, "k", [m0, m0 + 1])()
        for m0 in range(0, NCH_M, 2):
            bk_wv(b, [m0, m0 + 1])()

        g.wait_ge(q_o.h, q_o.v)
        g.wait_ge(q_o2.h, q_o2.v)
        if debug:
            g.wait_ge(q_g.h, q_g.v)

        @block.sync
        def _(e):
            sy.replay(e)

        @block.gpsimd
        def _(e):
            g.replay(e)

        @block.tensor
        def _(e):
            t.replay(e)

        @block.vector
        def _(e):
            v.replay(e)

        @block.scalar
        def _(e):
            s.replay(e)

    return nc


_PROG_CACHE = {}


def _get_prog(T, TB, debug=False):
    key = (T, TB, debug)
    if key not in _PROG_CACHE:
        _PROG_CACHE[key] = build_program(T, TB, debug)
    return _PROG_CACHE[key]


def make_in_map(b, T, L0_attn_x, L0_attn_W_in, L0_attn_conv_w, L0_attn_conv_b,
                L0_attn_A_log, L0_attn_D, L0_attn_W_x, L0_attn_W_dt, L0_attn_b_dt,
                L0_attn_W_out, L0_ffn_W_r, L0_ffn_W_k, L0_ffn_W_v, L0_ffn_mu_r,
                L0_ffn_mu_k):
    C = np.ascontiguousarray
    f = np.float32
    _bf16 = ml_dtypes.bfloat16
    return {
        "xT": C(np.asarray(L0_attn_x)[b, :T, :].T.astype(f)),
        "W_inT": C(np.asarray(L0_attn_W_in).T.astype(f)),
        "conv_wc": C(np.asarray(L0_attn_conv_w).reshape(NCH_I, 128, 4)
                     .transpose(1, 0, 2).reshape(128, NCH_I * 4).astype(f)),
        "conv_bc": C(np.asarray(L0_attn_conv_b).reshape(NCH_I, 128).T.astype(f)),
        "A_log_r": C(np.asarray(L0_attn_A_log).reshape(1, D_STATE).astype(f)),
        "D_c": C(np.asarray(L0_attn_D).reshape(NCH_I, 128).T.astype(f)),
        "W_xT": C(np.asarray(L0_attn_W_x).T.astype(f)),
        "W_dtT": C(np.asarray(L0_attn_W_dt).T.astype(f)),
        "b_dtc": C(np.asarray(L0_attn_b_dt).reshape(NCH_I, 128).T.astype(f)),
        "W_outT": C(np.asarray(L0_attn_W_out).T.astype(f)),
        "W_rT": C(np.asarray(L0_ffn_W_r).T.astype(f)),
        "W_kT": C(np.asarray(L0_ffn_W_k).T.astype(f)),
        "W_vT": C(np.asarray(L0_ffn_W_v).T.astype(f)),
        "mu_rc": C(np.asarray(L0_ffn_mu_r).reshape(NCH_M, 128).T.astype(f)),
        "mu_kc": C(np.asarray(L0_ffn_mu_k).reshape(NCH_M, 128).T.astype(f)),
        "eye_in": np.eye(128, dtype=_bf16),
    }


def kernel(**inputs):
    T = 8192
    nc = _get_prog(T, 512)
    in_maps = [make_in_map(c >> 1, T, **inputs) for c in range(8)]
    res = run_bass_kernel_spmd(nc, in_maps, core_ids=list(range(8)))
    B = 4
    outs = []
    for b in range(B):
        o = res.results[2 * b]["out"]
        outs.append(np.ascontiguousarray(o.T)[None])
    return np.concatenate(outs, axis=0)


# version 07
# speedup vs baseline: 1.2454x; 1.0067x over previous
import numpy as np
import ml_dtypes
import concourse.bass as bass
import concourse.mybir as mybir
from concourse.bass_utils import run_bass_kernel_spmd
from concourse.alu_op_type import AluOpType

F32 = mybir.dt.float32
BF16 = mybir.dt.bfloat16
AF = mybir.ActivationFunctionType

D_MODEL = 1024
D_INNER = 2048
D_STATE = 16
DT_RANK = 64
D_CONV = 4
NCH_M = D_MODEL // 128
NCH_I = D_INNER // 128
EPS = 1e-5


class Sem:
    __slots__ = ("h", "v")
    def __init__(self, h):
        self.h = h
        self.v = 0


def build_program(T, TB, debug=False):
    NB = T // TB
    assert TB == 512
    nc = bass.Bass("TRN2", target_bir_lowering=False)

    dr = lambda n, sh, dt=F32: nc.dram_tensor(n, sh, dt, kind="ExternalInput").ap()
    xT = dr("xT", [D_MODEL, T])
    W_inT = dr("W_inT", [D_MODEL, 2 * D_INNER])
    conv_w = dr("conv_wc", [128, NCH_I * D_CONV])
    conv_b = dr("conv_bc", [128, NCH_I])
    A_log = dr("A_log_r", [1, D_STATE])
    D_in = dr("D_c", [128, NCH_I])
    W_xT = dr("W_xT", [D_INNER, 96])
    W_dtT = dr("W_dtT", [DT_RANK, D_INNER])
    b_dt = dr("b_dtc", [128, NCH_I])
    W_outT = dr("W_outT", [D_INNER, D_MODEL])
    W_rT = dr("W_rT", [D_MODEL, D_MODEL])
    W_kT = dr("W_kT", [D_MODEL, D_MODEL])
    W_vT = dr("W_vT", [D_MODEL, D_MODEL])
    mu_r = dr("mu_rc", [128, NCH_M])
    mu_k = dr("mu_kc", [128, NCH_M])
    eye_in = dr("eye_in", [128, 128], BF16)

    out = nc.dram_tensor("out", [D_MODEL, T], F32, kind="ExternalOutput").ap()
    dbg = {}
    if debug:
        for nm, rows_ in (("d_ln", D_MODEL), ("d_xs", D_INNER), ("d_dt", D_INNER),
                          ("d_gy", D_INNER), ("d_x1", D_MODEL), ("d_ln2", D_MODEL)):
            dbg[nm] = nc.dram_tensor(nm, [rows_, T], F32, kind="ExternalOutput").ap()

    scr = lambda n, sh, dt=F32: nc.dram_tensor(n, sh, dt, kind="Internal").ap()
    W_inT_bf = scr("W_inT_bf", [D_MODEL, 2 * D_INNER], BF16)
    W_outT_bf = scr("W_outT_bf", [D_INNER, D_MODEL], BF16)
    W_rT_bf = scr("W_rT_bf", [D_MODEL, D_MODEL], BF16)
    W_kT_bf = scr("W_kT_bf", [D_MODEL, D_MODEL], BF16)
    W_vT_bf = scr("W_vT_bf", [D_MODEL, D_MODEL], BF16)
    W_dtT_bf = scr("W_dtT_bf", [DT_RANK, D_INNER], BF16)
    W_xT_bf = scr("W_xT_bf", [D_INNER, 96], BF16)
    bounce = [scr(f"bounce{p}", [4, TB]) for p in range(2)]
    bbc = [scr(f"bbc{p}", [32, TB], BF16) for p in range(2)]
    a_bounce = scr("a_bounce", [1, D_STATE])

    from contextlib import ExitStack
    _st = ExitStack()
    with _st:
        xt = _st.enter_context(nc.sbuf_tensor("xt", [128, NCH_M * TB], F32))
        xre = xt.ap()[:, 0:2 * TB]
        xsqa = _st.enter_context(nc.sbuf_tensor("xsqa", [128, TB], F32))
        xsqb = _st.enter_context(nc.sbuf_tensor("xsqb", [128, TB], F32))
        lnt = _st.enter_context(nc.sbuf_tensor("lnt", [128, NCH_M * TB], BF16))
        ln2t = _st.enter_context(nc.sbuf_tensor("ln2t", [128, NCH_M * TB], BF16))
        rows = _st.enter_context(nc.sbuf_tensor("rows", [1, 3 * TB], F32))
        mrep = _st.enter_context(nc.sbuf_tensor("mrep", [128, TB], BF16))
        rrep = _st.enter_context(nc.sbuf_tensor("rrep", [128, TB], BF16))
        xrt = _st.enter_context(nc.sbuf_tensor("xrt", [128, NCH_I * (TB + 3)], BF16))
        halo = _st.enter_context(nc.sbuf_tensor("halo", [128, NCH_I * 3], BF16))
        convacc = _st.enter_context(nc.sbuf_tensor("convacc", [128, 2 * TB], F32))
        szt = _st.enter_context(nc.sbuf_tensor("szt", [128, NCH_I * TB], BF16))
        dtt = _st.enter_context(nc.sbuf_tensor("dtt", [128, NCH_I * TB], BF16))
        xdbl = _st.enter_context(nc.sbuf_tensor("xdbl", [96, TB], BF16))
        dtr = None
        brep = _st.enter_context(nc.sbuf_tensor("brep", [128, D_STATE * TB], BF16))
        crep = _st.enter_context(nc.sbuf_tensor("crep", [128, D_STATE * TB], BF16))
        vch = _st.enter_context(nc.sbuf_tensor("vch", [128, TB], BF16))
        dxsb = _st.enter_context(nc.sbuf_tensor("dxsb", [128, TB], BF16))
        abuf = _st.enter_context(nc.sbuf_tensor("abuf", [128, 2 * TB], F32))
        ube = _st.enter_context(nc.sbuf_tensor("ube", [128, TB], BF16))
        ubo = _st.enter_context(nc.sbuf_tensor("ubo", [128, TB], BF16))
        hbe = _st.enter_context(nc.sbuf_tensor("hbe", [128, TB], BF16))
        hbo = _st.enter_context(nc.sbuf_tensor("hbo", [128, TB], BF16))
        chb4 = [_st.enter_context(nc.sbuf_tensor(f"chb{i}", [128, TB], BF16))
                for i in range(3)]
        ytmp = _st.enter_context(nc.sbuf_tensor("ytmp", [128, TB], F32))
        stce = _st.enter_context(nc.sbuf_tensor("stce", [128, NCH_I * 8], F32))
        stco = _st.enter_context(nc.sbuf_tensor("stco", [128, NCH_I * 8], F32))
        gat2 = [_st.enter_context(nc.sbuf_tensor(f"gat{i}", [128, NCH_I * TB], BF16))
                for i in range(2)]
        x1t = _st.enter_context(nc.sbuf_tensor("x1t", [128, NCH_M * TB], F32))
        ln2p = _st.enter_context(nc.sbuf_tensor("ln2p", [128, TB], BF16))
        carry = _st.enter_context(nc.sbuf_tensor("carry", [128, NCH_M], BF16))
        ffntmp = _st.enter_context(nc.sbuf_tensor("ffntmp", [128, TB], BF16))
        x2t = _st.enter_context(nc.sbuf_tensor("x2t", [128, TB], F32))
        wslab = _st.enter_context(nc.sbuf_tensor("wslab", [128, 2 * 16 * 128], BF16))
        cwc = _st.enter_context(nc.sbuf_tensor("cwc", [128, NCH_I * D_CONV], F32))
        cbc = _st.enter_context(nc.sbuf_tensor("cbc", [128, NCH_I], F32))
        bdtc = _st.enter_context(nc.sbuf_tensor("bdtc", [128, NCH_I], F32))
        dc = _st.enter_context(nc.sbuf_tensor("dc", [128, NCH_I], F32))
        murc = _st.enter_context(nc.sbuf_tensor("murc", [128, NCH_M], F32))
        mukc = _st.enter_context(nc.sbuf_tensor("mukc", [128, NCH_M], F32))
        omrc = _st.enter_context(nc.sbuf_tensor("omrc", [128, NCH_M], F32))
        omkc = _st.enter_context(nc.sbuf_tensor("omkc", [128, NCH_M], F32))
        arow = _st.enter_context(nc.sbuf_tensor("arow", [1, D_STATE], F32))
        abc = _st.enter_context(nc.sbuf_tensor("abc", [128, D_STATE], F32))
        eyeb = _st.enter_context(nc.sbuf_tensor("eyeb", [128, 128], BF16))
        onesf = _st.enter_context(nc.sbuf_tensor("onesf", [128, 1], F32))
        rt = ln2t
        def kat_of(b):
            return gat2[b & 1].ap()[:, 0:NCH_M * TB]
        ps_a = _st.enter_context(nc.psum_tensor("ps_a", [128, 512], F32))
        ps_b = _st.enter_context(nc.psum_tensor("ps_b", [128, 512], F32))
        ps_c = _st.enter_context(nc.psum_tensor("ps_c", [128, 512], F32))
        ps_d = _st.enter_context(nc.psum_tensor("ps_d", [128, 512], F32))
        ps_y0 = _st.enter_context(nc.psum_tensor("ps_y0", [128, 512], F32))
        ps_y1 = _st.enter_context(nc.psum_tensor("ps_y1", [128, 512], F32))
        ps_s1 = _st.enter_context(nc.psum_tensor("ps_s1", [1, 512], F32))
        ps_s2 = _st.enter_context(nc.psum_tensor("ps_s2", [1, 512], F32))
        _q_w = _st.enter_context(nc.semaphore("q_w"))
        _q_w2 = _st.enter_context(nc.semaphore("q_w2"))
        _q_w3 = _st.enter_context(nc.semaphore("q_w3"))
        _q_x = _st.enter_context(nc.semaphore("q_x"))
        _q_g = _st.enter_context(nc.semaphore("q_g"))
        _q_o = _st.enter_context(nc.semaphore("q_o"))
        _q_o2 = _st.enter_context(nc.semaphore("q_o2"))
        _s_T = _st.enter_context(nc.semaphore("s_T"))
        _s_V = _st.enter_context(nc.semaphore("s_V"))
        _s_S = _st.enter_context(nc.semaphore("s_S"))
        _s_G = _st.enter_context(nc.semaphore("s_G"))
        block = _st.enter_context(nc.Block())
        q_w, q_x, q_g, q_o = Sem(_q_w), Sem(_q_x), Sem(_q_g), Sem(_q_o)
        q_w2, q_w3, q_o2 = Sem(_q_w2), Sem(_q_w3), Sem(_q_o2)
        q_ws = [q_w, q_w2]
        q_b = q_w3
        q_os = [q_o, q_o2]
        s_T, s_V, s_S, s_G = Sem(_s_T), Sem(_s_V), Sem(_s_S), Sem(_s_G)

        class _Pending:
            __slots__ = ("incs",)
            def __init__(self):
                self.incs = []
            def then_inc(self, sem, n):
                self.incs.append((sem, n))
                return self

        def _nm(x):
            try:
                return x.tensor.name
            except AttributeError:
                return None

        _IOMAP = {
            "tensor_tensor": lambda a, kw: ((a[1], a[2]), (a[0],)),
            "tensor_scalar": lambda a, kw: ((a[1], a[2], a[3]), (a[0],)),
            "scalar_tensor_tensor": lambda a, kw: ((a[1], a[2], a[3]), (a[0],)),
            "tensor_tensor_scan": lambda a, kw: ((a[1], a[2], a[3]), (a[0],)),
            "tensor_copy": lambda a, kw: ((a[1],), (a[0],)),
            "memset": lambda a, kw: ((), (a[0],)),
            "activation": lambda a, kw: (
                (kw.get("in_"), kw.get("bias"), kw.get("scale")),
                (kw.get("out"),)),
            "matmul": lambda a, kw: ((a[1], a[2]), ()),
        }

        class _Rec:
            def __init__(self, sem=None):
                self.ops = []
                self.sem = sem
                self.waited = 0
                self.lw = {}
                self.lr = {}

            def _auto(self, meth, a, kw):
                if self.sem is None:
                    return
                if meth == "wait_ge":
                    if a and a[0] is self.sem.h and a[1] > self.waited:
                        self.waited = a[1]
                    return
                io = _IOMAP.get(meth)
                if io is None:
                    return
                rd, wr = io(a, kw)
                need = 0
                for x in rd:
                    n = _nm(x)
                    if n:
                        need = max(need, self.lw.get(n, 0))
                for x in wr:
                    n = _nm(x)
                    if n:
                        need = max(need, self.lw.get(n, 0), self.lr.get(n, 0))
                if need > self.waited:
                    self.ops.append(("wait_ge", (self.sem.h, need), {}, _Pending()))
                    self.waited = need
                cnt = self.sem.v + 1
                for x in rd:
                    n = _nm(x)
                    if n:
                        self.lr[n] = cnt
                for x in wr:
                    n = _nm(x)
                    if n:
                        self.lw[n] = cnt

            def __getattr__(self, meth):
                def call(*a, **kw):
                    self._auto(meth, a, kw)
                    p = _Pending()
                    self.ops.append((meth, a, kw, p))
                    return p
                return call

            def replay(self, eng):
                for meth, a, kw, p in self.ops:
                    instr = getattr(eng, meth)(*a, **kw)
                    for sem, n in p.incs:
                        instr.then_inc(sem, n)

        sy = _Rec()
        g = _Rec(s_G)
        t, v, s = _Rec(s_T), _Rec(s_V), _Rec(s_S)
        PS = [ps_a, ps_b, ps_c, ps_d]

        def dma(eng, q, **kw):
            eng.dma_start(**kw).then_inc(q.h, 16)
            q.v += 16

        def vi(instr):
            instr.then_inc(s_V.h, 1)
            s_V.v += 1

        def si(instr):
            instr.then_inc(s_S.h, 1)
            s_S.v += 1

        def ti(instr):
            instr.then_inc(s_T.h, 1)
            s_T.v += 1

        def gi(instr):
            instr.then_inc(s_G.h, 1)
            s_G.v += 1

        dma(sy, q_x, out=cwc[:], in_=conv_w[:])
        dma(sy, q_x, out=cbc[:], in_=conv_b[:])
        dma(sy, q_x, out=bdtc[:], in_=b_dt[:])
        dma(sy, q_x, out=dc[:], in_=D_in[:])
        dma(sy, q_x, out=murc[:], in_=mu_r[:])
        dma(sy, q_x, out=mukc[:], in_=mu_k[:])
        dma(sy, q_x, out=arow[:], in_=A_log[:])
        dma(sy, q_x, out=eyeb[:], in_=eye_in[:])
        s.wait_ge(q_x.h, q_x.v)
        si(s.activation(out=arow[:], in_=arow[:], func=AF.Exp))
        v.wait_ge(s_S.h, s_S.v)
        vi(v.tensor_scalar(arow[:], arow[:], -1.0, None, AluOpType.mult))
        g.wait_ge(s_V.h, s_V.v)
        dma(g, q_g, out=a_bounce[:], in_=arow[:])
        g.wait_ge(q_g.h, q_g.v)
        dma(g, q_g, out=abc[:], in_=a_bounce.broadcast_to((128, D_STATE)))

        v.wait_ge(q_x.h, q_x.v)
        vi(v.memset(onesf[:], 1.0))
        vi(v.memset(stce[:], 0.0))
        vi(v.memset(stco[:], 0.0))
        vi(v.memset(halo[:], 0.0))
        vi(v.memset(carry[:], 0.0))
        vi(v.tensor_scalar(omrc[:], murc[:], -1.0, 1.0, AluOpType.mult, AluOpType.add))
        vi(v.tensor_scalar(omkc[:], mukc[:], -1.0, 1.0, AluOpType.mult, AluOpType.add))

        def cast_weight(src, dst, R, C):
            for r0 in range(0, R, 128):
                sy.wait_ge(q_g.h, q_g.v)
                dma(sy, q_w, out=xt[:, :C], in_=src[r0:r0 + 128, :])
                v.wait_ge(q_w.h, q_w.v)
                vi(v.tensor_copy(gat2[0][:, :C], xt[:, :C]))
                g.wait_ge(s_V.h, s_V.v)
                dma(g, q_g, out=dst[r0:r0 + 128, :], in_=gat2[0][:, :C])

        cast_weight(W_inT, W_inT_bf, D_MODEL, 2 * D_INNER)
        cast_weight(W_outT, W_outT_bf, D_INNER, D_MODEL)
        cast_weight(W_rT, W_rT_bf, D_MODEL, D_MODEL)
        cast_weight(W_kT, W_kT_bf, D_MODEL, D_MODEL)
        cast_weight(W_vT, W_vT_bf, D_MODEL, D_MODEL)
        cast_weight(W_xT, W_xT_bf, D_INNER, 96)
        sy.wait_ge(q_g.h, q_g.v)
        dma(sy, q_w, out=xt[0:64, :D_INNER], in_=W_dtT[:, :])
        v.wait_ge(q_w.h, q_w.v)
        vi(v.tensor_copy(gat2[0][0:64, :D_INNER], xt[0:64, :D_INNER]))
        g.wait_ge(s_V.h, s_V.v)
        dma(g, q_g, out=W_dtT_bf[:, :], in_=gat2[0][0:64, :D_INNER])
        g.wait_ge(q_g.h, q_g.v)

        SLABW = 16 * 128
        slab_guard = [0, 0]
        slab_cnt = [0, 0]
        slab_rot = [0]

        def slab_load(src_bf, R, m0):
            sl = slab_rot[0]
            slab_rot[0] = (sl + 1) % 2
            nk = R // 128
            sy.wait_ge(s_T.h, slab_guard[sl])
            dma(sy, q_ws[sl],
                out=wslab[:, sl * SLABW: sl * SLABW + nk * 128]
                .rearrange("p (k c) -> p k c", c=128),
                in_=src_bf[:, m0 * 128:(m0 + 1) * 128]
                .rearrange("(k p) c -> p k c", p=128))
            slab_cnt[sl] = q_ws[sl].v
            return sl, q_ws[sl].v

        def slab_seq(loader, ms, body):
            nxt = loader(ms[0])
            for i, m in enumerate(ms):
                sl, cnt = nxt
                if i + 1 < len(ms):
                    nxt = loader(ms[i + 1])
                t.wait_ge(q_ws[sl].h, cnt)
                body(sl, m)

        def slab_load_dt():
            sl = slab_rot[0]
            slab_rot[0] = (sl + 1) % 2
            sy.wait_ge(s_T.h, slab_guard[sl])
            dma(sy, q_ws[sl], out=wslab[0:64, sl * SLABW: sl * SLABW + D_INNER],
                in_=W_dtT_bf[:, :])
            slab_cnt[sl] = q_ws[sl].v
            return sl, q_ws[sl].v

        def xs_c(c):
            return xrt[:, c * (TB + 3) + 3:(c + 1) * (TB + 3)]

        scan_g = {}
        ch_g = {}
        mm_g = {}
        ytmp_g = {}
        gate_g = {}
        vchfree = {}
        silu_g = {}
        zsilu_g = {}
        dt_g = {}
        lnap_g = {}
        x1_g = {}
        mix_g = {}
        dtr_g = {}
        brep_q = {}
        gatdone_g = {}
        rk_g = {}
        wv_g = {}
        wvmm_g = {}
        wvm_g = {}
        winmm_g = {}
        woutmm_g = {}
        xre_q = {}
        brep_si = {}
        winev_g = {}
        out_cnt = {0: 0, 1: 0}
        ln_ctr = [0]
        conv_prev = [0]

        def ln_stats_sq(src_t):
            t.wait_ge(s_V.h, s_V.v)
            for c in range(NCH_M):
                sl = src_t[:, c * TB:(c + 1) * TB]
                xsq = xsqa if (c & 1) == 0 else xsqb
                s.wait_ge(s_T.h, s_T.v)
                si(s.activation(out=xsq[:], in_=sl, func=AF.Square))
                t.wait_ge(s_S.h, s_S.v)
                ti(t.matmul(ps_s1[:, :], onesf[:], sl, start=(c == 0), stop=(c == NCH_M - 1)))
                ti(t.matmul(ps_s2[:, :], onesf[:], xsq[:], start=(c == 0), stop=(c == NCH_M - 1)))

        def ln_stats_fin():
            par = ln_ctr[0] & 1
            ln_ctr[0] += 1
            v.wait_ge(s_T.h, s_T.v)
            r_mean = rows[0:1, 0:TB]
            r_q = rows[0:1, TB:2 * TB]
            r_var = rows[0:1, 2 * TB:3 * TB]
            vi(v.tensor_scalar(r_mean, ps_s1[0:1, :], 1.0 / D_MODEL, None, AluOpType.mult))
            vi(v.tensor_scalar(r_q, ps_s2[0:1, :], 1.0 / D_MODEL, None, AluOpType.mult))
            vi(v.tensor_tensor(r_var, r_mean, r_mean, AluOpType.mult))
            vi(v.tensor_tensor(r_var, r_q, r_var, AluOpType.subtract))
            vi(v.tensor_scalar(r_var, r_var, EPS, None, AluOpType.add))
            s.wait_ge(s_V.h, s_V.v)
            si(s.activation(out=r_var, in_=r_var, func=AF.Ln))
            si(s.activation(out=r_var, in_=r_var, func=AF.Exp, scale=-0.5))
            g.wait_ge(s_V.h, s_V.v)
            g.wait_ge(s_S.h, s_S.v)
            dma(g, q_g, out=bounce[par][0:1, :], in_=r_mean)
            dma(g, q_g, out=bounce[par][1:2, :], in_=r_var)
            g.wait_ge(q_g.h, q_g.v)
            dma(g, q_g, out=mrep[:], in_=bounce[par][0:1, :].broadcast_to((128, TB)))
            dma(g, q_g, out=rrep[:], in_=bounce[par][1:2, :].broadcast_to((128, TB)))

        def ln_apply(src_t, dst_t):
            v.wait_ge(q_g.h, q_g.v)
            for c in range(NCH_M):
                sl = src_t[:, c * TB:(c + 1) * TB]
                dl = dst_t[:, c * TB:(c + 1) * TB]
                acc = convacc[:, (c & 1) * TB:((c & 1) + 1) * TB]
                vi(v.tensor_tensor(acc, sl, mrep[:], AluOpType.subtract))
                vi(v.tensor_tensor(dl, acc, rrep[:], AluOpType.mult))

        def fr_load_xt(j):
            def f():
                sy.wait_ge(s_V.h, s_V.v)
                sy.wait_ge(s_S.h, s_S.v)
                sy.wait_ge(s_T.h, s_T.v)
                t0j = j * TB
                for c in range(NCH_M):
                    dma(sy, q_x, out=xt[:, c * TB:(c + 1) * TB],
                        in_=xT[c * 128:(c + 1) * 128, t0j:t0j + TB])
            return f

        def fr_ln1_sq(j):
            def f():
                s.wait_ge(q_x.h, q_x.v)
                t.wait_ge(q_x.h, q_x.v)
                ln_stats_sq(xt)
            return f

        def fr_ln1_fin(j):
            def f():
                ln_stats_fin()
            return f

        def fr_ln1_apply(j):
            def f():
                if debug:
                    v.wait_ge(q_g.h, q_g.v)
                ln_apply(xt, lnt)
                lnap_g[("1", j)] = s_V.v
                if debug:
                    g.wait_ge(s_V.h, s_V.v)
                    for c in range(NCH_M):
                        dma(g, q_g, out=dbg["d_ln"][c * 128:(c + 1) * 128,
                                                    j * TB:(j + 1) * TB],
                            in_=lnt[:, c * TB:(c + 1) * TB])
            return f

        def fr_win_xr_mm(j, ms, k_cur):
            def f():
                t.wait_ge(s_V.h, lnap_g[("1", j)])
                t.wait_ge(s_S.h, dt_g.get((j - 1, NCH_I - 1), 0))
                def body(sl, m):
                    ps = PS[2 + (m & 1)]
                    t.wait_ge(s_V.h, winev_g.get((j, m - 2), 0))
                    if k_cur is not None:
                        t.wait_ge(s_V.h, vchfree.get((k_cur, m), 0))
                    for kk in range(NCH_M):
                        ti(t.matmul(ps[:, :],
                                    wslab[:, sl * SLABW + kk * 128: sl * SLABW + (kk + 1) * 128],
                                    lnt[:, kk * TB:(kk + 1) * TB],
                                    start=(kk == 0), stop=(kk == NCH_M - 1)))
                    slab_guard[sl] = s_T.v
                    winmm_g[(j, m)] = s_T.v
                slab_seq(lambda m: slab_load(W_inT_bf, D_MODEL, m), ms, body)
            return f

        def fr_win_xr_ev(j, ms, k_cur):
            def f():
                if debug:
                    v.wait_ge(q_g.h, q_g.v)
                for m in ms:
                    ps = PS[2 + (m & 1)]
                    v.wait_ge(s_T.h, winmm_g[(j, m)])
                    vi(v.tensor_copy(xrt[:, m * (TB + 3) + 3:(m + 1) * (TB + 3)], ps[:, :]))
                    winev_g[(j, m)] = s_V.v
            return f

        def fr_win_z(j, js, k_cur):
            def f():
                t.wait_ge(s_V.h, lnap_g[("1", j)])
                def body(sl, jz):
                    t.wait_ge(s_S.h, zsilu_g.get((j, jz - 1),
                                                 zsilu_g.get((j - 1, NCH_I - 1), 0)))
                    if k_cur is not None:
                        t.wait_ge(s_V.h, gate_g.get((k_cur, jz), 0))
                    for kk in range(NCH_M):
                        ti(t.matmul(ps_y1[:, :],
                                    wslab[:, sl * SLABW + kk * 128: sl * SLABW + (kk + 1) * 128],
                                    lnt[:, kk * TB:(kk + 1) * TB],
                                    start=(kk == 0), stop=(kk == NCH_M - 1)))
                    slab_guard[sl] = s_T.v
                    s.wait_ge(s_T.h, s_T.v)
                    si(s.activation(out=szt[:, jz * TB:(jz + 1) * TB], in_=ps_y1[:, :],
                                    func=AF.Silu))
                    zsilu_g[(j, jz)] = s_S.v
                slab_seq(lambda jz: slab_load(W_inT_bf, D_MODEL, NCH_I + jz), js, body)
            return f

        def fr_conv(j, cs):
            def f():
                for c in cs:
                    xr_ch = xrt[:, c * (TB + 3):(c + 1) * (TB + 3)]
                    acc0 = convacc[:, 0:TB]
                    acc1 = convacc[:, TB:2 * TB]
                    v.wait_ge(s_S.h, conv_prev[0])
                    vi(v.tensor_copy(xr_ch[:, 0:3], halo[:, c * 3:(c + 1) * 3]))
                    vi(v.tensor_scalar(acc0, xr_ch[:, 0:TB], cwc[:, c * 4:c * 4 + 1],
                                       None, AluOpType.mult))
                    vi(v.scalar_tensor_tensor(acc1, xr_ch[:, 1:1 + TB],
                                              cwc[:, c * 4 + 1:c * 4 + 2], acc0,
                                              AluOpType.mult, AluOpType.add))
                    vi(v.scalar_tensor_tensor(acc0, xr_ch[:, 2:2 + TB],
                                              cwc[:, c * 4 + 2:c * 4 + 3], acc1,
                                              AluOpType.mult, AluOpType.add))
                    vi(v.scalar_tensor_tensor(acc1, xr_ch[:, 3:3 + TB],
                                              cwc[:, c * 4 + 3:c * 4 + 4], acc0,
                                              AluOpType.mult, AluOpType.add))
                    vi(v.tensor_copy(halo[:, c * 3:(c + 1) * 3], xr_ch[:, TB:TB + 3]))
                    s.wait_ge(s_V.h, s_V.v)
                    si(s.activation(out=xr_ch[:, 3:3 + TB], in_=acc1, func=AF.Silu,
                                    bias=cbc[:, c:c + 1]))
                    silu_g[(j, c)] = s_S.v
                    conv_prev[0] = s_S.v
            return f

        def fr_wx_wdt(j):
            def f():
                slx = slab_rot[0]
                slab_rot[0] = (slx + 1) % 2
                sy.wait_ge(s_T.h, slab_guard[slx])
                dma(sy, q_ws[slx],
                    out=wslab[:, slx * SLABW: slx * SLABW + NCH_I * 96]
                    .rearrange("p (k c) -> p k c", c=96),
                    in_=W_xT_bf.rearrange("(k p) c -> p k c", p=128))
                slab_cnt[slx] = q_ws[slx].v
                t.wait_ge(s_S.h, silu_g[(j, NCH_I - 1)])
                t.wait_ge(s_S.h, dt_g.get((j - 1, NCH_I - 1), 0))
                t.wait_ge(s_V.h, s_V.v)
                t.wait_ge(q_ws[slx].h, slab_cnt[slx])
                for kk in range(NCH_I):
                    ti(t.matmul(ps_c[0:96, :],
                                wslab[:, slx * SLABW + kk * 96: slx * SLABW + (kk + 1) * 96],
                                xs_c(kk),
                                start=(kk == 0), stop=(kk == NCH_I - 1)))
                slab_guard[slx] = s_T.v
                v.wait_ge(s_T.h, s_T.v)
                v.wait_ge(q_g.h, q_g.v)
                vi(v.tensor_copy(xdbl[:, :], ps_c[0:96, :]))
                dtr_g[j] = s_V.v
                par = j & 1
                g.wait_ge(s_V.h, s_V.v)
                g.wait_ge(q_g.h, q_g.v)
                dma(g, q_g, out=bbc[par][:, :], in_=xdbl[64:96, :])
                g.wait_ge(q_g.h, q_g.v)
                for si_ in range(D_STATE):
                    dma(g, q_b, out=brep[:, si_ * TB:(si_ + 1) * TB],
                        in_=bbc[par][si_:si_ + 1, :].broadcast_to((128, TB)))
                    dma(g, q_b, out=crep[:, si_ * TB:(si_ + 1) * TB],
                        in_=bbc[par][16 + si_:17 + si_, :].broadcast_to((128, TB)))
                brep_q[j] = q_b.v
                sl, _cnt = slab_load_dt()
                if debug:
                    s.wait_ge(q_g.h, q_g.v)
                t.wait_ge(s_V.h, s_V.v)
                t.wait_ge(q_ws[sl].h, _cnt)
                for m in range(NCH_I):
                    ps = PS[2 + (m & 1)]
                    xsq = xsqb if (m & 1) == 0 else xsqa
                    if m >= 2:
                        t.wait_ge(s_S.h, dt_g[(j, m - 2)])
                    ti(t.matmul(ps[:, :], wslab[0:64, sl * SLABW + m * 128:
                                                sl * SLABW + (m + 1) * 128],
                                xdbl[0:64, :], start=True, stop=True))
                    s.wait_ge(s_T.h, s_T.v)
                    si(s.activation(out=xsq[:], in_=ps[:, :], func=AF.Exp,
                                    bias=bdtc[:, m:m + 1]))
                    si(s.activation(out=dtt[:, m * TB:(m + 1) * TB], in_=xsq[:],
                                    func=AF.Ln, bias=1.0))
                    dt_g[(j, m)] = s_S.v
                slab_guard[sl] = s_T.v
            return f

        def bk_wout_mm(b, ms):
            def f():
                t0b = b * TB
                sy.wait_ge(s_V.h, s_V.v)
                for m in ms:
                    dma(sy, q_x, out=xre[:, (m & 1) * TB:((m & 1) + 1) * TB],
                        in_=xT[m * 128:(m + 1) * 128, t0b:t0b + TB])
                xre_q[ms[0]] = q_x.v
                t.wait_ge(s_V.h, gatdone_g[b])
                t.wait_ge(s_V.h, wv_g.get(b - 1, 0))
                def body(sl, m):
                    ps = PS[m & 1]
                    t.wait_ge(s_V.h, x1_g.get((b, m - 2), 0))
                    for kk in range(NCH_I):
                        ti(t.matmul(ps[:, :],
                                    wslab[:, sl * SLABW + kk * 128: sl * SLABW + (kk + 1) * 128],
                                    gat2[b & 1][:, kk * TB:(kk + 1) * TB],
                                    start=(kk == 0), stop=(kk == NCH_I - 1)))
                    slab_guard[sl] = s_T.v
                    woutmm_g[(b, m)] = s_T.v
                slab_seq(lambda m: slab_load(W_outT_bf, D_INNER, m), ms, body)
            return f

        def bk_wout_x1(b, ms):
            def f():
                if debug:
                    v.wait_ge(q_g.h, q_g.v)
                for m in ms:
                    ps = PS[m & 1]
                    v.wait_ge(s_T.h, woutmm_g[(b, m)])
                    v.wait_ge(q_x.h, xre_q[ms[0]])
                    vi(v.tensor_tensor(x1t[:, m * TB:(m + 1) * TB], ps[:, :],
                                       xre[:, (m & 1) * TB:((m & 1) + 1) * TB],
                                       AluOpType.add))
                    x1_g[(b, m)] = s_V.v
                if debug and ms[-1] == NCH_M - 1:
                    g.wait_ge(s_V.h, s_V.v)
                    for c in range(NCH_M):
                        dma(g, q_g, out=dbg["d_x1"][c * 128:(c + 1) * 128,
                                                    b * TB:(b + 1) * TB],
                            in_=x1t[:, c * TB:(c + 1) * TB])
            return f

        def bk_ln2_sq(b):
            def f():
                ln_stats_sq(x1t)
            return f

        def bk_ln2_fin(b):
            def f():
                ln_stats_fin()
            return f

        def bk_ln2_apply(b):
            def f():
                ln_apply(x1t, ln2t)
                lnap_g[("2", b)] = s_V.v
                if debug:
                    g.wait_ge(s_V.h, s_V.v)
                    for c in range(NCH_M):
                        dma(g, q_g, out=dbg["d_ln2"][c * 128:(c + 1) * 128,
                                                     b * TB:(b + 1) * TB],
                            in_=ln2t[:, c * TB:(c + 1) * TB])
            return f

        def bk_ffn_mix(b, cs):
            xft = gat2[b & 1].ap()[:, 0:NCH_M * TB]
            xkt = gat2[b & 1].ap()[:, NCH_M * TB:2 * NCH_M * TB]
            def f():
                for c in cs:
                    l2 = ln2t[:, c * TB:(c + 1) * TB]
                    vi(v.tensor_copy(ln2p[:, 0:1], carry[:, c:c + 1]))
                    vi(v.tensor_copy(ln2p[:, 1:TB], l2[:, 0:TB - 1]))
                    vi(v.tensor_copy(carry[:, c:c + 1], l2[:, TB - 1:TB]))
                    xf = xft[:, c * TB:(c + 1) * TB]
                    xk = xkt[:, c * TB:(c + 1) * TB]
                    vi(v.tensor_scalar(xf, l2, murc[:, c:c + 1], None, AluOpType.mult))
                    vi(v.scalar_tensor_tensor(ffntmp[:], ln2p[:], omrc[:, c:c + 1], xf,
                                              AluOpType.mult, AluOpType.add))
                    vi(v.tensor_copy(xf, ffntmp[:]))
                    vi(v.tensor_scalar(xk, l2, mukc[:, c:c + 1], None, AluOpType.mult))
                    vi(v.scalar_tensor_tensor(ffntmp[:], ln2p[:], omkc[:, c:c + 1], xk,
                                              AluOpType.mult, AluOpType.add))
                    vi(v.tensor_copy(xk, ffntmp[:]))
                mix_g[b] = s_V.v
            return f

        def bk_wrk(b, which, ms):
            wt = W_rT_bf if which == "r" else W_kT_bf
            off = 0 if which == "r" else NCH_M * TB
            def f():
                src = gat2[b & 1].ap()[:, off:off + NCH_M * TB]
                if debug:
                    s.wait_ge(q_g.h, q_g.v)
                t.wait_ge(s_V.h, mix_g[b])
                def body(sl, m):
                    ps = PS[m & 1]
                    if which == "r":
                        t.wait_ge(s_V.h, x1_g[(b, 6 + (m & 1))])
                        t.wait_ge(s_S.h, rk_g.get((b, "r", m - 2), 0))
                    else:
                        prev = rk_g.get((b, "k", m - 2), rk_g.get((b, "r", 6 + (m & 1)), 0))
                        t.wait_ge(s_S.h, prev)
                    for kk in range(NCH_M):
                        ti(t.matmul(ps[:, :],
                                    wslab[:, sl * SLABW + kk * 128: sl * SLABW + (kk + 1) * 128],
                                    src[:, kk * TB:(kk + 1) * TB],
                                    start=(kk == 0), stop=(kk == NCH_M - 1)))
                    slab_guard[sl] = s_T.v
                    s.wait_ge(s_T.h, s_T.v)
                    if which == "r":
                        si(s.activation(out=rt[:, m * TB:(m + 1) * TB], in_=ps[:, :],
                                        func=AF.Sigmoid))
                    else:
                        xsq = xsqa if (m & 1) == 0 else xsqb
                        si(s.activation(out=xsq[:], in_=ps[:, :], func=AF.Relu))
                        si(s.activation(out=kat_of(b)[:, m * TB:(m + 1) * TB],
                                        in_=xsq[:], func=AF.Square))
                    rk_g[(b, which, m)] = s_S.v
                slab_seq(lambda m: slab_load(wt, D_MODEL, m), ms, body)
            return f

        def bk_wv_mm(b, ms):
            def f():
                t.wait_ge(s_S.h, rk_g[(b, "k", NCH_M - 1)])
                def body(sl, m):
                    ps = PS[m & 1]
                    t.wait_ge(s_S.h, rk_g.get((b, "k", 6 + (m & 1)), 0))
                    t.wait_ge(s_V.h, wv_g.get((b, m - 2), 0))
                    for kk in range(NCH_M):
                        ti(t.matmul(ps[:, :],
                                    wslab[:, sl * SLABW + kk * 128: sl * SLABW + (kk + 1) * 128],
                                    kat_of(b)[:, kk * TB:(kk + 1) * TB],
                                    start=(kk == 0), stop=(kk == NCH_M - 1)))
                    slab_guard[sl] = s_T.v
                    wvm_g[(b, m)] = s_T.v
                slab_seq(lambda m: slab_load(W_vT_bf, D_MODEL, m), ms, body)
                if ms[-1] == NCH_M - 1:
                    wvmm_g[b] = s_T.v
            return f

        def bk_wv_x2(b, ms):
            t0b = b * TB
            def f():
                for m in ms:
                    v.wait_ge(s_T.h, wvm_g[(b, m)])
                    v.wait_ge(q_os[m & 1].h, out_cnt[m & 1])
                    x2b = x2t[:, 0:TB] if (m & 1) == 0 else ytmp[:, 0:TB]
                    vi(v.tensor_tensor(ffntmp[:], PS[m & 1][:, :],
                                       rt[:, m * TB:(m + 1) * TB], AluOpType.mult))
                    vi(v.tensor_tensor(x2b, ffntmp[:], x1t[:, m * TB:(m + 1) * TB],
                                       AluOpType.add))
                    wv_g[(b, m)] = s_V.v
                    sy.wait_ge(s_V.h, s_V.v)
                    dma(sy, q_os[m & 1], out=out[m * 128:(m + 1) * 128, t0b:t0b + TB],
                        in_=x2b)
                    out_cnt[m & 1] = q_os[m & 1].v
                if ms[-1] == NCH_M - 1:
                    wv_g[b] = s_V.v
            return f

        UB = [ube, ubo]
        HB = [hbe, hbo]

        def scan_section(k, c):
            c_abs = k * NCH_I + c
            dt_ch = dtt[:, c * TB:(c + 1) * TB]
            v.wait_ge(s_S.h, dt_g[(k, c)])
            v.wait_ge(s_S.h, silu_g[(k, c)])
            if c == 0:
                v.wait_ge(q_os[1].h, out_cnt[1])
                v.wait_ge(s_T.h, wvmm_g.get(k - 2, 0))
            vi(v.tensor_tensor(vch[:], dt_ch, xs_c(c), AluOpType.mult))
            vi(v.tensor_scalar(dxsb[:], xs_c(c), dc[:, c:c + 1], None, AluOpType.mult))
            vchfree[(k, c)] = s_V.v
            if c == 0:
                v.wait_ge(q_b.h, brep_q[k])
            vi(v.tensor_tensor(UB[0][:], vch[:], brep[:, 0:TB], AluOpType.mult))

            def tail(j):
                kj = c_abs * D_STATE + j
                v.wait_ge(s_T.h, mm_g.get(kj - 3, 0))
                chb = chb4[j % 3]
                vi(v.tensor_tensor(chb[:], HB[j & 1][:], crep[:, j * TB:(j + 1) * TB],
                                   AluOpType.mult))
                ch_g[kj] = s_V.v
                stx = stce if (j & 1) == 0 else stco
                stcol = stx[:, c * 8 + (j >> 1): c * 8 + (j >> 1) + 1]
                vi(v.tensor_scalar(stcol, HB[j & 1][:, TB - 1:TB], 1.0, None,
                                   AluOpType.mult))
                if j == 0:
                    t.wait_ge(s_V.h, ytmp_g.get((k, c - 1) if c else (k - 1, NCH_I - 1), 0))
                t.wait_ge(s_V.h, ch_g[kj])
                ti(t.matmul(ps_y0[:, :], eyeb[:], chb[:],
                            start=(j == 0), stop=(j == D_STATE - 1)))
                mm_g[kj] = s_T.v

            for si_ in range(D_STATE):
                kj = c_abs * D_STATE + si_
                ab = abuf[:, (si_ & 1) * TB:((si_ & 1) + 1) * TB]
                s.wait_ge(s_V.h, scan_g.get(kj - 2, 0))
                si(s.activation(out=ab, in_=dt_ch, func=AF.Exp,
                                scale=abc[:, si_:si_ + 1]))
                ab_cnt = s_S.v
                if si_ + 1 < D_STATE:
                    vi(v.tensor_tensor(UB[(si_ + 1) & 1][:], vch[:],
                                       brep[:, (si_ + 1) * TB:(si_ + 2) * TB],
                                       AluOpType.mult))
                stx = stce if (si_ & 1) == 0 else stco
                stcol = stx[:, c * 8 + (si_ >> 1): c * 8 + (si_ >> 1) + 1]
                v.wait_ge(s_S.h, ab_cnt)
                scan_g[kj] = s_V.v + 1
                vi(v.tensor_tensor_scan(HB[si_ & 1][:], ab, UB[si_ & 1][:], stcol,
                                        AluOpType.mult, AluOpType.add))
                if si_ > 0:
                    tail(si_ - 1)
            tail(D_STATE - 1)
            v.wait_ge(s_T.h, mm_g[c_abs * D_STATE + D_STATE - 1])
            v.wait_ge(s_S.h, zsilu_g[(k, c)])
            vi(v.tensor_tensor(ytmp[:], ps_y0[:, :], dxsb[:], AluOpType.add))
            ytmp_g[(k, c)] = s_V.v
            vi(v.tensor_tensor(gat2[k & 1][:, c * TB:(c + 1) * TB], ytmp[:],
                               szt[:, c * TB:(c + 1) * TB], AluOpType.mult))
            gate_g[(k, c)] = s_V.v
            if c == NCH_I - 1:
                gatdone_g[k] = s_V.v
            if debug:
                g.wait_ge(s_V.h, s_V.v)
                t0k = k * TB
                dma(g, q_g, out=dbg["d_xs"][c * 128:(c + 1) * 128, t0k:t0k + TB],
                    in_=xs_c(c))
                dma(g, q_g, out=dbg["d_dt"][c * 128:(c + 1) * 128, t0k:t0k + TB],
                    in_=dt_ch)
                dma(g, q_g, out=dbg["d_gy"][c * 128:(c + 1) * 128, t0k:t0k + TB],
                    in_=gat2[k & 1][:, c * TB:(c + 1) * TB])

        def front_agenda(j, k_cur, ag, tailp):
            ag[4].append(fr_load_xt(j))
            ag[5].append(fr_ln1_sq(j))
            ag[6].append(fr_ln1_fin(j))
            ag[7].append(fr_ln1_apply(j))
            ag[8].append(fr_win_xr_mm(j, [0, 1], k_cur))
            ag[9].append(fr_win_xr_ev(j, [0, 1], k_cur))
            ag[9].append(fr_win_xr_mm(j, [2, 3], k_cur))
            ag[10].append(fr_win_xr_ev(j, [2, 3], k_cur))
            ag[10].append(fr_win_xr_mm(j, [4, 5], k_cur))
            ag[11].append(fr_win_xr_ev(j, [4, 5], k_cur))
            ag[11].append(fr_win_xr_mm(j, [6, 7], k_cur))
            ag[12].append(fr_win_xr_ev(j, [6, 7], k_cur))
            ag[12].append(fr_win_xr_mm(j, [8, 9], k_cur))
            ag[13].append(fr_win_xr_ev(j, [8, 9], k_cur))
            ag[13].append(fr_win_xr_mm(j, [10, 11], k_cur))
            ag[14].append(fr_win_xr_ev(j, [10, 11], k_cur))
            ag[14].append(fr_win_xr_mm(j, [12, 13], k_cur))
            ag[15].append(fr_win_xr_ev(j, [12, 13], k_cur))
            ag[15].append(fr_win_xr_mm(j, [14], k_cur))
            ag[10].append(fr_conv(j, [0, 1, 2, 3]))
            ag[10].append(fr_win_z(j, [0, 1, 2, 3], k_cur))
            ag[12].append(fr_win_z(j, [4, 5, 6, 7], k_cur))
            ag[13].append(fr_conv(j, [4, 5, 6, 7, 8]))
            ag[15].append(fr_conv(j, [9, 10, 11, 12, 13]))
            ag[15].append(fr_win_z(j, [8, 9, 10, 11, 12, 13, 14], k_cur))
            tailp.append(fr_win_xr_ev(j, [14], k_cur))
            tailp.append(fr_win_xr_mm(j, [15], k_cur))
            tailp.append(fr_win_xr_ev(j, [15], k_cur))
            tailp.append(fr_conv(j, [14, 15]))
            tailp.append(fr_win_z(j, [15], k_cur))
            tailp.append(fr_wx_wdt(j))

        fr_load_xt(0)()
        fr_ln1_sq(0)()
        fr_ln1_fin(0)()
        fr_ln1_apply(0)()
        for m0 in range(0, NCH_I, 2):
            fr_win_xr_mm(0, [m0, m0 + 1], None)()
            fr_win_xr_ev(0, [m0, m0 + 1], None)()
        for c0 in range(0, NCH_I, 2):
            fr_conv(0, [c0, c0 + 1])()
        for j0 in range(0, NCH_I, 2):
            fr_win_z(0, [j0, j0 + 1], None)()
        fr_wx_wdt(0)()

        for k in range(NB):
            ag = {cc: [] for cc in range(NCH_I)}
            tailp = []
            b = k - 1
            if b >= 0:
                ag[0].append(bk_wout_mm(b, [0, 1]))
                ag[1].append(bk_wout_x1(b, [0, 1]))
                ag[1].append(bk_wout_mm(b, [2, 3]))
                ag[2].append(bk_wout_x1(b, [2, 3]))
                ag[2].append(bk_wout_mm(b, [4, 5]))
                ag[3].append(bk_wout_x1(b, [4, 5]))
                ag[3].append(bk_wout_mm(b, [6, 7]))
                ag[4].append(bk_wout_x1(b, [6, 7]))
                ag[4].append(bk_ln2_sq(b))
                ag[5].append(bk_ln2_fin(b))
                ag[6].append(bk_ln2_apply(b))
                ag[7].append(bk_ffn_mix(b, [0, 1, 2, 3, 4, 5, 6, 7]))
                ag[8].append(bk_wrk(b, "r", [0, 1, 2, 3]))
                ag[10].append(bk_wrk(b, "r", [4, 5, 6, 7]))
                ag[12].append(bk_wrk(b, "k", [0, 1, 2, 3]))
                ag[13].append(bk_wrk(b, "k", [4, 5, 6, 7]))
                ag[14].append(bk_wv_mm(b, [0, 1]))
                ag[15].append(bk_wv_x2(b, [0, 1]))
                ag[15].append(bk_wv_mm(b, [2, 3]))
                wvp = [bk_wv_x2(b, [2, 3]), bk_wv_mm(b, [4, 5]),
                       bk_wv_x2(b, [4, 5]), bk_wv_mm(b, [6, 7]),
                       bk_wv_x2(b, [6, 7])]
            else:
                wvp = []
            j = k + 1
            if j < NB:
                front_agenda(j, k, ag, tailp)
            if tailp and tailp[-1].__name__ == "f":
                pass
            wx = [fn for fn in tailp if fn.__qualname__.startswith("build_program.<locals>.fr_wx_wdt")]
            pre = [fn for fn in tailp if fn not in wx]
            tailp = pre + wvp + wx

            for c in range(NCH_I):
                scan_section(k, c)
                for fn in ag[c]:
                    fn()
            for fn in tailp:
                fn()

        b = NB - 1
        for m0 in range(0, NCH_M, 2):
            bk_wout_mm(b, [m0, m0 + 1])()
            bk_wout_x1(b, [m0, m0 + 1])()
        bk_ln2_sq(b)()
        bk_ln2_fin(b)()
        bk_ln2_apply(b)()
        bk_ffn_mix(b, list(range(NCH_M)))()
        for m0 in range(0, NCH_M, 2):
            bk_wrk(b, "r", [m0, m0 + 1])()
        for m0 in range(0, NCH_M, 2):
            bk_wrk(b, "k", [m0, m0 + 1])()
        for m0 in range(0, NCH_M, 2):
            bk_wv_mm(b, [m0, m0 + 1])()
            bk_wv_x2(b, [m0, m0 + 1])()

        g.wait_ge(q_o.h, q_o.v)
        g.wait_ge(q_o2.h, q_o2.v)
        if debug:
            g.wait_ge(q_g.h, q_g.v)

        @block.sync
        def _(e):
            sy.replay(e)

        @block.gpsimd
        def _(e):
            g.replay(e)

        @block.tensor
        def _(e):
            t.replay(e)

        @block.vector
        def _(e):
            v.replay(e)

        @block.scalar
        def _(e):
            s.replay(e)

    return nc


_PROG_CACHE = {}


def _get_prog(T, TB, debug=False):
    key = (T, TB, debug)
    if key not in _PROG_CACHE:
        _PROG_CACHE[key] = build_program(T, TB, debug)
    return _PROG_CACHE[key]


def make_in_map(b, T, L0_attn_x, L0_attn_W_in, L0_attn_conv_w, L0_attn_conv_b,
                L0_attn_A_log, L0_attn_D, L0_attn_W_x, L0_attn_W_dt, L0_attn_b_dt,
                L0_attn_W_out, L0_ffn_W_r, L0_ffn_W_k, L0_ffn_W_v, L0_ffn_mu_r,
                L0_ffn_mu_k):
    C = np.ascontiguousarray
    f = np.float32
    _bf16 = ml_dtypes.bfloat16
    return {
        "xT": C(np.asarray(L0_attn_x)[b, :T, :].T.astype(f)),
        "W_inT": C(np.asarray(L0_attn_W_in).T.astype(f)),
        "conv_wc": C(np.asarray(L0_attn_conv_w).reshape(NCH_I, 128, 4)
                     .transpose(1, 0, 2).reshape(128, NCH_I * 4).astype(f)),
        "conv_bc": C(np.asarray(L0_attn_conv_b).reshape(NCH_I, 128).T.astype(f)),
        "A_log_r": C(np.asarray(L0_attn_A_log).reshape(1, D_STATE).astype(f)),
        "D_c": C(np.asarray(L0_attn_D).reshape(NCH_I, 128).T.astype(f)),
        "W_xT": C(np.asarray(L0_attn_W_x).T.astype(f)),
        "W_dtT": C(np.asarray(L0_attn_W_dt).T.astype(f)),
        "b_dtc": C(np.asarray(L0_attn_b_dt).reshape(NCH_I, 128).T.astype(f)),
        "W_outT": C(np.asarray(L0_attn_W_out).T.astype(f)),
        "W_rT": C(np.asarray(L0_ffn_W_r).T.astype(f)),
        "W_kT": C(np.asarray(L0_ffn_W_k).T.astype(f)),
        "W_vT": C(np.asarray(L0_ffn_W_v).T.astype(f)),
        "mu_rc": C(np.asarray(L0_ffn_mu_r).reshape(NCH_M, 128).T.astype(f)),
        "mu_kc": C(np.asarray(L0_ffn_mu_k).reshape(NCH_M, 128).T.astype(f)),
        "eye_in": np.eye(128, dtype=_bf16),
    }


def kernel(**inputs):
    T = 8192
    nc = _get_prog(T, 512)
    in_maps = [make_in_map(c >> 1, T, **inputs) for c in range(8)]
    res = run_bass_kernel_spmd(nc, in_maps, core_ids=list(range(8)))
    B = 4
    outs = []
    for b in range(B):
        o = res.results[2 * b]["out"]
        outs.append(np.ascontiguousarray(o.T)[None])
    return np.concatenate(outs, axis=0)


# version 08
# speedup vs baseline: 1.2458x; 1.0004x over previous
import numpy as np
import ml_dtypes
import concourse.bass as bass
import concourse.mybir as mybir
from concourse.bass_utils import run_bass_kernel_spmd
from concourse.alu_op_type import AluOpType

F32 = mybir.dt.float32
BF16 = mybir.dt.bfloat16
AF = mybir.ActivationFunctionType

D_MODEL = 1024
D_INNER = 2048
D_STATE = 16
DT_RANK = 64
D_CONV = 4
NCH_M = D_MODEL // 128
NCH_I = D_INNER // 128
EPS = 1e-5


class Sem:
    __slots__ = ("h", "v")
    def __init__(self, h):
        self.h = h
        self.v = 0


def build_program(T, TB, debug=False):
    NB = T // TB
    assert TB == 512
    nc = bass.Bass("TRN2", target_bir_lowering=False)

    dr = lambda n, sh, dt=F32: nc.dram_tensor(n, sh, dt, kind="ExternalInput").ap()
    xT = dr("xT", [D_MODEL, T])
    W_inT = dr("W_inT", [D_MODEL, 2 * D_INNER])
    conv_w = dr("conv_wc", [128, NCH_I * D_CONV])
    conv_b = dr("conv_bc", [128, NCH_I])
    A_log = dr("A_log_r", [1, D_STATE])
    D_in = dr("D_c", [128, NCH_I])
    W_xT = dr("W_xT", [D_INNER, 96])
    W_dtT = dr("W_dtT", [DT_RANK, D_INNER])
    b_dt = dr("b_dtc", [128, NCH_I])
    W_outT = dr("W_outT", [D_INNER, D_MODEL])
    W_rT = dr("W_rT", [D_MODEL, D_MODEL])
    W_kT = dr("W_kT", [D_MODEL, D_MODEL])
    W_vT = dr("W_vT", [D_MODEL, D_MODEL])
    mu_r = dr("mu_rc", [128, NCH_M])
    mu_k = dr("mu_kc", [128, NCH_M])
    eye_in = dr("eye_in", [128, 128], BF16)

    out = nc.dram_tensor("out", [D_MODEL, T], F32, kind="ExternalOutput").ap()
    dbg = {}
    if debug:
        for nm, rows_ in (("d_ln", D_MODEL), ("d_xs", D_INNER), ("d_dt", D_INNER),
                          ("d_gy", D_INNER), ("d_x1", D_MODEL), ("d_ln2", D_MODEL)):
            dbg[nm] = nc.dram_tensor(nm, [rows_, T], F32, kind="ExternalOutput").ap()

    scr = lambda n, sh, dt=F32: nc.dram_tensor(n, sh, dt, kind="Internal").ap()
    W_inT_bf = scr("W_inT_bf", [D_MODEL, 2 * D_INNER], BF16)
    W_outT_bf = scr("W_outT_bf", [D_INNER, D_MODEL], BF16)
    W_rT_bf = scr("W_rT_bf", [D_MODEL, D_MODEL], BF16)
    W_kT_bf = scr("W_kT_bf", [D_MODEL, D_MODEL], BF16)
    W_vT_bf = scr("W_vT_bf", [D_MODEL, D_MODEL], BF16)
    W_dtT_bf = scr("W_dtT_bf", [DT_RANK, D_INNER], BF16)
    W_xT_bf = scr("W_xT_bf", [D_INNER, 96], BF16)
    bounce = [scr(f"bounce{p}", [4, TB]) for p in range(2)]
    bbc = [scr(f"bbc{p}", [32, TB], BF16) for p in range(2)]
    a_bounce = scr("a_bounce", [1, D_STATE])

    from contextlib import ExitStack
    _st = ExitStack()
    with _st:
        xt = _st.enter_context(nc.sbuf_tensor("xt", [128, NCH_M * TB], F32))
        xre = xt.ap()[:, 0:2 * TB]
        xsqa = _st.enter_context(nc.sbuf_tensor("xsqa", [128, TB], F32))
        xsqb = _st.enter_context(nc.sbuf_tensor("xsqb", [128, TB], F32))
        lnt = _st.enter_context(nc.sbuf_tensor("lnt", [128, NCH_M * TB], BF16))
        ln2t = _st.enter_context(nc.sbuf_tensor("ln2t", [128, NCH_M * TB], BF16))
        rows = _st.enter_context(nc.sbuf_tensor("rows", [1, 2 * TB], F32))
        mrep = _st.enter_context(nc.sbuf_tensor("mrep", [128, TB], BF16))
        rrep = _st.enter_context(nc.sbuf_tensor("rrep", [128, TB], BF16))
        xrt = _st.enter_context(nc.sbuf_tensor("xrt", [128, NCH_I * (TB + 3)], BF16))
        halo = _st.enter_context(nc.sbuf_tensor("halo", [128, NCH_I * 3], BF16))
        convacc = _st.enter_context(nc.sbuf_tensor("convacc", [128, 2 * TB], F32))
        szt = _st.enter_context(nc.sbuf_tensor("szt", [128, NCH_I * TB], BF16))
        dtt = _st.enter_context(nc.sbuf_tensor("dtt", [128, NCH_I * TB], BF16))
        xdbl = _st.enter_context(nc.sbuf_tensor("xdbl", [96, TB], BF16))
        dtr = None
        brep = _st.enter_context(nc.sbuf_tensor("brep", [128, D_STATE * TB], BF16))
        crep = _st.enter_context(nc.sbuf_tensor("crep", [128, D_STATE * TB], BF16))
        vch = _st.enter_context(nc.sbuf_tensor("vch", [128, TB], BF16))
        dxsb = _st.enter_context(nc.sbuf_tensor("dxsb", [128, TB], BF16))
        abuf = _st.enter_context(nc.sbuf_tensor("abuf", [128, 3 * TB], F32))
        ube = _st.enter_context(nc.sbuf_tensor("ube", [128, TB], BF16))
        ubo = _st.enter_context(nc.sbuf_tensor("ubo", [128, TB], BF16))
        hbe = _st.enter_context(nc.sbuf_tensor("hbe", [128, TB], BF16))
        hbo = _st.enter_context(nc.sbuf_tensor("hbo", [128, TB], BF16))
        chb4 = [_st.enter_context(nc.sbuf_tensor(f"chb{i}", [128, TB], BF16))
                for i in range(3)]
        ytmp = _st.enter_context(nc.sbuf_tensor("ytmp", [128, TB], F32))
        stce = _st.enter_context(nc.sbuf_tensor("stce", [128, NCH_I * 8], F32))
        stco = _st.enter_context(nc.sbuf_tensor("stco", [128, NCH_I * 8], F32))
        gat2 = [_st.enter_context(nc.sbuf_tensor(f"gat{i}", [128, NCH_I * TB], BF16))
                for i in range(2)]
        x1t = _st.enter_context(nc.sbuf_tensor("x1t", [128, NCH_M * TB], F32))
        ln2p = _st.enter_context(nc.sbuf_tensor("ln2p", [128, TB], BF16))
        carry = _st.enter_context(nc.sbuf_tensor("carry", [128, NCH_M], BF16))
        ffntmp = _st.enter_context(nc.sbuf_tensor("ffntmp", [128, TB], BF16))
        x2t = _st.enter_context(nc.sbuf_tensor("x2t", [128, TB], F32))
        wslab = _st.enter_context(nc.sbuf_tensor("wslab", [128, 2 * 16 * 128], BF16))
        cwc = _st.enter_context(nc.sbuf_tensor("cwc", [128, NCH_I * D_CONV], F32))
        cbc = _st.enter_context(nc.sbuf_tensor("cbc", [128, NCH_I], F32))
        bdtc = _st.enter_context(nc.sbuf_tensor("bdtc", [128, NCH_I], F32))
        dc = _st.enter_context(nc.sbuf_tensor("dc", [128, NCH_I], F32))
        murc = _st.enter_context(nc.sbuf_tensor("murc", [128, NCH_M], F32))
        mukc = _st.enter_context(nc.sbuf_tensor("mukc", [128, NCH_M], F32))
        omrc = _st.enter_context(nc.sbuf_tensor("omrc", [128, NCH_M], F32))
        omkc = _st.enter_context(nc.sbuf_tensor("omkc", [128, NCH_M], F32))
        arow = _st.enter_context(nc.sbuf_tensor("arow", [1, D_STATE], F32))
        abc = _st.enter_context(nc.sbuf_tensor("abc", [128, D_STATE], F32))
        eyeb = _st.enter_context(nc.sbuf_tensor("eyeb", [128, 128], BF16))
        onesf = _st.enter_context(nc.sbuf_tensor("onesf", [128, 1], F32))
        rt = ln2t
        def kat_of(b):
            return gat2[b & 1].ap()[:, 0:NCH_M * TB]
        ps_a = _st.enter_context(nc.psum_tensor("ps_a", [128, 512], F32))
        ps_b = _st.enter_context(nc.psum_tensor("ps_b", [128, 512], F32))
        ps_c = _st.enter_context(nc.psum_tensor("ps_c", [128, 512], F32))
        ps_d = _st.enter_context(nc.psum_tensor("ps_d", [128, 512], F32))
        ps_y0 = _st.enter_context(nc.psum_tensor("ps_y0", [128, 512], F32))
        ps_y1 = _st.enter_context(nc.psum_tensor("ps_y1", [128, 512], F32))
        ps_s1 = _st.enter_context(nc.psum_tensor("ps_s1", [1, 512], F32))
        ps_s2 = _st.enter_context(nc.psum_tensor("ps_s2", [1, 512], F32))
        _q_w = _st.enter_context(nc.semaphore("q_w"))
        _q_w2 = _st.enter_context(nc.semaphore("q_w2"))
        _q_w3 = _st.enter_context(nc.semaphore("q_w3"))
        _q_x = _st.enter_context(nc.semaphore("q_x"))
        _q_g = _st.enter_context(nc.semaphore("q_g"))
        _q_o = _st.enter_context(nc.semaphore("q_o"))
        _q_o2 = _st.enter_context(nc.semaphore("q_o2"))
        _s_T = _st.enter_context(nc.semaphore("s_T"))
        _s_V = _st.enter_context(nc.semaphore("s_V"))
        _s_S = _st.enter_context(nc.semaphore("s_S"))
        _s_G = _st.enter_context(nc.semaphore("s_G"))
        block = _st.enter_context(nc.Block())
        q_w, q_x, q_g, q_o = Sem(_q_w), Sem(_q_x), Sem(_q_g), Sem(_q_o)
        q_w2, q_w3, q_o2 = Sem(_q_w2), Sem(_q_w3), Sem(_q_o2)
        q_ws = [q_w, q_w2]
        q_b = q_w3
        q_os = [q_o, q_o2]
        s_T, s_V, s_S, s_G = Sem(_s_T), Sem(_s_V), Sem(_s_S), Sem(_s_G)

        class _Pending:
            __slots__ = ("incs",)
            def __init__(self):
                self.incs = []
            def then_inc(self, sem, n):
                self.incs.append((sem, n))
                return self

        def _nm(x):
            try:
                return x.tensor.name
            except AttributeError:
                return None

        _IOMAP = {
            "tensor_tensor": lambda a, kw: ((a[1], a[2]), (a[0],)),
            "tensor_scalar": lambda a, kw: ((a[1], a[2], a[3]), (a[0],)),
            "scalar_tensor_tensor": lambda a, kw: ((a[1], a[2], a[3]), (a[0],)),
            "tensor_tensor_scan": lambda a, kw: ((a[1], a[2], a[3]), (a[0],)),
            "tensor_copy": lambda a, kw: ((a[1],), (a[0],)),
            "memset": lambda a, kw: ((), (a[0],)),
            "activation": lambda a, kw: (
                (kw.get("in_"), kw.get("bias"), kw.get("scale")),
                (kw.get("out"),)),
            "matmul": lambda a, kw: ((a[1], a[2]), ()),
        }

        class _Rec:
            def __init__(self, sem=None):
                self.ops = []
                self.sem = sem
                self.waited = 0
                self.lw = {}
                self.lr = {}

            def _auto(self, meth, a, kw):
                if self.sem is None:
                    return
                if meth == "wait_ge":
                    if a and a[0] is self.sem.h and a[1] > self.waited:
                        self.waited = a[1]
                    return
                io = _IOMAP.get(meth)
                if io is None:
                    return
                rd, wr = io(a, kw)
                need = 0
                for x in rd:
                    n = _nm(x)
                    if n:
                        need = max(need, self.lw.get(n, 0))
                for x in wr:
                    n = _nm(x)
                    if n:
                        need = max(need, self.lw.get(n, 0), self.lr.get(n, 0))
                if need > self.waited:
                    self.ops.append(("wait_ge", (self.sem.h, need), {}, _Pending()))
                    self.waited = need
                cnt = self.sem.v + 1
                for x in rd:
                    n = _nm(x)
                    if n:
                        self.lr[n] = cnt
                for x in wr:
                    n = _nm(x)
                    if n:
                        self.lw[n] = cnt

            def __getattr__(self, meth):
                def call(*a, **kw):
                    self._auto(meth, a, kw)
                    p = _Pending()
                    self.ops.append((meth, a, kw, p))
                    return p
                return call

            def replay(self, eng):
                for meth, a, kw, p in self.ops:
                    instr = getattr(eng, meth)(*a, **kw)
                    for sem, n in p.incs:
                        instr.then_inc(sem, n)

        sy = _Rec()
        g = _Rec(s_G)
        t, v, s = _Rec(s_T), _Rec(s_V), _Rec(s_S)
        PS = [ps_a, ps_b, ps_c, ps_d]

        def dma(eng, q, **kw):
            eng.dma_start(**kw).then_inc(q.h, 16)
            q.v += 16

        def vi(instr):
            instr.then_inc(s_V.h, 1)
            s_V.v += 1

        def si(instr):
            instr.then_inc(s_S.h, 1)
            s_S.v += 1

        def ti(instr):
            instr.then_inc(s_T.h, 1)
            s_T.v += 1

        def gi(instr):
            instr.then_inc(s_G.h, 1)
            s_G.v += 1

        dma(sy, q_x, out=cwc[:], in_=conv_w[:])
        dma(sy, q_x, out=cbc[:], in_=conv_b[:])
        dma(sy, q_x, out=bdtc[:], in_=b_dt[:])
        dma(sy, q_x, out=dc[:], in_=D_in[:])
        dma(sy, q_x, out=murc[:], in_=mu_r[:])
        dma(sy, q_x, out=mukc[:], in_=mu_k[:])
        dma(sy, q_x, out=arow[:], in_=A_log[:])
        dma(sy, q_x, out=eyeb[:], in_=eye_in[:])
        s.wait_ge(q_x.h, q_x.v)
        si(s.activation(out=arow[:], in_=arow[:], func=AF.Exp))
        v.wait_ge(s_S.h, s_S.v)
        vi(v.tensor_scalar(arow[:], arow[:], -1.0, None, AluOpType.mult))
        g.wait_ge(s_V.h, s_V.v)
        dma(g, q_g, out=a_bounce[:], in_=arow[:])
        g.wait_ge(q_g.h, q_g.v)
        dma(g, q_g, out=abc[:], in_=a_bounce.broadcast_to((128, D_STATE)))

        v.wait_ge(q_x.h, q_x.v)
        vi(v.memset(onesf[:], 1.0))
        vi(v.memset(stce[:], 0.0))
        vi(v.memset(stco[:], 0.0))
        vi(v.memset(halo[:], 0.0))
        vi(v.memset(carry[:], 0.0))
        vi(v.tensor_scalar(omrc[:], murc[:], -1.0, 1.0, AluOpType.mult, AluOpType.add))
        vi(v.tensor_scalar(omkc[:], mukc[:], -1.0, 1.0, AluOpType.mult, AluOpType.add))

        def cast_weight(src, dst, R, C):
            for r0 in range(0, R, 128):
                sy.wait_ge(q_g.h, q_g.v)
                dma(sy, q_w, out=xt[:, :C], in_=src[r0:r0 + 128, :])
                v.wait_ge(q_w.h, q_w.v)
                vi(v.tensor_copy(gat2[0][:, :C], xt[:, :C]))
                g.wait_ge(s_V.h, s_V.v)
                dma(g, q_g, out=dst[r0:r0 + 128, :], in_=gat2[0][:, :C])

        cast_weight(W_inT, W_inT_bf, D_MODEL, 2 * D_INNER)
        cast_weight(W_outT, W_outT_bf, D_INNER, D_MODEL)
        cast_weight(W_rT, W_rT_bf, D_MODEL, D_MODEL)
        cast_weight(W_kT, W_kT_bf, D_MODEL, D_MODEL)
        cast_weight(W_vT, W_vT_bf, D_MODEL, D_MODEL)
        cast_weight(W_xT, W_xT_bf, D_INNER, 96)
        sy.wait_ge(q_g.h, q_g.v)
        dma(sy, q_w, out=xt[0:64, :D_INNER], in_=W_dtT[:, :])
        v.wait_ge(q_w.h, q_w.v)
        vi(v.tensor_copy(gat2[0][0:64, :D_INNER], xt[0:64, :D_INNER]))
        g.wait_ge(s_V.h, s_V.v)
        dma(g, q_g, out=W_dtT_bf[:, :], in_=gat2[0][0:64, :D_INNER])
        g.wait_ge(q_g.h, q_g.v)

        SLABW = 16 * 128
        slab_guard = [0, 0]
        slab_cnt = [0, 0]
        slab_rot = [0]

        def slab_load(src_bf, R, m0):
            sl = slab_rot[0]
            slab_rot[0] = (sl + 1) % 2
            nk = R // 128
            sy.wait_ge(s_T.h, slab_guard[sl])
            dma(sy, q_ws[sl],
                out=wslab[:, sl * SLABW: sl * SLABW + nk * 128]
                .rearrange("p (k c) -> p k c", c=128),
                in_=src_bf[:, m0 * 128:(m0 + 1) * 128]
                .rearrange("(k p) c -> p k c", p=128))
            slab_cnt[sl] = q_ws[sl].v
            return sl, q_ws[sl].v

        def slab_seq(loader, ms, body):
            nxt = loader(ms[0])
            for i, m in enumerate(ms):
                sl, cnt = nxt
                if i + 1 < len(ms):
                    nxt = loader(ms[i + 1])
                t.wait_ge(q_ws[sl].h, cnt)
                body(sl, m)

        def slab_load_dt():
            sl = slab_rot[0]
            slab_rot[0] = (sl + 1) % 2
            sy.wait_ge(s_T.h, slab_guard[sl])
            dma(sy, q_ws[sl], out=wslab[0:64, sl * SLABW: sl * SLABW + D_INNER],
                in_=W_dtT_bf[:, :])
            slab_cnt[sl] = q_ws[sl].v
            return sl, q_ws[sl].v

        def xs_c(c):
            return xrt[:, c * (TB + 3) + 3:(c + 1) * (TB + 3)]

        scan_g = {}
        ch_g = {}
        mm_g = {}
        ytmp_g = {}
        gate_g = {}
        vchfree = {}
        silu_g = {}
        zsilu_g = {}
        dt_g = {}
        lnap_g = {}
        x1_g = {}
        mix_g = {}
        dtr_g = {}
        brep_q = {}
        gatdone_g = {}
        rk_g = {}
        wv_g = {}
        wvmm_g = {}
        wvm_g = {}
        winmm_g = {}
        woutmm_g = {}
        xre_q = {}
        brep_si = {}
        winev_g = {}
        out_cnt = {0: 0, 1: 0}
        ln_ctr = [0]
        conv_prev = [0]

        def ln_stats_sq(src_t):
            t.wait_ge(s_V.h, s_V.v)
            for c in range(NCH_M):
                sl = src_t[:, c * TB:(c + 1) * TB]
                xsq = xsqa if (c & 1) == 0 else xsqb
                s.wait_ge(s_T.h, s_T.v)
                si(s.activation(out=xsq[:], in_=sl, func=AF.Square))
                t.wait_ge(s_S.h, s_S.v)
                ti(t.matmul(ps_s1[:, :], onesf[:], sl, start=(c == 0), stop=(c == NCH_M - 1)))
                ti(t.matmul(ps_s2[:, :], onesf[:], xsq[:], start=(c == 0), stop=(c == NCH_M - 1)))

        def ln_stats_fin():
            par = ln_ctr[0] & 1
            ln_ctr[0] += 1
            v.wait_ge(s_T.h, s_T.v)
            r_mean = rows[0:1, 0:TB]
            r_var = rows[0:1, TB:2 * TB]
            msq = ps_s1[0:1, :]
            vi(v.tensor_scalar(r_mean, ps_s1[0:1, :], 1.0 / D_MODEL, None, AluOpType.mult))
            vi(v.tensor_scalar(r_var, ps_s2[0:1, :], 1.0 / D_MODEL, None, AluOpType.mult))
            vi(v.tensor_tensor(msq, r_mean, r_mean, AluOpType.mult))
            vi(v.tensor_tensor(r_var, r_var, msq, AluOpType.subtract))
            vi(v.tensor_scalar(r_var, r_var, EPS, None, AluOpType.add))
            s.wait_ge(s_V.h, s_V.v)
            si(s.activation(out=r_var, in_=r_var, func=AF.Ln))
            si(s.activation(out=r_var, in_=r_var, func=AF.Exp, scale=-0.5))
            g.wait_ge(s_V.h, s_V.v)
            g.wait_ge(s_S.h, s_S.v)
            dma(g, q_g, out=bounce[par][0:1, :], in_=r_mean)
            dma(g, q_g, out=bounce[par][1:2, :], in_=r_var)
            g.wait_ge(q_g.h, q_g.v)
            dma(g, q_g, out=mrep[:], in_=bounce[par][0:1, :].broadcast_to((128, TB)))
            dma(g, q_g, out=rrep[:], in_=bounce[par][1:2, :].broadcast_to((128, TB)))

        def ln_apply(src_t, dst_t):
            v.wait_ge(q_g.h, q_g.v)
            for c in range(NCH_M):
                sl = src_t[:, c * TB:(c + 1) * TB]
                dl = dst_t[:, c * TB:(c + 1) * TB]
                acc = convacc[:, (c & 1) * TB:((c & 1) + 1) * TB]
                vi(v.tensor_tensor(acc, sl, mrep[:], AluOpType.subtract))
                vi(v.tensor_tensor(dl, acc, rrep[:], AluOpType.mult))

        def fr_load_xt(j):
            def f():
                sy.wait_ge(s_V.h, s_V.v)
                sy.wait_ge(s_S.h, s_S.v)
                sy.wait_ge(s_T.h, s_T.v)
                t0j = j * TB
                for c in range(NCH_M):
                    dma(sy, q_x, out=xt[:, c * TB:(c + 1) * TB],
                        in_=xT[c * 128:(c + 1) * 128, t0j:t0j + TB])
            return f

        def fr_ln1_sq(j):
            def f():
                s.wait_ge(q_x.h, q_x.v)
                t.wait_ge(q_x.h, q_x.v)
                ln_stats_sq(xt)
            return f

        def fr_ln1_fin(j):
            def f():
                ln_stats_fin()
            return f

        def fr_ln1_apply(j):
            def f():
                if debug:
                    v.wait_ge(q_g.h, q_g.v)
                ln_apply(xt, lnt)
                lnap_g[("1", j)] = s_V.v
                if debug:
                    g.wait_ge(s_V.h, s_V.v)
                    for c in range(NCH_M):
                        dma(g, q_g, out=dbg["d_ln"][c * 128:(c + 1) * 128,
                                                    j * TB:(j + 1) * TB],
                            in_=lnt[:, c * TB:(c + 1) * TB])
            return f

        def fr_win_xr_mm(j, ms, k_cur):
            def f():
                t.wait_ge(s_V.h, lnap_g[("1", j)])
                t.wait_ge(s_S.h, dt_g.get((j - 1, NCH_I - 1), 0))
                def body(sl, m):
                    ps = PS[2 + (m & 1)]
                    t.wait_ge(s_V.h, winev_g.get((j, m - 2), 0))
                    if k_cur is not None:
                        t.wait_ge(s_V.h, vchfree.get((k_cur, m), 0))
                    for kk in range(NCH_M):
                        ti(t.matmul(ps[:, :],
                                    wslab[:, sl * SLABW + kk * 128: sl * SLABW + (kk + 1) * 128],
                                    lnt[:, kk * TB:(kk + 1) * TB],
                                    start=(kk == 0), stop=(kk == NCH_M - 1)))
                    slab_guard[sl] = s_T.v
                    winmm_g[(j, m)] = s_T.v
                slab_seq(lambda m: slab_load(W_inT_bf, D_MODEL, m), ms, body)
            return f

        def fr_win_xr_ev(j, ms, k_cur):
            def f():
                if debug:
                    v.wait_ge(q_g.h, q_g.v)
                for m in ms:
                    ps = PS[2 + (m & 1)]
                    v.wait_ge(s_T.h, winmm_g[(j, m)])
                    vi(v.tensor_copy(xrt[:, m * (TB + 3) + 3:(m + 1) * (TB + 3)], ps[:, :]))
                    winev_g[(j, m)] = s_V.v
            return f

        def fr_win_z(j, js, k_cur):
            def f():
                t.wait_ge(s_V.h, lnap_g[("1", j)])
                def body(sl, jz):
                    t.wait_ge(s_S.h, zsilu_g.get((j, jz - 1),
                                                 zsilu_g.get((j - 1, NCH_I - 1), 0)))
                    if k_cur is not None:
                        t.wait_ge(s_V.h, gate_g.get((k_cur, jz), 0))
                    for kk in range(NCH_M):
                        ti(t.matmul(ps_y1[:, :],
                                    wslab[:, sl * SLABW + kk * 128: sl * SLABW + (kk + 1) * 128],
                                    lnt[:, kk * TB:(kk + 1) * TB],
                                    start=(kk == 0), stop=(kk == NCH_M - 1)))
                    slab_guard[sl] = s_T.v
                    s.wait_ge(s_T.h, s_T.v)
                    si(s.activation(out=szt[:, jz * TB:(jz + 1) * TB], in_=ps_y1[:, :],
                                    func=AF.Silu))
                    zsilu_g[(j, jz)] = s_S.v
                slab_seq(lambda jz: slab_load(W_inT_bf, D_MODEL, NCH_I + jz), js, body)
            return f

        def fr_conv(j, cs):
            def f():
                for c in cs:
                    xr_ch = xrt[:, c * (TB + 3):(c + 1) * (TB + 3)]
                    acc0 = convacc[:, 0:TB]
                    acc1 = convacc[:, TB:2 * TB]
                    v.wait_ge(s_S.h, conv_prev[0])
                    vi(v.tensor_copy(xr_ch[:, 0:3], halo[:, c * 3:(c + 1) * 3]))
                    vi(v.tensor_scalar(acc0, xr_ch[:, 0:TB], cwc[:, c * 4:c * 4 + 1],
                                       None, AluOpType.mult))
                    vi(v.scalar_tensor_tensor(acc1, xr_ch[:, 1:1 + TB],
                                              cwc[:, c * 4 + 1:c * 4 + 2], acc0,
                                              AluOpType.mult, AluOpType.add))
                    vi(v.scalar_tensor_tensor(acc0, xr_ch[:, 2:2 + TB],
                                              cwc[:, c * 4 + 2:c * 4 + 3], acc1,
                                              AluOpType.mult, AluOpType.add))
                    vi(v.scalar_tensor_tensor(acc1, xr_ch[:, 3:3 + TB],
                                              cwc[:, c * 4 + 3:c * 4 + 4], acc0,
                                              AluOpType.mult, AluOpType.add))
                    vi(v.tensor_copy(halo[:, c * 3:(c + 1) * 3], xr_ch[:, TB:TB + 3]))
                    s.wait_ge(s_V.h, s_V.v)
                    si(s.activation(out=xr_ch[:, 3:3 + TB], in_=acc1, func=AF.Silu,
                                    bias=cbc[:, c:c + 1]))
                    silu_g[(j, c)] = s_S.v
                    conv_prev[0] = s_S.v
            return f

        def fr_wx_wdt(j):
            def f():
                slx = slab_rot[0]
                slab_rot[0] = (slx + 1) % 2
                sy.wait_ge(s_T.h, slab_guard[slx])
                dma(sy, q_ws[slx],
                    out=wslab[:, slx * SLABW: slx * SLABW + NCH_I * 96]
                    .rearrange("p (k c) -> p k c", c=96),
                    in_=W_xT_bf.rearrange("(k p) c -> p k c", p=128))
                slab_cnt[slx] = q_ws[slx].v
                t.wait_ge(s_S.h, silu_g[(j, NCH_I - 1)])
                t.wait_ge(s_S.h, dt_g.get((j - 1, NCH_I - 1), 0))
                t.wait_ge(s_V.h, s_V.v)
                t.wait_ge(q_ws[slx].h, slab_cnt[slx])
                for kk in range(NCH_I):
                    ti(t.matmul(ps_c[0:96, :],
                                wslab[:, slx * SLABW + kk * 96: slx * SLABW + (kk + 1) * 96],
                                xs_c(kk),
                                start=(kk == 0), stop=(kk == NCH_I - 1)))
                slab_guard[slx] = s_T.v
                v.wait_ge(s_T.h, s_T.v)
                v.wait_ge(q_g.h, q_g.v)
                vi(v.tensor_copy(xdbl[:, :], ps_c[0:96, :]))
                dtr_g[j] = s_V.v
                par = j & 1
                g.wait_ge(s_V.h, s_V.v)
                g.wait_ge(q_g.h, q_g.v)
                dma(g, q_g, out=bbc[par][:, :], in_=xdbl[64:96, :])
                g.wait_ge(q_g.h, q_g.v)
                for si_ in range(D_STATE):
                    dma(g, q_b, out=brep[:, si_ * TB:(si_ + 1) * TB],
                        in_=bbc[par][si_:si_ + 1, :].broadcast_to((128, TB)))
                    dma(g, q_b, out=crep[:, si_ * TB:(si_ + 1) * TB],
                        in_=bbc[par][16 + si_:17 + si_, :].broadcast_to((128, TB)))
                brep_q[j] = q_b.v
                sl, _cnt = slab_load_dt()
                if debug:
                    s.wait_ge(q_g.h, q_g.v)
                t.wait_ge(s_V.h, s_V.v)
                t.wait_ge(q_ws[sl].h, _cnt)
                for m in range(NCH_I):
                    ps = PS[2 + (m & 1)]
                    xsq = xsqb if (m & 1) == 0 else xsqa
                    if m >= 2:
                        t.wait_ge(s_S.h, dt_g[(j, m - 2)])
                    ti(t.matmul(ps[:, :], wslab[0:64, sl * SLABW + m * 128:
                                                sl * SLABW + (m + 1) * 128],
                                xdbl[0:64, :], start=True, stop=True))
                    s.wait_ge(s_T.h, s_T.v)
                    si(s.activation(out=xsq[:], in_=ps[:, :], func=AF.Exp,
                                    bias=bdtc[:, m:m + 1]))
                    si(s.activation(out=dtt[:, m * TB:(m + 1) * TB], in_=xsq[:],
                                    func=AF.Ln, bias=1.0))
                    dt_g[(j, m)] = s_S.v
                slab_guard[sl] = s_T.v
            return f

        def bk_wout_mm(b, ms):
            def f():
                t0b = b * TB
                sy.wait_ge(s_V.h, s_V.v)
                for m in ms:
                    dma(sy, q_x, out=xre[:, (m & 1) * TB:((m & 1) + 1) * TB],
                        in_=xT[m * 128:(m + 1) * 128, t0b:t0b + TB])
                xre_q[ms[0]] = q_x.v
                t.wait_ge(s_V.h, gatdone_g[b])
                t.wait_ge(s_V.h, wv_g.get(b - 1, 0))
                def body(sl, m):
                    ps = PS[m & 1]
                    t.wait_ge(s_V.h, x1_g.get((b, m - 2), 0))
                    for kk in range(NCH_I):
                        ti(t.matmul(ps[:, :],
                                    wslab[:, sl * SLABW + kk * 128: sl * SLABW + (kk + 1) * 128],
                                    gat2[b & 1][:, kk * TB:(kk + 1) * TB],
                                    start=(kk == 0), stop=(kk == NCH_I - 1)))
                    slab_guard[sl] = s_T.v
                    woutmm_g[(b, m)] = s_T.v
                slab_seq(lambda m: slab_load(W_outT_bf, D_INNER, m), ms, body)
            return f

        def bk_wout_x1(b, ms):
            def f():
                if debug:
                    v.wait_ge(q_g.h, q_g.v)
                for m in ms:
                    ps = PS[m & 1]
                    v.wait_ge(s_T.h, woutmm_g[(b, m)])
                    v.wait_ge(q_x.h, xre_q[ms[0]])
                    vi(v.tensor_tensor(x1t[:, m * TB:(m + 1) * TB], ps[:, :],
                                       xre[:, (m & 1) * TB:((m & 1) + 1) * TB],
                                       AluOpType.add))
                    x1_g[(b, m)] = s_V.v
                if debug and ms[-1] == NCH_M - 1:
                    g.wait_ge(s_V.h, s_V.v)
                    for c in range(NCH_M):
                        dma(g, q_g, out=dbg["d_x1"][c * 128:(c + 1) * 128,
                                                    b * TB:(b + 1) * TB],
                            in_=x1t[:, c * TB:(c + 1) * TB])
            return f

        def bk_ln2_sq(b):
            def f():
                ln_stats_sq(x1t)
            return f

        def bk_ln2_fin(b):
            def f():
                ln_stats_fin()
            return f

        def bk_ln2_apply(b):
            def f():
                ln_apply(x1t, ln2t)
                lnap_g[("2", b)] = s_V.v
                if debug:
                    g.wait_ge(s_V.h, s_V.v)
                    for c in range(NCH_M):
                        dma(g, q_g, out=dbg["d_ln2"][c * 128:(c + 1) * 128,
                                                     b * TB:(b + 1) * TB],
                            in_=ln2t[:, c * TB:(c + 1) * TB])
            return f

        def bk_ffn_mix(b, cs):
            xft = gat2[b & 1].ap()[:, 0:NCH_M * TB]
            xkt = gat2[b & 1].ap()[:, NCH_M * TB:2 * NCH_M * TB]
            def f():
                for c in cs:
                    l2 = ln2t[:, c * TB:(c + 1) * TB]
                    vi(v.tensor_copy(ln2p[:, 0:1], carry[:, c:c + 1]))
                    vi(v.tensor_copy(ln2p[:, 1:TB], l2[:, 0:TB - 1]))
                    vi(v.tensor_copy(carry[:, c:c + 1], l2[:, TB - 1:TB]))
                    xf = xft[:, c * TB:(c + 1) * TB]
                    xk = xkt[:, c * TB:(c + 1) * TB]
                    vi(v.tensor_scalar(xf, l2, murc[:, c:c + 1], None, AluOpType.mult))
                    vi(v.scalar_tensor_tensor(ffntmp[:], ln2p[:], omrc[:, c:c + 1], xf,
                                              AluOpType.mult, AluOpType.add))
                    vi(v.tensor_copy(xf, ffntmp[:]))
                    vi(v.tensor_scalar(xk, l2, mukc[:, c:c + 1], None, AluOpType.mult))
                    vi(v.scalar_tensor_tensor(ffntmp[:], ln2p[:], omkc[:, c:c + 1], xk,
                                              AluOpType.mult, AluOpType.add))
                    vi(v.tensor_copy(xk, ffntmp[:]))
                mix_g[b] = s_V.v
            return f

        def bk_wrk(b, which, ms):
            wt = W_rT_bf if which == "r" else W_kT_bf
            off = 0 if which == "r" else NCH_M * TB
            def f():
                src = gat2[b & 1].ap()[:, off:off + NCH_M * TB]
                if debug:
                    s.wait_ge(q_g.h, q_g.v)
                t.wait_ge(s_V.h, mix_g[b])
                def body(sl, m):
                    ps = PS[m & 1]
                    if which == "r":
                        t.wait_ge(s_V.h, x1_g[(b, 6 + (m & 1))])
                        t.wait_ge(s_S.h, rk_g.get((b, "r", m - 2), 0))
                    else:
                        prev = rk_g.get((b, "k", m - 2), rk_g.get((b, "r", 6 + (m & 1)), 0))
                        t.wait_ge(s_S.h, prev)
                    for kk in range(NCH_M):
                        ti(t.matmul(ps[:, :],
                                    wslab[:, sl * SLABW + kk * 128: sl * SLABW + (kk + 1) * 128],
                                    src[:, kk * TB:(kk + 1) * TB],
                                    start=(kk == 0), stop=(kk == NCH_M - 1)))
                    slab_guard[sl] = s_T.v
                    s.wait_ge(s_T.h, s_T.v)
                    if which == "r":
                        si(s.activation(out=rt[:, m * TB:(m + 1) * TB], in_=ps[:, :],
                                        func=AF.Sigmoid))
                    else:
                        xsq = xsqa if (m & 1) == 0 else xsqb
                        si(s.activation(out=xsq[:], in_=ps[:, :], func=AF.Relu))
                        si(s.activation(out=kat_of(b)[:, m * TB:(m + 1) * TB],
                                        in_=xsq[:], func=AF.Square))
                    rk_g[(b, which, m)] = s_S.v
                slab_seq(lambda m: slab_load(wt, D_MODEL, m), ms, body)
            return f

        def bk_wv_mm(b, ms):
            def f():
                t.wait_ge(s_S.h, rk_g[(b, "k", NCH_M - 1)])
                def body(sl, m):
                    ps = PS[m & 1]
                    t.wait_ge(s_S.h, rk_g.get((b, "k", 6 + (m & 1)), 0))
                    t.wait_ge(s_V.h, wv_g.get((b, m - 2), 0))
                    for kk in range(NCH_M):
                        ti(t.matmul(ps[:, :],
                                    wslab[:, sl * SLABW + kk * 128: sl * SLABW + (kk + 1) * 128],
                                    kat_of(b)[:, kk * TB:(kk + 1) * TB],
                                    start=(kk == 0), stop=(kk == NCH_M - 1)))
                    slab_guard[sl] = s_T.v
                    wvm_g[(b, m)] = s_T.v
                slab_seq(lambda m: slab_load(W_vT_bf, D_MODEL, m), ms, body)
                if ms[-1] == NCH_M - 1:
                    wvmm_g[b] = s_T.v
            return f

        def bk_wv_x2(b, ms):
            t0b = b * TB
            def f():
                for m in ms:
                    v.wait_ge(s_T.h, wvm_g[(b, m)])
                    v.wait_ge(q_os[m & 1].h, out_cnt[m & 1])
                    x2b = x2t[:, 0:TB] if (m & 1) == 0 else ytmp[:, 0:TB]
                    vi(v.tensor_tensor(ffntmp[:], PS[m & 1][:, :],
                                       rt[:, m * TB:(m + 1) * TB], AluOpType.mult))
                    vi(v.tensor_tensor(x2b, ffntmp[:], x1t[:, m * TB:(m + 1) * TB],
                                       AluOpType.add))
                    wv_g[(b, m)] = s_V.v
                    sy.wait_ge(s_V.h, s_V.v)
                    dma(sy, q_os[m & 1], out=out[m * 128:(m + 1) * 128, t0b:t0b + TB],
                        in_=x2b)
                    out_cnt[m & 1] = q_os[m & 1].v
                if ms[-1] == NCH_M - 1:
                    wv_g[b] = s_V.v
            return f

        UB = [ube, ubo]
        HB = [hbe, hbo]

        def scan_section(k, c):
            c_abs = k * NCH_I + c
            dt_ch = dtt[:, c * TB:(c + 1) * TB]
            v.wait_ge(s_S.h, dt_g[(k, c)])
            v.wait_ge(s_S.h, silu_g[(k, c)])
            if c == 0:
                v.wait_ge(q_os[1].h, out_cnt[1])
                v.wait_ge(s_T.h, wvmm_g.get(k - 2, 0))
            vi(v.tensor_tensor(vch[:], dt_ch, xs_c(c), AluOpType.mult))
            vi(v.tensor_scalar(dxsb[:], xs_c(c), dc[:, c:c + 1], None, AluOpType.mult))
            vchfree[(k, c)] = s_V.v
            if c == 0:
                v.wait_ge(q_b.h, brep_q[k])
            vi(v.tensor_tensor(UB[0][:], vch[:], brep[:, 0:TB], AluOpType.mult))

            def tail(j):
                kj = c_abs * D_STATE + j
                v.wait_ge(s_T.h, mm_g.get(kj - 3, 0))
                chb = chb4[j % 3]
                vi(v.tensor_tensor(chb[:], HB[j & 1][:], crep[:, j * TB:(j + 1) * TB],
                                   AluOpType.mult))
                ch_g[kj] = s_V.v
                stx = stce if (j & 1) == 0 else stco
                stcol = stx[:, c * 8 + (j >> 1): c * 8 + (j >> 1) + 1]
                vi(v.tensor_scalar(stcol, HB[j & 1][:, TB - 1:TB], 1.0, None,
                                   AluOpType.mult))
                if j == 0:
                    t.wait_ge(s_V.h, ytmp_g.get((k, c - 1) if c else (k - 1, NCH_I - 1), 0))
                t.wait_ge(s_V.h, ch_g[kj])
                ti(t.matmul(ps_y0[:, :], eyeb[:], chb[:],
                            start=(j == 0), stop=(j == D_STATE - 1)))
                mm_g[kj] = s_T.v

            for si_ in range(D_STATE):
                kj = c_abs * D_STATE + si_
                ab = abuf[:, (kj % 3) * TB:((kj % 3) + 1) * TB]
                s.wait_ge(s_V.h, scan_g.get(kj - 3, 0))
                si(s.activation(out=ab, in_=dt_ch, func=AF.Exp,
                                scale=abc[:, si_:si_ + 1]))
                ab_cnt = s_S.v
                if si_ + 1 < D_STATE:
                    vi(v.tensor_tensor(UB[(si_ + 1) & 1][:], vch[:],
                                       brep[:, (si_ + 1) * TB:(si_ + 2) * TB],
                                       AluOpType.mult))
                stx = stce if (si_ & 1) == 0 else stco
                stcol = stx[:, c * 8 + (si_ >> 1): c * 8 + (si_ >> 1) + 1]
                v.wait_ge(s_S.h, ab_cnt)
                scan_g[kj] = s_V.v + 1
                vi(v.tensor_tensor_scan(HB[si_ & 1][:], ab, UB[si_ & 1][:], stcol,
                                        AluOpType.mult, AluOpType.add))
                if si_ > 0:
                    tail(si_ - 1)
            tail(D_STATE - 1)
            v.wait_ge(s_T.h, mm_g[c_abs * D_STATE + D_STATE - 1])
            v.wait_ge(s_S.h, zsilu_g[(k, c)])
            vi(v.tensor_tensor(ytmp[:], ps_y0[:, :], dxsb[:], AluOpType.add))
            ytmp_g[(k, c)] = s_V.v
            vi(v.tensor_tensor(gat2[k & 1][:, c * TB:(c + 1) * TB], ytmp[:],
                               szt[:, c * TB:(c + 1) * TB], AluOpType.mult))
            gate_g[(k, c)] = s_V.v
            if c == NCH_I - 1:
                gatdone_g[k] = s_V.v
            if debug:
                g.wait_ge(s_V.h, s_V.v)
                t0k = k * TB
                dma(g, q_g, out=dbg["d_xs"][c * 128:(c + 1) * 128, t0k:t0k + TB],
                    in_=xs_c(c))
                dma(g, q_g, out=dbg["d_dt"][c * 128:(c + 1) * 128, t0k:t0k + TB],
                    in_=dt_ch)
                dma(g, q_g, out=dbg["d_gy"][c * 128:(c + 1) * 128, t0k:t0k + TB],
                    in_=gat2[k & 1][:, c * TB:(c + 1) * TB])

        def front_agenda(j, k_cur, ag, tailp):
            ag[4].append(fr_load_xt(j))
            ag[5].append(fr_ln1_sq(j))
            ag[6].append(fr_ln1_fin(j))
            ag[7].append(fr_ln1_apply(j))
            ag[8].append(fr_win_xr_mm(j, [0, 1], k_cur))
            ag[9].append(fr_win_xr_ev(j, [0, 1], k_cur))
            ag[9].append(fr_win_xr_mm(j, [2, 3], k_cur))
            ag[10].append(fr_win_xr_ev(j, [2, 3], k_cur))
            ag[10].append(fr_win_xr_mm(j, [4, 5], k_cur))
            ag[11].append(fr_win_xr_ev(j, [4, 5], k_cur))
            ag[11].append(fr_win_xr_mm(j, [6, 7], k_cur))
            ag[12].append(fr_win_xr_ev(j, [6, 7], k_cur))
            ag[12].append(fr_win_xr_mm(j, [8, 9], k_cur))
            ag[13].append(fr_win_xr_ev(j, [8, 9], k_cur))
            ag[13].append(fr_win_xr_mm(j, [10, 11], k_cur))
            ag[14].append(fr_win_xr_ev(j, [10, 11], k_cur))
            ag[14].append(fr_win_xr_mm(j, [12, 13], k_cur))
            ag[15].append(fr_win_xr_ev(j, [12, 13], k_cur))
            ag[15].append(fr_win_xr_mm(j, [14], k_cur))
            ag[10].append(fr_conv(j, [0, 1, 2, 3]))
            ag[10].append(fr_win_z(j, [0, 1, 2, 3], k_cur))
            ag[12].append(fr_win_z(j, [4, 5, 6, 7], k_cur))
            ag[13].append(fr_conv(j, [4, 5, 6, 7, 8]))
            ag[15].append(fr_conv(j, [9, 10, 11, 12, 13]))
            ag[15].append(fr_win_z(j, [8, 9, 10, 11, 12, 13, 14], k_cur))
            tailp.append(fr_win_xr_ev(j, [14], k_cur))
            tailp.append(fr_win_xr_mm(j, [15], k_cur))
            tailp.append(fr_win_xr_ev(j, [15], k_cur))
            tailp.append(fr_conv(j, [14, 15]))
            tailp.append(fr_win_z(j, [15], k_cur))
            tailp.append(fr_wx_wdt(j))

        fr_load_xt(0)()
        fr_ln1_sq(0)()
        fr_ln1_fin(0)()
        fr_ln1_apply(0)()
        for m0 in range(0, NCH_I, 2):
            fr_win_xr_mm(0, [m0, m0 + 1], None)()
            fr_win_xr_ev(0, [m0, m0 + 1], None)()
        for c0 in range(0, NCH_I, 2):
            fr_conv(0, [c0, c0 + 1])()
        for j0 in range(0, NCH_I, 2):
            fr_win_z(0, [j0, j0 + 1], None)()
        fr_wx_wdt(0)()

        for k in range(NB):
            ag = {cc: [] for cc in range(NCH_I)}
            tailp = []
            b = k - 1
            if b >= 0:
                ag[0].append(bk_wout_mm(b, [0, 1]))
                ag[1].append(bk_wout_x1(b, [0, 1]))
                ag[1].append(bk_wout_mm(b, [2, 3]))
                ag[2].append(bk_wout_x1(b, [2, 3]))
                ag[2].append(bk_wout_mm(b, [4, 5]))
                ag[3].append(bk_wout_x1(b, [4, 5]))
                ag[3].append(bk_wout_mm(b, [6, 7]))
                ag[4].append(bk_wout_x1(b, [6, 7]))
                ag[4].append(bk_ln2_sq(b))
                ag[5].append(bk_ln2_fin(b))
                ag[6].append(bk_ln2_apply(b))
                ag[7].append(bk_ffn_mix(b, [0, 1, 2, 3, 4, 5, 6, 7]))
                ag[8].append(bk_wrk(b, "r", [0, 1, 2, 3]))
                ag[10].append(bk_wrk(b, "r", [4, 5, 6, 7]))
                ag[12].append(bk_wrk(b, "k", [0, 1, 2, 3]))
                ag[13].append(bk_wrk(b, "k", [4, 5, 6, 7]))
                ag[14].append(bk_wv_mm(b, [0, 1]))
                ag[15].append(bk_wv_x2(b, [0, 1]))
                ag[15].append(bk_wv_mm(b, [2, 3]))
                wvp = [bk_wv_x2(b, [2, 3]), bk_wv_mm(b, [4, 5]),
                       bk_wv_x2(b, [4, 5]), bk_wv_mm(b, [6, 7]),
                       bk_wv_x2(b, [6, 7])]
            else:
                wvp = []
            j = k + 1
            if j < NB:
                front_agenda(j, k, ag, tailp)
            if tailp and tailp[-1].__name__ == "f":
                pass
            wx = [fn for fn in tailp if fn.__qualname__.startswith("build_program.<locals>.fr_wx_wdt")]
            pre = [fn for fn in tailp if fn not in wx]
            tailp = pre + wvp + wx

            for c in range(NCH_I):
                scan_section(k, c)
                for fn in ag[c]:
                    fn()
            for fn in tailp:
                fn()

        b = NB - 1
        for m0 in range(0, NCH_M, 2):
            bk_wout_mm(b, [m0, m0 + 1])()
            bk_wout_x1(b, [m0, m0 + 1])()
        bk_ln2_sq(b)()
        bk_ln2_fin(b)()
        bk_ln2_apply(b)()
        bk_ffn_mix(b, list(range(NCH_M)))()
        for m0 in range(0, NCH_M, 2):
            bk_wrk(b, "r", [m0, m0 + 1])()
        for m0 in range(0, NCH_M, 2):
            bk_wrk(b, "k", [m0, m0 + 1])()
        for m0 in range(0, NCH_M, 2):
            bk_wv_mm(b, [m0, m0 + 1])()
            bk_wv_x2(b, [m0, m0 + 1])()

        g.wait_ge(q_o.h, q_o.v)
        g.wait_ge(q_o2.h, q_o2.v)
        if debug:
            g.wait_ge(q_g.h, q_g.v)

        @block.sync
        def _(e):
            sy.replay(e)

        @block.gpsimd
        def _(e):
            g.replay(e)

        @block.tensor
        def _(e):
            t.replay(e)

        @block.vector
        def _(e):
            v.replay(e)

        @block.scalar
        def _(e):
            s.replay(e)

    return nc


_PROG_CACHE = {}


def _get_prog(T, TB, debug=False):
    key = (T, TB, debug)
    if key not in _PROG_CACHE:
        _PROG_CACHE[key] = build_program(T, TB, debug)
    return _PROG_CACHE[key]


def make_in_map(b, T, L0_attn_x, L0_attn_W_in, L0_attn_conv_w, L0_attn_conv_b,
                L0_attn_A_log, L0_attn_D, L0_attn_W_x, L0_attn_W_dt, L0_attn_b_dt,
                L0_attn_W_out, L0_ffn_W_r, L0_ffn_W_k, L0_ffn_W_v, L0_ffn_mu_r,
                L0_ffn_mu_k):
    C = np.ascontiguousarray
    f = np.float32
    _bf16 = ml_dtypes.bfloat16
    return {
        "xT": C(np.asarray(L0_attn_x)[b, :T, :].T.astype(f)),
        "W_inT": C(np.asarray(L0_attn_W_in).T.astype(f)),
        "conv_wc": C(np.asarray(L0_attn_conv_w).reshape(NCH_I, 128, 4)
                     .transpose(1, 0, 2).reshape(128, NCH_I * 4).astype(f)),
        "conv_bc": C(np.asarray(L0_attn_conv_b).reshape(NCH_I, 128).T.astype(f)),
        "A_log_r": C(np.asarray(L0_attn_A_log).reshape(1, D_STATE).astype(f)),
        "D_c": C(np.asarray(L0_attn_D).reshape(NCH_I, 128).T.astype(f)),
        "W_xT": C(np.asarray(L0_attn_W_x).T.astype(f)),
        "W_dtT": C(np.asarray(L0_attn_W_dt).T.astype(f)),
        "b_dtc": C(np.asarray(L0_attn_b_dt).reshape(NCH_I, 128).T.astype(f)),
        "W_outT": C(np.asarray(L0_attn_W_out).T.astype(f)),
        "W_rT": C(np.asarray(L0_ffn_W_r).T.astype(f)),
        "W_kT": C(np.asarray(L0_ffn_W_k).T.astype(f)),
        "W_vT": C(np.asarray(L0_ffn_W_v).T.astype(f)),
        "mu_rc": C(np.asarray(L0_ffn_mu_r).reshape(NCH_M, 128).T.astype(f)),
        "mu_kc": C(np.asarray(L0_ffn_mu_k).reshape(NCH_M, 128).T.astype(f)),
        "eye_in": np.eye(128, dtype=_bf16),
    }


def kernel(**inputs):
    T = 8192
    nc = _get_prog(T, 512)
    in_maps = [make_in_map(c >> 1, T, **inputs) for c in range(8)]
    res = run_bass_kernel_spmd(nc, in_maps, core_ids=list(range(8)))
    B = 4
    outs = []
    for b in range(B):
        o = res.results[2 * b]["out"]
        outs.append(np.ascontiguousarray(o.T)[None])
    return np.concatenate(outs, axis=0)


# version 09
# speedup vs baseline: 1.2631x; 1.0139x over previous
import numpy as np
import ml_dtypes
import concourse.bass as bass
import concourse.mybir as mybir
from concourse.bass_utils import run_bass_kernel_spmd
from concourse.alu_op_type import AluOpType

F32 = mybir.dt.float32
BF16 = mybir.dt.bfloat16
AF = mybir.ActivationFunctionType

D_MODEL = 1024
D_INNER = 2048
D_STATE = 16
DT_RANK = 64
D_CONV = 4
NCH_M = D_MODEL // 128
NCH_I = D_INNER // 128
EPS = 1e-5


class Sem:
    __slots__ = ("h", "v")
    def __init__(self, h):
        self.h = h
        self.v = 0


def build_program(T, TB, debug=False):
    NB = T // TB
    assert TB == 512
    nc = bass.Bass("TRN2", target_bir_lowering=False)

    dr = lambda n, sh, dt=F32: nc.dram_tensor(n, sh, dt, kind="ExternalInput").ap()
    xT = dr("xT", [D_MODEL, T])
    W_inT = dr("W_inT", [D_MODEL, 2 * D_INNER])
    conv_w = dr("conv_wc", [128, NCH_I * D_CONV])
    conv_b = dr("conv_bc", [128, NCH_I])
    A_log = dr("A_log_r", [1, D_STATE])
    D_in = dr("D_c", [128, NCH_I])
    W_xT = dr("W_xT", [D_INNER, 96])
    W_dtT = dr("W_dtT", [DT_RANK, D_INNER])
    b_dt = dr("b_dtc", [128, NCH_I])
    W_outT = dr("W_outT", [D_INNER, D_MODEL])
    W_rT = dr("W_rT", [D_MODEL, D_MODEL])
    W_kT = dr("W_kT", [D_MODEL, D_MODEL])
    W_vT = dr("W_vT", [D_MODEL, D_MODEL])
    mu_r = dr("mu_rc", [128, NCH_M])
    mu_k = dr("mu_kc", [128, NCH_M])
    eye_in = dr("eye_in", [128, 128], BF16)

    out = nc.dram_tensor("out", [D_MODEL, T], F32, kind="ExternalOutput").ap()
    dbg = {}
    if debug:
        for nm, rows_ in (("d_ln", D_MODEL), ("d_xs", D_INNER), ("d_dt", D_INNER),
                          ("d_gy", D_INNER), ("d_x1", D_MODEL), ("d_ln2", D_MODEL)):
            dbg[nm] = nc.dram_tensor(nm, [rows_, T], F32, kind="ExternalOutput").ap()

    scr = lambda n, sh, dt=F32: nc.dram_tensor(n, sh, dt, kind="Internal").ap()
    W_inT_bf = scr("W_inT_bf", [D_MODEL, 2 * D_INNER], BF16)
    W_outT_bf = scr("W_outT_bf", [D_INNER, D_MODEL], BF16)
    W_rT_bf = scr("W_rT_bf", [D_MODEL, D_MODEL], BF16)
    W_kT_bf = scr("W_kT_bf", [D_MODEL, D_MODEL], BF16)
    W_vT_bf = scr("W_vT_bf", [D_MODEL, D_MODEL], BF16)
    W_dtT_bf = scr("W_dtT_bf", [DT_RANK, D_INNER], BF16)
    W_xT_bf = scr("W_xT_bf", [D_INNER, 96], BF16)
    bounce = [scr(f"bounce{p}", [4, TB]) for p in range(2)]
    bbc = [scr(f"bbc{p}", [32, TB], BF16) for p in range(2)]
    a_bounce = scr("a_bounce", [1, D_STATE])

    from contextlib import ExitStack
    _st = ExitStack()
    with _st:
        xt = _st.enter_context(nc.sbuf_tensor("xt", [128, NCH_M * TB], F32))
        xre = xt.ap()[:, 0:2 * TB]
        xsqa = _st.enter_context(nc.sbuf_tensor("xsqa", [128, TB], F32))
        xsqb = _st.enter_context(nc.sbuf_tensor("xsqb", [128, TB], F32))
        lnt = _st.enter_context(nc.sbuf_tensor("lnt", [128, NCH_M * TB], BF16))
        ln2t = _st.enter_context(nc.sbuf_tensor("ln2t", [128, NCH_M * TB], BF16))
        rows = _st.enter_context(nc.sbuf_tensor("rows", [1, 2 * TB], F32))
        mrep = _st.enter_context(nc.sbuf_tensor("mrep", [128, TB], BF16))
        rrep = _st.enter_context(nc.sbuf_tensor("rrep", [128, TB], BF16))
        xrt = _st.enter_context(nc.sbuf_tensor("xrt", [128, NCH_I * (TB + 3)], BF16))
        halo = _st.enter_context(nc.sbuf_tensor("halo", [128, NCH_I * 3], BF16))
        convacc = _st.enter_context(nc.sbuf_tensor("convacc", [128, 2 * TB], F32))
        szt = _st.enter_context(nc.sbuf_tensor("szt", [128, NCH_I * TB], BF16))
        dtt = _st.enter_context(nc.sbuf_tensor("dtt", [128, NCH_I * TB], BF16))
        xdbl = _st.enter_context(nc.sbuf_tensor("xdbl", [96, TB], BF16))
        dtr = None
        brep = _st.enter_context(nc.sbuf_tensor("brep", [128, D_STATE * TB], BF16))
        crep = _st.enter_context(nc.sbuf_tensor("crep", [128, D_STATE * TB], BF16))
        vch = _st.enter_context(nc.sbuf_tensor("vch", [128, TB], BF16))
        dxsb = _st.enter_context(nc.sbuf_tensor("dxsb", [128, TB], BF16))
        abuf = _st.enter_context(nc.sbuf_tensor("abuf", [128, 2 * TB], F32))
        ube = _st.enter_context(nc.sbuf_tensor("ube", [128, TB], BF16))
        ubo = _st.enter_context(nc.sbuf_tensor("ubo", [128, TB], BF16))
        hbe = _st.enter_context(nc.sbuf_tensor("hbe", [128, TB], BF16))
        hbo = _st.enter_context(nc.sbuf_tensor("hbo", [128, TB], BF16))
        chb4 = [_st.enter_context(nc.sbuf_tensor(f"chb{i}", [128, TB], BF16))
                for i in range(4)]
        ytmp = _st.enter_context(nc.sbuf_tensor("ytmp", [128, TB], F32))
        stce = _st.enter_context(nc.sbuf_tensor("stce", [128, NCH_I * 8], F32))
        stco = _st.enter_context(nc.sbuf_tensor("stco", [128, NCH_I * 8], F32))
        gat2 = [_st.enter_context(nc.sbuf_tensor(f"gat{i}", [128, NCH_I * TB], BF16))
                for i in range(2)]
        x1t = _st.enter_context(nc.sbuf_tensor("x1t", [128, NCH_M * TB], F32))
        ln2p = _st.enter_context(nc.sbuf_tensor("ln2p", [128, TB], BF16))
        carry = _st.enter_context(nc.sbuf_tensor("carry", [128, NCH_M], BF16))
        ffntmp = _st.enter_context(nc.sbuf_tensor("ffntmp", [128, TB], BF16))
        x2t = _st.enter_context(nc.sbuf_tensor("x2t", [128, TB], F32))
        wslab = _st.enter_context(nc.sbuf_tensor("wslab", [128, 2 * 16 * 128], BF16))
        cwc = _st.enter_context(nc.sbuf_tensor("cwc", [128, NCH_I * D_CONV], F32))
        cbc = _st.enter_context(nc.sbuf_tensor("cbc", [128, NCH_I], F32))
        bdtc = _st.enter_context(nc.sbuf_tensor("bdtc", [128, NCH_I], F32))
        dc = _st.enter_context(nc.sbuf_tensor("dc", [128, NCH_I], F32))
        murc = _st.enter_context(nc.sbuf_tensor("murc", [128, NCH_M], F32))
        mukc = _st.enter_context(nc.sbuf_tensor("mukc", [128, NCH_M], F32))
        omrc = _st.enter_context(nc.sbuf_tensor("omrc", [128, NCH_M], F32))
        omkc = _st.enter_context(nc.sbuf_tensor("omkc", [128, NCH_M], F32))
        arow = _st.enter_context(nc.sbuf_tensor("arow", [1, D_STATE], F32))
        abc = _st.enter_context(nc.sbuf_tensor("abc", [128, D_STATE], F32))
        eyeb = _st.enter_context(nc.sbuf_tensor("eyeb", [128, 128], BF16))
        onesf = _st.enter_context(nc.sbuf_tensor("onesf", [128, 1], F32))
        rt = ln2t
        def kat_of(b):
            return gat2[b & 1].ap()[:, 0:NCH_M * TB]
        ps_a = _st.enter_context(nc.psum_tensor("ps_a", [128, 512], F32))
        ps_b = _st.enter_context(nc.psum_tensor("ps_b", [128, 512], F32))
        ps_c = _st.enter_context(nc.psum_tensor("ps_c", [128, 512], F32))
        ps_d = _st.enter_context(nc.psum_tensor("ps_d", [128, 512], F32))
        ps_y0 = _st.enter_context(nc.psum_tensor("ps_y0", [128, 512], F32))
        ps_y1 = _st.enter_context(nc.psum_tensor("ps_y1", [128, 512], F32))
        ps_s1 = _st.enter_context(nc.psum_tensor("ps_s1", [1, 512], F32))
        ps_s2 = _st.enter_context(nc.psum_tensor("ps_s2", [1, 512], F32))
        _q_w = _st.enter_context(nc.semaphore("q_w"))
        _q_w2 = _st.enter_context(nc.semaphore("q_w2"))
        _q_w3 = _st.enter_context(nc.semaphore("q_w3"))
        _q_x = _st.enter_context(nc.semaphore("q_x"))
        _q_g = _st.enter_context(nc.semaphore("q_g"))
        _q_o = _st.enter_context(nc.semaphore("q_o"))
        _q_o2 = _st.enter_context(nc.semaphore("q_o2"))
        _s_T = _st.enter_context(nc.semaphore("s_T"))
        _s_V = _st.enter_context(nc.semaphore("s_V"))
        _s_S = _st.enter_context(nc.semaphore("s_S"))
        _s_G = _st.enter_context(nc.semaphore("s_G"))
        block = _st.enter_context(nc.Block())
        q_w, q_x, q_g, q_o = Sem(_q_w), Sem(_q_x), Sem(_q_g), Sem(_q_o)
        q_w2, q_w3, q_o2 = Sem(_q_w2), Sem(_q_w3), Sem(_q_o2)
        q_ws = [q_w, q_w2]
        q_b = q_w3
        q_os = [q_o, q_o2]
        s_T, s_V, s_S, s_G = Sem(_s_T), Sem(_s_V), Sem(_s_S), Sem(_s_G)

        class _Pending:
            __slots__ = ("incs",)
            def __init__(self):
                self.incs = []
            def then_inc(self, sem, n):
                self.incs.append((sem, n))
                return self

        def _nm(x):
            try:
                return x.tensor.name
            except AttributeError:
                return None

        _IOMAP = {
            "tensor_tensor": lambda a, kw: ((a[1], a[2]), (a[0],)),
            "tensor_scalar": lambda a, kw: ((a[1], a[2], a[3]), (a[0],)),
            "scalar_tensor_tensor": lambda a, kw: ((a[1], a[2], a[3]), (a[0],)),
            "tensor_tensor_scan": lambda a, kw: ((a[1], a[2], a[3]), (a[0],)),
            "tensor_copy": lambda a, kw: ((a[1],), (a[0],)),
            "memset": lambda a, kw: ((), (a[0],)),
            "activation": lambda a, kw: (
                (kw.get("in_"), kw.get("bias"), kw.get("scale")),
                (kw.get("out"),)),
            "matmul": lambda a, kw: ((a[1], a[2]), ()),
        }

        class _Rec:
            def __init__(self, sem=None):
                self.ops = []
                self.sem = sem
                self.waited = 0
                self.lw = {}
                self.lr = {}

            def _auto(self, meth, a, kw):
                if self.sem is None:
                    return
                if meth == "wait_ge":
                    if a and a[0] is self.sem.h and a[1] > self.waited:
                        self.waited = a[1]
                    return
                io = _IOMAP.get(meth)
                if io is None:
                    return
                rd, wr = io(a, kw)
                need = 0
                for x in rd:
                    n = _nm(x)
                    if n:
                        need = max(need, self.lw.get(n, 0))
                for x in wr:
                    n = _nm(x)
                    if n:
                        need = max(need, self.lw.get(n, 0), self.lr.get(n, 0))
                if need > self.waited:
                    self.ops.append(("wait_ge", (self.sem.h, need), {}, _Pending()))
                    self.waited = need
                cnt = self.sem.v + 1
                for x in rd:
                    n = _nm(x)
                    if n:
                        self.lr[n] = cnt
                for x in wr:
                    n = _nm(x)
                    if n:
                        self.lw[n] = cnt

            def __getattr__(self, meth):
                def call(*a, **kw):
                    self._auto(meth, a, kw)
                    p = _Pending()
                    self.ops.append((meth, a, kw, p))
                    return p
                return call

            def replay(self, eng):
                for meth, a, kw, p in self.ops:
                    instr = getattr(eng, meth)(*a, **kw)
                    for sem, n in p.incs:
                        instr.then_inc(sem, n)

        sy = _Rec()
        g = _Rec(s_G)
        t, v, s = _Rec(s_T), _Rec(s_V), _Rec(s_S)
        PS = [ps_a, ps_b, ps_c, ps_d]

        def dma(eng, q, **kw):
            eng.dma_start(**kw).then_inc(q.h, 16)
            q.v += 16

        def vi(instr):
            instr.then_inc(s_V.h, 1)
            s_V.v += 1

        def si(instr):
            instr.then_inc(s_S.h, 1)
            s_S.v += 1

        def ti(instr):
            instr.then_inc(s_T.h, 1)
            s_T.v += 1

        def gi(instr):
            instr.then_inc(s_G.h, 1)
            s_G.v += 1

        dma(sy, q_x, out=cwc[:], in_=conv_w[:])
        dma(sy, q_x, out=cbc[:], in_=conv_b[:])
        dma(sy, q_x, out=bdtc[:], in_=b_dt[:])
        dma(sy, q_x, out=dc[:], in_=D_in[:])
        dma(sy, q_x, out=murc[:], in_=mu_r[:])
        dma(sy, q_x, out=mukc[:], in_=mu_k[:])
        dma(sy, q_x, out=arow[:], in_=A_log[:])
        dma(sy, q_x, out=eyeb[:], in_=eye_in[:])
        s.wait_ge(q_x.h, q_x.v)
        si(s.activation(out=arow[:], in_=arow[:], func=AF.Exp))
        v.wait_ge(s_S.h, s_S.v)
        vi(v.tensor_scalar(arow[:], arow[:], -1.0, None, AluOpType.mult))
        g.wait_ge(s_V.h, s_V.v)
        dma(g, q_g, out=a_bounce[:], in_=arow[:])
        g.wait_ge(q_g.h, q_g.v)
        dma(g, q_g, out=abc[:], in_=a_bounce.broadcast_to((128, D_STATE)))

        v.wait_ge(q_x.h, q_x.v)
        vi(v.memset(onesf[:], 1.0))
        vi(v.memset(stce[:], 0.0))
        vi(v.memset(stco[:], 0.0))
        vi(v.memset(halo[:], 0.0))
        vi(v.memset(carry[:], 0.0))
        vi(v.tensor_scalar(omrc[:], murc[:], -1.0, 1.0, AluOpType.mult, AluOpType.add))
        vi(v.tensor_scalar(omkc[:], mukc[:], -1.0, 1.0, AluOpType.mult, AluOpType.add))

        def cast_weight(src, dst, R, C):
            for r0 in range(0, R, 128):
                sy.wait_ge(q_g.h, q_g.v)
                dma(sy, q_w, out=xt[:, :C], in_=src[r0:r0 + 128, :])
                v.wait_ge(q_w.h, q_w.v)
                vi(v.tensor_copy(gat2[0][:, :C], xt[:, :C]))
                g.wait_ge(s_V.h, s_V.v)
                dma(g, q_g, out=dst[r0:r0 + 128, :], in_=gat2[0][:, :C])

        cast_weight(W_inT, W_inT_bf, D_MODEL, 2 * D_INNER)
        cast_weight(W_outT, W_outT_bf, D_INNER, D_MODEL)
        cast_weight(W_rT, W_rT_bf, D_MODEL, D_MODEL)
        cast_weight(W_kT, W_kT_bf, D_MODEL, D_MODEL)
        cast_weight(W_vT, W_vT_bf, D_MODEL, D_MODEL)
        cast_weight(W_xT, W_xT_bf, D_INNER, 96)
        sy.wait_ge(q_g.h, q_g.v)
        dma(sy, q_w, out=xt[0:64, :D_INNER], in_=W_dtT[:, :])
        v.wait_ge(q_w.h, q_w.v)
        vi(v.tensor_copy(gat2[0][0:64, :D_INNER], xt[0:64, :D_INNER]))
        g.wait_ge(s_V.h, s_V.v)
        dma(g, q_g, out=W_dtT_bf[:, :], in_=gat2[0][0:64, :D_INNER])
        g.wait_ge(q_g.h, q_g.v)

        SLABW = 16 * 128
        slab_guard = [0, 0]
        slab_cnt = [0, 0]
        slab_rot = [0]

        def slab_load(src_bf, R, m0):
            sl = slab_rot[0]
            slab_rot[0] = (sl + 1) % 2
            nk = R // 128
            sy.wait_ge(s_T.h, slab_guard[sl])
            dma(sy, q_ws[sl],
                out=wslab[:, sl * SLABW: sl * SLABW + nk * 128]
                .rearrange("p (k c) -> p k c", c=128),
                in_=src_bf[:, m0 * 128:(m0 + 1) * 128]
                .rearrange("(k p) c -> p k c", p=128))
            slab_cnt[sl] = q_ws[sl].v
            return sl, q_ws[sl].v

        def slab_seq(loader, ms, body):
            nxt = loader(ms[0])
            for i, m in enumerate(ms):
                sl, cnt = nxt
                if i + 1 < len(ms):
                    nxt = loader(ms[i + 1])
                t.wait_ge(q_ws[sl].h, cnt)
                body(sl, m)

        def slab_load_dt():
            sl = slab_rot[0]
            slab_rot[0] = (sl + 1) % 2
            sy.wait_ge(s_T.h, slab_guard[sl])
            dma(sy, q_ws[sl], out=wslab[0:64, sl * SLABW: sl * SLABW + D_INNER],
                in_=W_dtT_bf[:, :])
            slab_cnt[sl] = q_ws[sl].v
            return sl, q_ws[sl].v

        def xs_c(c):
            return xrt[:, c * (TB + 3) + 3:(c + 1) * (TB + 3)]

        scan_g = {}
        ch_g = {}
        mm_g = {}
        ytmp_g = {}
        gate_g = {}
        vchfree = {}
        silu_g = {}
        zsilu_g = {}
        dt_g = {}
        lnap_g = {}
        x1_g = {}
        mix_g = {}
        dtr_g = {}
        brep_q = {}
        gatdone_g = {}
        rk_g = {}
        wv_g = {}
        wvmm_g = {}
        wvm_g = {}
        winmm_g = {}
        woutmm_g = {}
        xre_q = {}
        brep_si = {}
        winev_g = {}
        out_cnt = {0: 0, 1: 0}
        ln_ctr = [0]
        conv_prev = [0]

        def ln_stats_sq(src_t):
            t.wait_ge(s_V.h, s_V.v)
            for c in range(NCH_M):
                sl = src_t[:, c * TB:(c + 1) * TB]
                xsq = xsqa if (c & 1) == 0 else xsqb
                s.wait_ge(s_T.h, s_T.v)
                si(s.activation(out=xsq[:], in_=sl, func=AF.Square))
                t.wait_ge(s_S.h, s_S.v)
                ti(t.matmul(ps_s1[:, :], onesf[:], sl, start=(c == 0), stop=(c == NCH_M - 1)))
                ti(t.matmul(ps_s2[:, :], onesf[:], xsq[:], start=(c == 0), stop=(c == NCH_M - 1)))

        def ln_stats_fin():
            par = ln_ctr[0] & 1
            ln_ctr[0] += 1
            v.wait_ge(s_T.h, s_T.v)
            r_mean = rows[0:1, 0:TB]
            r_var = rows[0:1, TB:2 * TB]
            msq = ps_s1[0:1, :]
            vi(v.tensor_scalar(r_mean, ps_s1[0:1, :], 1.0 / D_MODEL, None, AluOpType.mult))
            vi(v.tensor_scalar(r_var, ps_s2[0:1, :], 1.0 / D_MODEL, None, AluOpType.mult))
            vi(v.tensor_tensor(msq, r_mean, r_mean, AluOpType.mult))
            vi(v.tensor_tensor(r_var, r_var, msq, AluOpType.subtract))
            vi(v.tensor_scalar(r_var, r_var, EPS, None, AluOpType.add))
            s.wait_ge(s_V.h, s_V.v)
            si(s.activation(out=r_var, in_=r_var, func=AF.Ln))
            si(s.activation(out=r_var, in_=r_var, func=AF.Exp, scale=-0.5))
            g.wait_ge(s_V.h, s_V.v)
            g.wait_ge(s_S.h, s_S.v)
            dma(g, q_g, out=bounce[par][0:1, :], in_=r_mean)
            dma(g, q_g, out=bounce[par][1:2, :], in_=r_var)
            g.wait_ge(q_g.h, q_g.v)
            dma(g, q_g, out=mrep[:], in_=bounce[par][0:1, :].broadcast_to((128, TB)))
            dma(g, q_g, out=rrep[:], in_=bounce[par][1:2, :].broadcast_to((128, TB)))

        def ln_apply(src_t, dst_t):
            v.wait_ge(q_g.h, q_g.v)
            for c in range(NCH_M):
                sl = src_t[:, c * TB:(c + 1) * TB]
                dl = dst_t[:, c * TB:(c + 1) * TB]
                acc = convacc[:, (c & 1) * TB:((c & 1) + 1) * TB]
                vi(v.tensor_tensor(acc, sl, mrep[:], AluOpType.subtract))
                vi(v.tensor_tensor(dl, acc, rrep[:], AluOpType.mult))

        def fr_load_xt(j):
            def f():
                sy.wait_ge(s_V.h, s_V.v)
                sy.wait_ge(s_S.h, s_S.v)
                sy.wait_ge(s_T.h, s_T.v)
                t0j = j * TB
                for c in range(NCH_M):
                    dma(sy, q_x, out=xt[:, c * TB:(c + 1) * TB],
                        in_=xT[c * 128:(c + 1) * 128, t0j:t0j + TB])
            return f

        def fr_ln1_sq(j):
            def f():
                s.wait_ge(q_x.h, q_x.v)
                t.wait_ge(q_x.h, q_x.v)
                ln_stats_sq(xt)
            return f

        def fr_ln1_fin(j):
            def f():
                ln_stats_fin()
            return f

        def fr_ln1_apply(j):
            def f():
                if debug:
                    v.wait_ge(q_g.h, q_g.v)
                ln_apply(xt, lnt)
                lnap_g[("1", j)] = s_V.v
                if debug:
                    g.wait_ge(s_V.h, s_V.v)
                    for c in range(NCH_M):
                        dma(g, q_g, out=dbg["d_ln"][c * 128:(c + 1) * 128,
                                                    j * TB:(j + 1) * TB],
                            in_=lnt[:, c * TB:(c + 1) * TB])
            return f

        def fr_win_xr_mm(j, ms, k_cur):
            def f():
                t.wait_ge(s_V.h, lnap_g[("1", j)])
                t.wait_ge(s_S.h, dt_g.get((j - 1, NCH_I - 1), 0))
                def body(sl, m):
                    ps = PS[2 + (m & 1)]
                    t.wait_ge(s_V.h, winev_g.get((j, m - 2), 0))
                    if k_cur is not None:
                        t.wait_ge(s_V.h, vchfree.get((k_cur, m), 0))
                    for kk in range(NCH_M):
                        ti(t.matmul(ps[:, :],
                                    wslab[:, sl * SLABW + kk * 128: sl * SLABW + (kk + 1) * 128],
                                    lnt[:, kk * TB:(kk + 1) * TB],
                                    start=(kk == 0), stop=(kk == NCH_M - 1)))
                    slab_guard[sl] = s_T.v
                    winmm_g[(j, m)] = s_T.v
                slab_seq(lambda m: slab_load(W_inT_bf, D_MODEL, m), ms, body)
            return f

        def fr_win_xr_ev(j, ms, k_cur):
            def f():
                if debug:
                    v.wait_ge(q_g.h, q_g.v)
                for m in ms:
                    ps = PS[2 + (m & 1)]
                    v.wait_ge(s_T.h, winmm_g[(j, m)])
                    vi(v.tensor_copy(xrt[:, m * (TB + 3) + 3:(m + 1) * (TB + 3)], ps[:, :]))
                    winev_g[(j, m)] = s_V.v
            return f

        def fr_win_z(j, js, k_cur):
            def f():
                t.wait_ge(s_V.h, lnap_g[("1", j)])
                def body(sl, jz):
                    t.wait_ge(s_S.h, zsilu_g.get((j, jz - 1),
                                                 zsilu_g.get((j - 1, NCH_I - 1), 0)))
                    if k_cur is not None:
                        t.wait_ge(s_V.h, gate_g.get((k_cur, jz), 0))
                    for kk in range(NCH_M):
                        ti(t.matmul(ps_y1[:, :],
                                    wslab[:, sl * SLABW + kk * 128: sl * SLABW + (kk + 1) * 128],
                                    lnt[:, kk * TB:(kk + 1) * TB],
                                    start=(kk == 0), stop=(kk == NCH_M - 1)))
                    slab_guard[sl] = s_T.v
                    s.wait_ge(s_T.h, s_T.v)
                    si(s.activation(out=szt[:, jz * TB:(jz + 1) * TB], in_=ps_y1[:, :],
                                    func=AF.Silu))
                    zsilu_g[(j, jz)] = s_S.v
                slab_seq(lambda jz: slab_load(W_inT_bf, D_MODEL, NCH_I + jz), js, body)
            return f

        def fr_conv(j, cs):
            def f():
                for c in cs:
                    xr_ch = xrt[:, c * (TB + 3):(c + 1) * (TB + 3)]
                    acc0 = convacc[:, 0:TB]
                    acc1 = convacc[:, TB:2 * TB]
                    v.wait_ge(s_S.h, conv_prev[0])
                    vi(v.tensor_copy(xr_ch[:, 0:3], halo[:, c * 3:(c + 1) * 3]))
                    vi(v.tensor_scalar(acc0, xr_ch[:, 0:TB], cwc[:, c * 4:c * 4 + 1],
                                       None, AluOpType.mult))
                    vi(v.scalar_tensor_tensor(acc1, xr_ch[:, 1:1 + TB],
                                              cwc[:, c * 4 + 1:c * 4 + 2], acc0,
                                              AluOpType.mult, AluOpType.add))
                    vi(v.scalar_tensor_tensor(acc0, xr_ch[:, 2:2 + TB],
                                              cwc[:, c * 4 + 2:c * 4 + 3], acc1,
                                              AluOpType.mult, AluOpType.add))
                    vi(v.scalar_tensor_tensor(acc1, xr_ch[:, 3:3 + TB],
                                              cwc[:, c * 4 + 3:c * 4 + 4], acc0,
                                              AluOpType.mult, AluOpType.add))
                    vi(v.tensor_copy(halo[:, c * 3:(c + 1) * 3], xr_ch[:, TB:TB + 3]))
                    s.wait_ge(s_V.h, s_V.v)
                    si(s.activation(out=xr_ch[:, 3:3 + TB], in_=acc1, func=AF.Silu,
                                    bias=cbc[:, c:c + 1]))
                    silu_g[(j, c)] = s_S.v
                    conv_prev[0] = s_S.v
            return f

        def fr_wx_wdt(j):
            def f():
                slx = slab_rot[0]
                slab_rot[0] = (slx + 1) % 2
                sy.wait_ge(s_T.h, slab_guard[slx])
                dma(sy, q_ws[slx],
                    out=wslab[:, slx * SLABW: slx * SLABW + NCH_I * 96]
                    .rearrange("p (k c) -> p k c", c=96),
                    in_=W_xT_bf.rearrange("(k p) c -> p k c", p=128))
                slab_cnt[slx] = q_ws[slx].v
                t.wait_ge(s_S.h, silu_g[(j, NCH_I - 1)])
                t.wait_ge(s_S.h, dt_g.get((j - 1, NCH_I - 1), 0))
                t.wait_ge(s_V.h, s_V.v)
                t.wait_ge(q_ws[slx].h, slab_cnt[slx])
                for kk in range(NCH_I):
                    ti(t.matmul(ps_c[0:96, :],
                                wslab[:, slx * SLABW + kk * 96: slx * SLABW + (kk + 1) * 96],
                                xs_c(kk),
                                start=(kk == 0), stop=(kk == NCH_I - 1)))
                slab_guard[slx] = s_T.v
                v.wait_ge(s_T.h, s_T.v)
                v.wait_ge(q_g.h, q_g.v)
                vi(v.tensor_copy(xdbl[:, :], ps_c[0:96, :]))
                dtr_g[j] = s_V.v
                par = j & 1
                g.wait_ge(s_V.h, s_V.v)
                g.wait_ge(q_g.h, q_g.v)
                dma(g, q_g, out=bbc[par][:, :], in_=xdbl[64:96, :])
                g.wait_ge(q_g.h, q_g.v)
                for si_ in range(D_STATE):
                    dma(g, q_b, out=brep[:, si_ * TB:(si_ + 1) * TB],
                        in_=bbc[par][si_:si_ + 1, :].broadcast_to((128, TB)))
                    dma(g, q_b, out=crep[:, si_ * TB:(si_ + 1) * TB],
                        in_=bbc[par][16 + si_:17 + si_, :].broadcast_to((128, TB)))
                brep_q[j] = q_b.v
                sl, _cnt = slab_load_dt()
                if debug:
                    s.wait_ge(q_g.h, q_g.v)
                t.wait_ge(s_V.h, s_V.v)
                t.wait_ge(q_ws[sl].h, _cnt)
                for m in range(NCH_I):
                    ps = PS[2 + (m & 1)]
                    xsq = xsqb if (m & 1) == 0 else xsqa
                    if m >= 2:
                        t.wait_ge(s_S.h, dt_g[(j, m - 2)])
                    ti(t.matmul(ps[:, :], wslab[0:64, sl * SLABW + m * 128:
                                                sl * SLABW + (m + 1) * 128],
                                xdbl[0:64, :], start=True, stop=True))
                    s.wait_ge(s_T.h, s_T.v)
                    si(s.activation(out=xsq[:], in_=ps[:, :], func=AF.Exp,
                                    bias=bdtc[:, m:m + 1]))
                    si(s.activation(out=dtt[:, m * TB:(m + 1) * TB], in_=xsq[:],
                                    func=AF.Ln, bias=1.0))
                    dt_g[(j, m)] = s_S.v
                slab_guard[sl] = s_T.v
            return f

        def bk_wout_mm(b, ms):
            def f():
                t0b = b * TB
                sy.wait_ge(s_V.h, s_V.v)
                for m in ms:
                    dma(sy, q_x, out=xre[:, (m & 1) * TB:((m & 1) + 1) * TB],
                        in_=xT[m * 128:(m + 1) * 128, t0b:t0b + TB])
                xre_q[ms[0]] = q_x.v
                t.wait_ge(s_V.h, gatdone_g[b])
                t.wait_ge(s_V.h, wv_g.get(b - 1, 0))
                def body(sl, m):
                    ps = PS[m & 1]
                    t.wait_ge(s_V.h, x1_g.get((b, m - 2), 0))
                    for kk in range(NCH_I):
                        ti(t.matmul(ps[:, :],
                                    wslab[:, sl * SLABW + kk * 128: sl * SLABW + (kk + 1) * 128],
                                    gat2[b & 1][:, kk * TB:(kk + 1) * TB],
                                    start=(kk == 0), stop=(kk == NCH_I - 1)))
                    slab_guard[sl] = s_T.v
                    woutmm_g[(b, m)] = s_T.v
                slab_seq(lambda m: slab_load(W_outT_bf, D_INNER, m), ms, body)
            return f

        def bk_wout_x1(b, ms):
            def f():
                if debug:
                    v.wait_ge(q_g.h, q_g.v)
                for m in ms:
                    ps = PS[m & 1]
                    v.wait_ge(s_T.h, woutmm_g[(b, m)])
                    v.wait_ge(q_x.h, xre_q[ms[0]])
                    vi(v.tensor_tensor(x1t[:, m * TB:(m + 1) * TB], ps[:, :],
                                       xre[:, (m & 1) * TB:((m & 1) + 1) * TB],
                                       AluOpType.add))
                    x1_g[(b, m)] = s_V.v
                if debug and ms[-1] == NCH_M - 1:
                    g.wait_ge(s_V.h, s_V.v)
                    for c in range(NCH_M):
                        dma(g, q_g, out=dbg["d_x1"][c * 128:(c + 1) * 128,
                                                    b * TB:(b + 1) * TB],
                            in_=x1t[:, c * TB:(c + 1) * TB])
            return f

        def bk_ln2_sq(b):
            def f():
                ln_stats_sq(x1t)
            return f

        def bk_ln2_fin(b):
            def f():
                ln_stats_fin()
            return f

        def bk_ln2_apply(b):
            def f():
                ln_apply(x1t, ln2t)
                lnap_g[("2", b)] = s_V.v
                if debug:
                    g.wait_ge(s_V.h, s_V.v)
                    for c in range(NCH_M):
                        dma(g, q_g, out=dbg["d_ln2"][c * 128:(c + 1) * 128,
                                                     b * TB:(b + 1) * TB],
                            in_=ln2t[:, c * TB:(c + 1) * TB])
            return f

        def bk_ffn_mix(b, cs):
            xft = gat2[b & 1].ap()[:, 0:NCH_M * TB]
            xkt = gat2[b & 1].ap()[:, NCH_M * TB:2 * NCH_M * TB]
            def f():
                for c in cs:
                    l2 = ln2t[:, c * TB:(c + 1) * TB]
                    vi(v.tensor_copy(ln2p[:, 0:1], carry[:, c:c + 1]))
                    vi(v.tensor_copy(ln2p[:, 1:TB], l2[:, 0:TB - 1]))
                    vi(v.tensor_copy(carry[:, c:c + 1], l2[:, TB - 1:TB]))
                    xf = xft[:, c * TB:(c + 1) * TB]
                    xk = xkt[:, c * TB:(c + 1) * TB]
                    vi(v.tensor_scalar(xf, l2, murc[:, c:c + 1], None, AluOpType.mult))
                    vi(v.scalar_tensor_tensor(ffntmp[:], ln2p[:], omrc[:, c:c + 1], xf,
                                              AluOpType.mult, AluOpType.add))
                    vi(v.tensor_copy(xf, ffntmp[:]))
                    vi(v.tensor_scalar(xk, l2, mukc[:, c:c + 1], None, AluOpType.mult))
                    vi(v.scalar_tensor_tensor(ffntmp[:], ln2p[:], omkc[:, c:c + 1], xk,
                                              AluOpType.mult, AluOpType.add))
                    vi(v.tensor_copy(xk, ffntmp[:]))
                mix_g[b] = s_V.v
            return f

        def bk_wrk(b, which, ms):
            wt = W_rT_bf if which == "r" else W_kT_bf
            off = 0 if which == "r" else NCH_M * TB
            def f():
                src = gat2[b & 1].ap()[:, off:off + NCH_M * TB]
                if debug:
                    s.wait_ge(q_g.h, q_g.v)
                t.wait_ge(s_V.h, mix_g[b])
                def body(sl, m):
                    ps = PS[m & 1]
                    if which == "r":
                        t.wait_ge(s_V.h, x1_g[(b, 6 + (m & 1))])
                        t.wait_ge(s_S.h, rk_g.get((b, "r", m - 2), 0))
                    else:
                        prev = rk_g.get((b, "k", m - 2), rk_g.get((b, "r", 6 + (m & 1)), 0))
                        t.wait_ge(s_S.h, prev)
                    for kk in range(NCH_M):
                        ti(t.matmul(ps[:, :],
                                    wslab[:, sl * SLABW + kk * 128: sl * SLABW + (kk + 1) * 128],
                                    src[:, kk * TB:(kk + 1) * TB],
                                    start=(kk == 0), stop=(kk == NCH_M - 1)))
                    slab_guard[sl] = s_T.v
                    s.wait_ge(s_T.h, s_T.v)
                    if which == "r":
                        si(s.activation(out=rt[:, m * TB:(m + 1) * TB], in_=ps[:, :],
                                        func=AF.Sigmoid))
                    else:
                        xsq = xsqa if (m & 1) == 0 else xsqb
                        si(s.activation(out=xsq[:], in_=ps[:, :], func=AF.Relu))
                        si(s.activation(out=kat_of(b)[:, m * TB:(m + 1) * TB],
                                        in_=xsq[:], func=AF.Square))
                    rk_g[(b, which, m)] = s_S.v
                slab_seq(lambda m: slab_load(wt, D_MODEL, m), ms, body)
            return f

        def bk_wv_mm(b, ms):
            def f():
                t.wait_ge(s_S.h, rk_g[(b, "k", NCH_M - 1)])
                def body(sl, m):
                    ps = PS[m & 1]
                    t.wait_ge(s_S.h, rk_g.get((b, "k", 6 + (m & 1)), 0))
                    t.wait_ge(s_V.h, wv_g.get((b, m - 2), 0))
                    for kk in range(NCH_M):
                        ti(t.matmul(ps[:, :],
                                    wslab[:, sl * SLABW + kk * 128: sl * SLABW + (kk + 1) * 128],
                                    kat_of(b)[:, kk * TB:(kk + 1) * TB],
                                    start=(kk == 0), stop=(kk == NCH_M - 1)))
                    slab_guard[sl] = s_T.v
                    wvm_g[(b, m)] = s_T.v
                slab_seq(lambda m: slab_load(W_vT_bf, D_MODEL, m), ms, body)
                if ms[-1] == NCH_M - 1:
                    wvmm_g[b] = s_T.v
            return f

        def bk_wv_x2(b, ms):
            t0b = b * TB
            def f():
                for m in ms:
                    v.wait_ge(s_T.h, wvm_g[(b, m)])
                    v.wait_ge(q_os[m & 1].h, out_cnt[m & 1])
                    x2b = x2t[:, 0:TB] if (m & 1) == 0 else ytmp[:, 0:TB]
                    vi(v.tensor_tensor(ffntmp[:], PS[m & 1][:, :],
                                       rt[:, m * TB:(m + 1) * TB], AluOpType.mult))
                    vi(v.tensor_tensor(x2b, ffntmp[:], x1t[:, m * TB:(m + 1) * TB],
                                       AluOpType.add))
                    wv_g[(b, m)] = s_V.v
                    sy.wait_ge(s_V.h, s_V.v)
                    dma(sy, q_os[m & 1], out=out[m * 128:(m + 1) * 128, t0b:t0b + TB],
                        in_=x2b)
                    out_cnt[m & 1] = q_os[m & 1].v
                if ms[-1] == NCH_M - 1:
                    wv_g[b] = s_V.v
            return f

        UB = [ube, ubo]
        HB = [hbe, hbo]

        def scan_section(k, c):
            c_abs = k * NCH_I + c
            dt_ch = dtt[:, c * TB:(c + 1) * TB]
            v.wait_ge(s_S.h, dt_g[(k, c)])
            v.wait_ge(s_S.h, silu_g[(k, c)])
            if c == 0:
                v.wait_ge(q_os[1].h, out_cnt[1])
                v.wait_ge(s_T.h, wvmm_g.get(k - 2, 0))
            vi(v.tensor_tensor(vch[:], dt_ch, xs_c(c), AluOpType.mult))
            vi(v.tensor_scalar(dxsb[:], xs_c(c), dc[:, c:c + 1], None, AluOpType.mult))
            vchfree[(k, c)] = s_V.v
            if c == 0:
                v.wait_ge(q_b.h, brep_q[k])
            vi(v.tensor_tensor(UB[0][:], vch[:], brep[:, 0:TB], AluOpType.mult))

            def tail(j):
                kj = c_abs * D_STATE + j
                v.wait_ge(s_T.h, mm_g.get(kj - 4, 0))
                chb = chb4[j & 3]
                vi(v.tensor_tensor(chb[:], HB[j & 1][:], crep[:, j * TB:(j + 1) * TB],
                                   AluOpType.mult))
                ch_g[kj] = s_V.v
                stx = stce if (j & 1) == 0 else stco
                stcol = stx[:, c * 8 + (j >> 1): c * 8 + (j >> 1) + 1]
                vi(v.tensor_scalar(stcol, HB[j & 1][:, TB - 1:TB], 1.0, None,
                                   AluOpType.mult))
                if j == 0:
                    t.wait_ge(s_V.h, ytmp_g.get((k, c - 1) if c else (k - 1, NCH_I - 1), 0))
                t.wait_ge(s_V.h, ch_g[kj])
                ti(t.matmul(ps_y0[:, :], eyeb[:], chb[:],
                            start=(j == 0), stop=(j == D_STATE - 1)))
                mm_g[kj] = s_T.v

            for si_ in range(D_STATE):
                kj = c_abs * D_STATE + si_
                ab = abuf[:, (si_ & 1) * TB:((si_ & 1) + 1) * TB]
                s.wait_ge(s_V.h, scan_g.get(kj - 2, 0))
                si(s.activation(out=ab, in_=dt_ch, func=AF.Exp,
                                scale=abc[:, si_:si_ + 1]))
                ab_cnt = s_S.v
                if si_ + 1 < D_STATE:
                    vi(v.tensor_tensor(UB[(si_ + 1) & 1][:], vch[:],
                                       brep[:, (si_ + 1) * TB:(si_ + 2) * TB],
                                       AluOpType.mult))
                stx = stce if (si_ & 1) == 0 else stco
                stcol = stx[:, c * 8 + (si_ >> 1): c * 8 + (si_ >> 1) + 1]
                v.wait_ge(s_S.h, ab_cnt)
                scan_g[kj] = s_V.v + 1
                vi(v.tensor_tensor_scan(HB[si_ & 1][:], ab, UB[si_ & 1][:], stcol,
                                        AluOpType.mult, AluOpType.add))
                if si_ > 0:
                    tail(si_ - 1)
            tail(D_STATE - 1)
            v.wait_ge(s_T.h, mm_g[c_abs * D_STATE + D_STATE - 1])
            v.wait_ge(s_S.h, zsilu_g[(k, c)])
            vi(v.tensor_tensor(ytmp[:], ps_y0[:, :], dxsb[:], AluOpType.add))
            ytmp_g[(k, c)] = s_V.v
            vi(v.tensor_tensor(gat2[k & 1][:, c * TB:(c + 1) * TB], ytmp[:],
                               szt[:, c * TB:(c + 1) * TB], AluOpType.mult))
            gate_g[(k, c)] = s_V.v
            if c == NCH_I - 1:
                gatdone_g[k] = s_V.v
            if debug:
                g.wait_ge(s_V.h, s_V.v)
                t0k = k * TB
                dma(g, q_g, out=dbg["d_xs"][c * 128:(c + 1) * 128, t0k:t0k + TB],
                    in_=xs_c(c))
                dma(g, q_g, out=dbg["d_dt"][c * 128:(c + 1) * 128, t0k:t0k + TB],
                    in_=dt_ch)
                dma(g, q_g, out=dbg["d_gy"][c * 128:(c + 1) * 128, t0k:t0k + TB],
                    in_=gat2[k & 1][:, c * TB:(c + 1) * TB])

        def front_agenda(j, k_cur, ag, tailp):
            ag[4].append(fr_load_xt(j))
            ag[5].append(fr_ln1_sq(j))
            ag[6].append(fr_ln1_fin(j))
            ag[7].append(fr_ln1_apply(j))
            ag[8].append(fr_win_xr_mm(j, [0, 1], k_cur))
            ag[9].append(fr_win_xr_ev(j, [0, 1], k_cur))
            ag[9].append(fr_win_xr_mm(j, [2, 3], k_cur))
            ag[10].append(fr_win_xr_ev(j, [2, 3], k_cur))
            ag[10].append(fr_win_xr_mm(j, [4, 5], k_cur))
            ag[11].append(fr_win_xr_ev(j, [4, 5], k_cur))
            ag[11].append(fr_win_xr_mm(j, [6, 7], k_cur))
            ag[12].append(fr_win_xr_ev(j, [6, 7], k_cur))
            ag[12].append(fr_win_xr_mm(j, [8, 9], k_cur))
            ag[13].append(fr_win_xr_ev(j, [8, 9], k_cur))
            ag[13].append(fr_win_xr_mm(j, [10, 11], k_cur))
            ag[14].append(fr_win_xr_ev(j, [10, 11], k_cur))
            ag[14].append(fr_win_xr_mm(j, [12, 13], k_cur))
            ag[15].append(fr_win_xr_ev(j, [12, 13], k_cur))
            ag[15].append(fr_win_xr_mm(j, [14], k_cur))
            ag[10].append(fr_conv(j, [0, 1, 2, 3]))
            ag[10].append(fr_win_z(j, [0, 1, 2, 3], k_cur))
            ag[12].append(fr_win_z(j, [4, 5, 6, 7], k_cur))
            ag[13].append(fr_conv(j, [4, 5, 6, 7, 8]))
            ag[15].append(fr_conv(j, [9, 10, 11, 12, 13]))
            ag[15].append(fr_win_z(j, [8, 9, 10, 11, 12, 13, 14], k_cur))
            tailp.append(fr_win_xr_ev(j, [14], k_cur))
            tailp.append(fr_win_xr_mm(j, [15], k_cur))
            tailp.append(fr_win_xr_ev(j, [15], k_cur))
            tailp.append(fr_conv(j, [14, 15]))
            tailp.append(fr_win_z(j, [15], k_cur))
            tailp.append(fr_wx_wdt(j))

        fr_load_xt(0)()
        fr_ln1_sq(0)()
        fr_ln1_fin(0)()
        fr_ln1_apply(0)()
        for m0 in range(0, NCH_I, 2):
            fr_win_xr_mm(0, [m0, m0 + 1], None)()
            fr_win_xr_ev(0, [m0, m0 + 1], None)()
        for c0 in range(0, NCH_I, 2):
            fr_conv(0, [c0, c0 + 1])()
        for j0 in range(0, NCH_I, 2):
            fr_win_z(0, [j0, j0 + 1], None)()
        fr_wx_wdt(0)()

        for k in range(NB):
            ag = {cc: [] for cc in range(NCH_I)}
            tailp = []
            b = k - 1
            if b >= 0:
                ag[0].append(bk_wout_mm(b, [0, 1]))
                ag[1].append(bk_wout_x1(b, [0, 1]))
                ag[1].append(bk_wout_mm(b, [2, 3]))
                ag[2].append(bk_wout_x1(b, [2, 3]))
                ag[2].append(bk_wout_mm(b, [4, 5]))
                ag[3].append(bk_wout_x1(b, [4, 5]))
                ag[3].append(bk_wout_mm(b, [6, 7]))
                ag[4].append(bk_wout_x1(b, [6, 7]))
                ag[4].append(bk_ln2_sq(b))
                ag[5].append(bk_ln2_fin(b))
                ag[6].append(bk_ln2_apply(b))
                ag[7].append(bk_ffn_mix(b, [0, 1, 2, 3, 4, 5, 6, 7]))
                ag[8].append(bk_wrk(b, "r", [0, 1, 2, 3]))
                ag[10].append(bk_wrk(b, "r", [4, 5, 6, 7]))
                ag[12].append(bk_wrk(b, "k", [0, 1, 2, 3]))
                ag[13].append(bk_wrk(b, "k", [4, 5, 6, 7]))
                ag[14].append(bk_wv_mm(b, [0, 1]))
                ag[15].append(bk_wv_x2(b, [0, 1]))
                ag[15].append(bk_wv_mm(b, [2, 3]))
                wvp = [bk_wv_x2(b, [2, 3]), bk_wv_mm(b, [4, 5]),
                       bk_wv_x2(b, [4, 5]), bk_wv_mm(b, [6, 7]),
                       bk_wv_x2(b, [6, 7])]
            else:
                wvp = []
            j = k + 1
            if j < NB:
                front_agenda(j, k, ag, tailp)
            if tailp and tailp[-1].__name__ == "f":
                pass
            wx = [fn for fn in tailp if fn.__qualname__.startswith("build_program.<locals>.fr_wx_wdt")]
            pre = [fn for fn in tailp if fn not in wx]
            tailp = pre + wvp + wx

            for c in range(NCH_I):
                scan_section(k, c)
                for fn in ag[c]:
                    fn()
            for fn in tailp:
                fn()

        b = NB - 1
        for m0 in range(0, NCH_M, 2):
            bk_wout_mm(b, [m0, m0 + 1])()
            bk_wout_x1(b, [m0, m0 + 1])()
        bk_ln2_sq(b)()
        bk_ln2_fin(b)()
        bk_ln2_apply(b)()
        bk_ffn_mix(b, list(range(NCH_M)))()
        for m0 in range(0, NCH_M, 2):
            bk_wrk(b, "r", [m0, m0 + 1])()
        for m0 in range(0, NCH_M, 2):
            bk_wrk(b, "k", [m0, m0 + 1])()
        for m0 in range(0, NCH_M, 2):
            bk_wv_mm(b, [m0, m0 + 1])()
            bk_wv_x2(b, [m0, m0 + 1])()

        g.wait_ge(q_o.h, q_o.v)
        g.wait_ge(q_o2.h, q_o2.v)
        if debug:
            g.wait_ge(q_g.h, q_g.v)

        @block.sync
        def _(e):
            sy.replay(e)

        @block.gpsimd
        def _(e):
            g.replay(e)

        @block.tensor
        def _(e):
            t.replay(e)

        @block.vector
        def _(e):
            v.replay(e)

        @block.scalar
        def _(e):
            s.replay(e)

    return nc


_PROG_CACHE = {}


def _get_prog(T, TB, debug=False):
    key = (T, TB, debug)
    if key not in _PROG_CACHE:
        _PROG_CACHE[key] = build_program(T, TB, debug)
    return _PROG_CACHE[key]


def make_in_map(b, T, L0_attn_x, L0_attn_W_in, L0_attn_conv_w, L0_attn_conv_b,
                L0_attn_A_log, L0_attn_D, L0_attn_W_x, L0_attn_W_dt, L0_attn_b_dt,
                L0_attn_W_out, L0_ffn_W_r, L0_ffn_W_k, L0_ffn_W_v, L0_ffn_mu_r,
                L0_ffn_mu_k):
    C = np.ascontiguousarray
    f = np.float32
    _bf16 = ml_dtypes.bfloat16
    return {
        "xT": C(np.asarray(L0_attn_x)[b, :T, :].T.astype(f)),
        "W_inT": C(np.asarray(L0_attn_W_in).T.astype(f)),
        "conv_wc": C(np.asarray(L0_attn_conv_w).reshape(NCH_I, 128, 4)
                     .transpose(1, 0, 2).reshape(128, NCH_I * 4).astype(f)),
        "conv_bc": C(np.asarray(L0_attn_conv_b).reshape(NCH_I, 128).T.astype(f)),
        "A_log_r": C(np.asarray(L0_attn_A_log).reshape(1, D_STATE).astype(f)),
        "D_c": C(np.asarray(L0_attn_D).reshape(NCH_I, 128).T.astype(f)),
        "W_xT": C(np.asarray(L0_attn_W_x).T.astype(f)),
        "W_dtT": C(np.asarray(L0_attn_W_dt).T.astype(f)),
        "b_dtc": C(np.asarray(L0_attn_b_dt).reshape(NCH_I, 128).T.astype(f)),
        "W_outT": C(np.asarray(L0_attn_W_out).T.astype(f)),
        "W_rT": C(np.asarray(L0_ffn_W_r).T.astype(f)),
        "W_kT": C(np.asarray(L0_ffn_W_k).T.astype(f)),
        "W_vT": C(np.asarray(L0_ffn_W_v).T.astype(f)),
        "mu_rc": C(np.asarray(L0_ffn_mu_r).reshape(NCH_M, 128).T.astype(f)),
        "mu_kc": C(np.asarray(L0_ffn_mu_k).reshape(NCH_M, 128).T.astype(f)),
        "eye_in": np.eye(128, dtype=_bf16),
    }


def kernel(**inputs):
    T = 8192
    nc = _get_prog(T, 512)
    in_maps = [make_in_map(c >> 1, T, **inputs) for c in range(8)]
    res = run_bass_kernel_spmd(nc, in_maps, core_ids=list(range(8)))
    B = 4
    outs = []
    for b in range(B):
        o = res.results[2 * b]["out"]
        outs.append(np.ascontiguousarray(o.T)[None])
    return np.concatenate(outs, axis=0)


# version 10 (final)
# speedup vs baseline: 1.2634x; 1.0002x over previous
import numpy as np
import ml_dtypes
import concourse.bass as bass
import concourse.mybir as mybir
from concourse.bass_utils import run_bass_kernel_spmd
from concourse.alu_op_type import AluOpType

F32 = mybir.dt.float32
BF16 = mybir.dt.bfloat16
AF = mybir.ActivationFunctionType

D_MODEL = 1024
D_INNER = 2048
D_STATE = 16
DT_RANK = 64
D_CONV = 4
NCH_M = D_MODEL // 128
NCH_I = D_INNER // 128
EPS = 1e-5


class Sem:
    __slots__ = ("h", "v")
    def __init__(self, h):
        self.h = h
        self.v = 0


def build_program(T, TB, debug=False):
    NB = T // TB
    assert TB == 512
    nc = bass.Bass("TRN2", target_bir_lowering=False)

    dr = lambda n, sh, dt=F32: nc.dram_tensor(n, sh, dt, kind="ExternalInput").ap()
    xT = dr("xT", [D_MODEL, T])
    W_inT = dr("W_inT", [D_MODEL, 2 * D_INNER])
    conv_w = dr("conv_wc", [128, NCH_I * D_CONV])
    conv_b = dr("conv_bc", [128, NCH_I])
    A_log = dr("A_log_r", [1, D_STATE])
    D_in = dr("D_c", [128, NCH_I])
    W_xT = dr("W_xT", [D_INNER, 96])
    W_dtT = dr("W_dtT", [DT_RANK, D_INNER])
    b_dt = dr("b_dtc", [128, NCH_I])
    W_outT = dr("W_outT", [D_INNER, D_MODEL])
    W_rT = dr("W_rT", [D_MODEL, D_MODEL])
    W_kT = dr("W_kT", [D_MODEL, D_MODEL])
    W_vT = dr("W_vT", [D_MODEL, D_MODEL])
    mu_r = dr("mu_rc", [128, NCH_M])
    mu_k = dr("mu_kc", [128, NCH_M])
    eye_in = dr("eye_in", [128, 128], BF16)

    out = nc.dram_tensor("out", [D_MODEL, T], F32, kind="ExternalOutput").ap()
    dbg = {}
    if debug:
        for nm, rows_ in (("d_ln", D_MODEL), ("d_xs", D_INNER), ("d_dt", D_INNER),
                          ("d_gy", D_INNER), ("d_x1", D_MODEL), ("d_ln2", D_MODEL)):
            dbg[nm] = nc.dram_tensor(nm, [rows_, T], F32, kind="ExternalOutput").ap()

    scr = lambda n, sh, dt=F32: nc.dram_tensor(n, sh, dt, kind="Internal").ap()
    W_inT_bf = scr("W_inT_bf", [D_MODEL, 2 * D_INNER], BF16)
    W_outT_bf = scr("W_outT_bf", [D_INNER, D_MODEL], BF16)
    W_rT_bf = scr("W_rT_bf", [D_MODEL, D_MODEL], BF16)
    W_kT_bf = scr("W_kT_bf", [D_MODEL, D_MODEL], BF16)
    W_vT_bf = scr("W_vT_bf", [D_MODEL, D_MODEL], BF16)
    W_dtT_bf = scr("W_dtT_bf", [DT_RANK, D_INNER], BF16)
    W_xT_bf = scr("W_xT_bf", [D_INNER, 96], BF16)
    bounce = [scr(f"bounce{p}", [4, TB]) for p in range(2)]
    bbc = [scr(f"bbc{p}", [32, TB], BF16) for p in range(2)]
    a_bounce = scr("a_bounce", [1, D_STATE])

    from contextlib import ExitStack
    _st = ExitStack()
    with _st:
        xt = _st.enter_context(nc.sbuf_tensor("xt", [128, NCH_M * TB], F32))
        xre = xt.ap()[:, 0:2 * TB]
        xsqa = _st.enter_context(nc.sbuf_tensor("xsqa", [128, TB], F32))
        xsqb = _st.enter_context(nc.sbuf_tensor("xsqb", [128, TB], F32))
        lnt = _st.enter_context(nc.sbuf_tensor("lnt", [128, NCH_M * TB], BF16))
        ln2t = _st.enter_context(nc.sbuf_tensor("ln2t", [128, NCH_M * TB], BF16))
        rows = _st.enter_context(nc.sbuf_tensor("rows", [1, 2 * TB], F32))
        mrep = _st.enter_context(nc.sbuf_tensor("mrep", [128, TB], BF16))
        rrep = _st.enter_context(nc.sbuf_tensor("rrep", [128, TB], BF16))
        xrt = _st.enter_context(nc.sbuf_tensor("xrt", [128, NCH_I * (TB + 3)], BF16))
        halo = _st.enter_context(nc.sbuf_tensor("halo", [128, NCH_I * 3], BF16))
        convacc = _st.enter_context(nc.sbuf_tensor("convacc", [128, 2 * TB], F32))
        szt = _st.enter_context(nc.sbuf_tensor("szt", [128, NCH_I * TB], BF16))
        dtt = _st.enter_context(nc.sbuf_tensor("dtt", [128, NCH_I * TB], BF16))
        xdbl = _st.enter_context(nc.sbuf_tensor("xdbl", [96, TB], BF16))
        dtr = None
        brep = _st.enter_context(nc.sbuf_tensor("brep", [128, D_STATE * TB], BF16))
        crep = _st.enter_context(nc.sbuf_tensor("crep", [128, D_STATE * TB], BF16))
        vch = _st.enter_context(nc.sbuf_tensor("vch", [128, TB], BF16))
        dxsb = _st.enter_context(nc.sbuf_tensor("dxsb", [128, TB], BF16))
        abuf = _st.enter_context(nc.sbuf_tensor("abuf", [128, 2 * TB], F32))
        ube = _st.enter_context(nc.sbuf_tensor("ube", [128, TB], BF16))
        ubo = _st.enter_context(nc.sbuf_tensor("ubo", [128, TB], BF16))
        hbe = _st.enter_context(nc.sbuf_tensor("hbe", [128, TB], BF16))
        hbo = _st.enter_context(nc.sbuf_tensor("hbo", [128, TB], BF16))
        chb4 = [_st.enter_context(nc.sbuf_tensor(f"chb{i}", [128, TB], BF16))
                for i in range(4)]
        ytmp = _st.enter_context(nc.sbuf_tensor("ytmp", [128, TB], F32))
        stce = _st.enter_context(nc.sbuf_tensor("stce", [128, NCH_I * 8], F32))
        stco = _st.enter_context(nc.sbuf_tensor("stco", [128, NCH_I * 8], F32))
        gat2 = [_st.enter_context(nc.sbuf_tensor(f"gat{i}", [128, NCH_I * TB], BF16))
                for i in range(2)]
        x1t = _st.enter_context(nc.sbuf_tensor("x1t", [128, NCH_M * TB], F32))
        ln2p = _st.enter_context(nc.sbuf_tensor("ln2p", [128, TB], BF16))
        carry = _st.enter_context(nc.sbuf_tensor("carry", [128, NCH_M], BF16))
        ffntmp = _st.enter_context(nc.sbuf_tensor("ffntmp", [128, TB], BF16))
        x2t = _st.enter_context(nc.sbuf_tensor("x2t", [128, TB], F32))
        wslab = _st.enter_context(nc.sbuf_tensor("wslab", [128, 2 * 16 * 128], BF16))
        cwc = _st.enter_context(nc.sbuf_tensor("cwc", [128, NCH_I * D_CONV], F32))
        cbc = _st.enter_context(nc.sbuf_tensor("cbc", [128, NCH_I], F32))
        bdtc = _st.enter_context(nc.sbuf_tensor("bdtc", [128, NCH_I], F32))
        dc = _st.enter_context(nc.sbuf_tensor("dc", [128, NCH_I], F32))
        murc = _st.enter_context(nc.sbuf_tensor("murc", [128, NCH_M], F32))
        mukc = _st.enter_context(nc.sbuf_tensor("mukc", [128, NCH_M], F32))
        omrc = _st.enter_context(nc.sbuf_tensor("omrc", [128, NCH_M], F32))
        omkc = _st.enter_context(nc.sbuf_tensor("omkc", [128, NCH_M], F32))
        arow = _st.enter_context(nc.sbuf_tensor("arow", [1, D_STATE], F32))
        abc = _st.enter_context(nc.sbuf_tensor("abc", [128, D_STATE], F32))
        eyeb = _st.enter_context(nc.sbuf_tensor("eyeb", [128, 128], BF16))
        onesf = _st.enter_context(nc.sbuf_tensor("onesf", [128, 1], F32))
        rt = ln2t
        def kat_of(b):
            return gat2[b & 1].ap()[:, 0:NCH_M * TB]
        ps_a = _st.enter_context(nc.psum_tensor("ps_a", [128, 512], F32))
        ps_b = _st.enter_context(nc.psum_tensor("ps_b", [128, 512], F32))
        ps_c = _st.enter_context(nc.psum_tensor("ps_c", [128, 512], F32))
        ps_d = _st.enter_context(nc.psum_tensor("ps_d", [128, 512], F32))
        ps_y0 = _st.enter_context(nc.psum_tensor("ps_y0", [128, 512], F32))
        ps_y1 = _st.enter_context(nc.psum_tensor("ps_y1", [128, 512], F32))
        ps_s1 = _st.enter_context(nc.psum_tensor("ps_s1", [1, 512], F32))
        ps_s2 = _st.enter_context(nc.psum_tensor("ps_s2", [1, 512], F32))
        _q_w = _st.enter_context(nc.semaphore("q_w"))
        _q_w2 = _st.enter_context(nc.semaphore("q_w2"))
        _q_w3 = _st.enter_context(nc.semaphore("q_w3"))
        _q_x = _st.enter_context(nc.semaphore("q_x"))
        _q_g = _st.enter_context(nc.semaphore("q_g"))
        _q_o = _st.enter_context(nc.semaphore("q_o"))
        _q_o2 = _st.enter_context(nc.semaphore("q_o2"))
        _q_c = _st.enter_context(nc.semaphore("q_c"))
        _s_T = _st.enter_context(nc.semaphore("s_T"))
        _s_V = _st.enter_context(nc.semaphore("s_V"))
        _s_S = _st.enter_context(nc.semaphore("s_S"))
        _s_G = _st.enter_context(nc.semaphore("s_G"))
        block = _st.enter_context(nc.Block())
        q_w, q_x, q_g, q_o = Sem(_q_w), Sem(_q_x), Sem(_q_g), Sem(_q_o)
        q_w2, q_w3, q_o2 = Sem(_q_w2), Sem(_q_w3), Sem(_q_o2)
        q_ws = [q_w, q_w2]
        q_b = q_w3
        q_c = Sem(_q_c)
        q_os = [q_o, q_o2]
        s_T, s_V, s_S, s_G = Sem(_s_T), Sem(_s_V), Sem(_s_S), Sem(_s_G)

        class _Pending:
            __slots__ = ("incs",)
            def __init__(self):
                self.incs = []
            def then_inc(self, sem, n):
                self.incs.append((sem, n))
                return self

        def _nm(x):
            try:
                return x.tensor.name
            except AttributeError:
                return None

        _IOMAP = {
            "tensor_tensor": lambda a, kw: ((a[1], a[2]), (a[0],)),
            "tensor_scalar": lambda a, kw: ((a[1], a[2], a[3]), (a[0],)),
            "scalar_tensor_tensor": lambda a, kw: ((a[1], a[2], a[3]), (a[0],)),
            "tensor_tensor_scan": lambda a, kw: ((a[1], a[2], a[3]), (a[0],)),
            "tensor_copy": lambda a, kw: ((a[1],), (a[0],)),
            "memset": lambda a, kw: ((), (a[0],)),
            "activation": lambda a, kw: (
                (kw.get("in_"), kw.get("bias"), kw.get("scale")),
                (kw.get("out"),)),
            "matmul": lambda a, kw: ((a[1], a[2]), ()),
        }

        class _Rec:
            def __init__(self, sem=None):
                self.ops = []
                self.sem = sem
                self.waited = 0
                self.lw = {}
                self.lr = {}

            def _auto(self, meth, a, kw):
                if self.sem is None:
                    return
                if meth == "wait_ge":
                    if a and a[0] is self.sem.h and a[1] > self.waited:
                        self.waited = a[1]
                    return
                io = _IOMAP.get(meth)
                if io is None:
                    return
                rd, wr = io(a, kw)
                need = 0
                for x in rd:
                    n = _nm(x)
                    if n:
                        need = max(need, self.lw.get(n, 0))
                for x in wr:
                    n = _nm(x)
                    if n:
                        need = max(need, self.lw.get(n, 0), self.lr.get(n, 0))
                if need > self.waited:
                    self.ops.append(("wait_ge", (self.sem.h, need), {}, _Pending()))
                    self.waited = need
                cnt = self.sem.v + 1
                for x in rd:
                    n = _nm(x)
                    if n:
                        self.lr[n] = cnt
                for x in wr:
                    n = _nm(x)
                    if n:
                        self.lw[n] = cnt

            def __getattr__(self, meth):
                def call(*a, **kw):
                    self._auto(meth, a, kw)
                    p = _Pending()
                    self.ops.append((meth, a, kw, p))
                    return p
                return call

            def replay(self, eng):
                for meth, a, kw, p in self.ops:
                    instr = getattr(eng, meth)(*a, **kw)
                    for sem, n in p.incs:
                        instr.then_inc(sem, n)

        sy = _Rec()
        g = _Rec(s_G)
        t, v, s = _Rec(s_T), _Rec(s_V), _Rec(s_S)
        PS = [ps_a, ps_b, ps_c, ps_d]

        def dma(eng, q, **kw):
            eng.dma_start(**kw).then_inc(q.h, 16)
            q.v += 16

        def vi(instr):
            instr.then_inc(s_V.h, 1)
            s_V.v += 1

        def si(instr):
            instr.then_inc(s_S.h, 1)
            s_S.v += 1

        def ti(instr):
            instr.then_inc(s_T.h, 1)
            s_T.v += 1

        def gi(instr):
            instr.then_inc(s_G.h, 1)
            s_G.v += 1

        dma(sy, q_x, out=cwc[:], in_=conv_w[:])
        dma(sy, q_x, out=cbc[:], in_=conv_b[:])
        dma(sy, q_x, out=bdtc[:], in_=b_dt[:])
        dma(sy, q_x, out=dc[:], in_=D_in[:])
        dma(sy, q_x, out=murc[:], in_=mu_r[:])
        dma(sy, q_x, out=mukc[:], in_=mu_k[:])
        dma(sy, q_x, out=arow[:], in_=A_log[:])
        dma(sy, q_x, out=eyeb[:], in_=eye_in[:])
        s.wait_ge(q_x.h, q_x.v)
        si(s.activation(out=arow[:], in_=arow[:], func=AF.Exp))
        v.wait_ge(s_S.h, s_S.v)
        vi(v.tensor_scalar(arow[:], arow[:], -1.0, None, AluOpType.mult))
        g.wait_ge(s_V.h, s_V.v)
        dma(g, q_g, out=a_bounce[:], in_=arow[:])
        g.wait_ge(q_g.h, q_g.v)
        dma(g, q_g, out=abc[:], in_=a_bounce.broadcast_to((128, D_STATE)))

        v.wait_ge(q_x.h, q_x.v)
        vi(v.memset(onesf[:], 1.0))
        vi(v.memset(stce[:], 0.0))
        vi(v.memset(stco[:], 0.0))
        vi(v.memset(halo[:], 0.0))
        vi(v.memset(carry[:], 0.0))
        vi(v.tensor_scalar(omrc[:], murc[:], -1.0, 1.0, AluOpType.mult, AluOpType.add))
        vi(v.tensor_scalar(omkc[:], mukc[:], -1.0, 1.0, AluOpType.mult, AluOpType.add))

        def cast_weight(src, dst, R, C):
            for r0 in range(0, R, 128):
                sy.wait_ge(q_g.h, q_g.v)
                dma(sy, q_w, out=xt[:, :C], in_=src[r0:r0 + 128, :])
                v.wait_ge(q_w.h, q_w.v)
                vi(v.tensor_copy(gat2[0][:, :C], xt[:, :C]))
                g.wait_ge(s_V.h, s_V.v)
                dma(g, q_g, out=dst[r0:r0 + 128, :], in_=gat2[0][:, :C])

        cast_weight(W_inT, W_inT_bf, D_MODEL, 2 * D_INNER)
        cast_weight(W_outT, W_outT_bf, D_INNER, D_MODEL)
        cast_weight(W_rT, W_rT_bf, D_MODEL, D_MODEL)
        cast_weight(W_kT, W_kT_bf, D_MODEL, D_MODEL)
        cast_weight(W_vT, W_vT_bf, D_MODEL, D_MODEL)
        cast_weight(W_xT, W_xT_bf, D_INNER, 96)
        sy.wait_ge(q_g.h, q_g.v)
        dma(sy, q_w, out=xt[0:64, :D_INNER], in_=W_dtT[:, :])
        v.wait_ge(q_w.h, q_w.v)
        vi(v.tensor_copy(gat2[0][0:64, :D_INNER], xt[0:64, :D_INNER]))
        g.wait_ge(s_V.h, s_V.v)
        dma(g, q_g, out=W_dtT_bf[:, :], in_=gat2[0][0:64, :D_INNER])
        g.wait_ge(q_g.h, q_g.v)

        SLABW = 16 * 128
        slab_guard = [0, 0]
        slab_cnt = [0, 0]
        slab_rot = [0]

        def slab_load(src_bf, R, m0):
            sl = slab_rot[0]
            slab_rot[0] = (sl + 1) % 2
            nk = R // 128
            sy.wait_ge(s_T.h, slab_guard[sl])
            dma(sy, q_ws[sl],
                out=wslab[:, sl * SLABW: sl * SLABW + nk * 128]
                .rearrange("p (k c) -> p k c", c=128),
                in_=src_bf[:, m0 * 128:(m0 + 1) * 128]
                .rearrange("(k p) c -> p k c", p=128))
            slab_cnt[sl] = q_ws[sl].v
            return sl, q_ws[sl].v

        def slab_seq(loader, ms, body):
            nxt = loader(ms[0])
            for i, m in enumerate(ms):
                sl, cnt = nxt
                if i + 1 < len(ms):
                    nxt = loader(ms[i + 1])
                t.wait_ge(q_ws[sl].h, cnt)
                body(sl, m)

        def slab_load_dt():
            sl = slab_rot[0]
            slab_rot[0] = (sl + 1) % 2
            sy.wait_ge(s_T.h, slab_guard[sl])
            dma(sy, q_ws[sl], out=wslab[0:64, sl * SLABW: sl * SLABW + D_INNER],
                in_=W_dtT_bf[:, :])
            slab_cnt[sl] = q_ws[sl].v
            return sl, q_ws[sl].v

        def xs_c(c):
            return xrt[:, c * (TB + 3) + 3:(c + 1) * (TB + 3)]

        scan_g = {}
        ch_g = {}
        mm_g = {}
        ytmp_g = {}
        gate_g = {}
        vchfree = {}
        silu_g = {}
        zsilu_g = {}
        dt_g = {}
        lnap_g = {}
        x1_g = {}
        mix_g = {}
        dtr_g = {}
        brep_q = {}
        crep_q = {}
        gatdone_g = {}
        rk_g = {}
        wv_g = {}
        wvmm_g = {}
        wvm_g = {}
        winmm_g = {}
        woutmm_g = {}
        xre_q = {}
        brep_si = {}
        winev_g = {}
        out_cnt = {0: 0, 1: 0}
        ln_ctr = [0]
        conv_prev = [0]

        def ln_stats_sq(src_t):
            t.wait_ge(s_V.h, s_V.v)
            for c in range(NCH_M):
                sl = src_t[:, c * TB:(c + 1) * TB]
                xsq = xsqa if (c & 1) == 0 else xsqb
                s.wait_ge(s_T.h, s_T.v)
                si(s.activation(out=xsq[:], in_=sl, func=AF.Square))
                t.wait_ge(s_S.h, s_S.v)
                ti(t.matmul(ps_s1[:, :], onesf[:], sl, start=(c == 0), stop=(c == NCH_M - 1)))
                ti(t.matmul(ps_s2[:, :], onesf[:], xsq[:], start=(c == 0), stop=(c == NCH_M - 1)))

        def ln_stats_fin():
            par = ln_ctr[0] & 1
            ln_ctr[0] += 1
            v.wait_ge(s_T.h, s_T.v)
            r_mean = rows[0:1, 0:TB]
            r_var = rows[0:1, TB:2 * TB]
            msq = ps_s1[0:1, :]
            vi(v.tensor_scalar(r_mean, ps_s1[0:1, :], 1.0 / D_MODEL, None, AluOpType.mult))
            vi(v.tensor_scalar(r_var, ps_s2[0:1, :], 1.0 / D_MODEL, None, AluOpType.mult))
            vi(v.tensor_tensor(msq, r_mean, r_mean, AluOpType.mult))
            vi(v.tensor_tensor(r_var, r_var, msq, AluOpType.subtract))
            vi(v.tensor_scalar(r_var, r_var, EPS, None, AluOpType.add))
            s.wait_ge(s_V.h, s_V.v)
            si(s.activation(out=r_var, in_=r_var, func=AF.Ln))
            si(s.activation(out=r_var, in_=r_var, func=AF.Exp, scale=-0.5))
            g.wait_ge(s_V.h, s_V.v)
            g.wait_ge(s_S.h, s_S.v)
            dma(g, q_g, out=bounce[par][0:1, :], in_=r_mean)
            dma(g, q_g, out=bounce[par][1:2, :], in_=r_var)
            g.wait_ge(q_g.h, q_g.v)
            dma(g, q_g, out=mrep[:], in_=bounce[par][0:1, :].broadcast_to((128, TB)))
            dma(g, q_g, out=rrep[:], in_=bounce[par][1:2, :].broadcast_to((128, TB)))

        def ln_apply(src_t, dst_t):
            v.wait_ge(q_g.h, q_g.v)
            for c in range(NCH_M):
                sl = src_t[:, c * TB:(c + 1) * TB]
                dl = dst_t[:, c * TB:(c + 1) * TB]
                acc = convacc[:, (c & 1) * TB:((c & 1) + 1) * TB]
                vi(v.tensor_tensor(acc, sl, mrep[:], AluOpType.subtract))
                vi(v.tensor_tensor(dl, acc, rrep[:], AluOpType.mult))

        def fr_load_xt(j):
            def f():
                sy.wait_ge(s_V.h, s_V.v)
                sy.wait_ge(s_S.h, s_S.v)
                sy.wait_ge(s_T.h, s_T.v)
                t0j = j * TB
                for c in range(NCH_M):
                    dma(sy, q_x, out=xt[:, c * TB:(c + 1) * TB],
                        in_=xT[c * 128:(c + 1) * 128, t0j:t0j + TB])
            return f

        def fr_ln1_sq(j):
            def f():
                s.wait_ge(q_x.h, q_x.v)
                t.wait_ge(q_x.h, q_x.v)
                ln_stats_sq(xt)
            return f

        def fr_ln1_fin(j):
            def f():
                ln_stats_fin()
            return f

        def fr_ln1_apply(j):
            def f():
                if debug:
                    v.wait_ge(q_g.h, q_g.v)
                ln_apply(xt, lnt)
                lnap_g[("1", j)] = s_V.v
                if debug:
                    g.wait_ge(s_V.h, s_V.v)
                    for c in range(NCH_M):
                        dma(g, q_g, out=dbg["d_ln"][c * 128:(c + 1) * 128,
                                                    j * TB:(j + 1) * TB],
                            in_=lnt[:, c * TB:(c + 1) * TB])
            return f

        def fr_win_xr_mm(j, ms, k_cur):
            def f():
                t.wait_ge(s_V.h, lnap_g[("1", j)])
                t.wait_ge(s_S.h, dt_g.get((j - 1, NCH_I - 1), 0))
                def body(sl, m):
                    ps = PS[2 + (m & 1)]
                    t.wait_ge(s_V.h, winev_g.get((j, m - 2), 0))
                    if k_cur is not None:
                        t.wait_ge(s_V.h, vchfree.get((k_cur, m), 0))
                    for kk in range(NCH_M):
                        ti(t.matmul(ps[:, :],
                                    wslab[:, sl * SLABW + kk * 128: sl * SLABW + (kk + 1) * 128],
                                    lnt[:, kk * TB:(kk + 1) * TB],
                                    start=(kk == 0), stop=(kk == NCH_M - 1)))
                    slab_guard[sl] = s_T.v
                    winmm_g[(j, m)] = s_T.v
                slab_seq(lambda m: slab_load(W_inT_bf, D_MODEL, m), ms, body)
            return f

        def fr_win_xr_ev(j, ms, k_cur):
            def f():
                if debug:
                    v.wait_ge(q_g.h, q_g.v)
                for m in ms:
                    ps = PS[2 + (m & 1)]
                    v.wait_ge(s_T.h, winmm_g[(j, m)])
                    vi(v.tensor_copy(xrt[:, m * (TB + 3) + 3:(m + 1) * (TB + 3)], ps[:, :]))
                    winev_g[(j, m)] = s_V.v
            return f

        def fr_win_z(j, js, k_cur):
            def f():
                t.wait_ge(s_V.h, lnap_g[("1", j)])
                def body(sl, jz):
                    t.wait_ge(s_S.h, zsilu_g.get((j, jz - 1),
                                                 zsilu_g.get((j - 1, NCH_I - 1), 0)))
                    if k_cur is not None:
                        t.wait_ge(s_V.h, gate_g.get((k_cur, jz), 0))
                    for kk in range(NCH_M):
                        ti(t.matmul(ps_y1[:, :],
                                    wslab[:, sl * SLABW + kk * 128: sl * SLABW + (kk + 1) * 128],
                                    lnt[:, kk * TB:(kk + 1) * TB],
                                    start=(kk == 0), stop=(kk == NCH_M - 1)))
                    slab_guard[sl] = s_T.v
                    s.wait_ge(s_T.h, s_T.v)
                    si(s.activation(out=szt[:, jz * TB:(jz + 1) * TB], in_=ps_y1[:, :],
                                    func=AF.Silu))
                    zsilu_g[(j, jz)] = s_S.v
                slab_seq(lambda jz: slab_load(W_inT_bf, D_MODEL, NCH_I + jz), js, body)
            return f

        def fr_conv(j, cs):
            def f():
                for c in cs:
                    xr_ch = xrt[:, c * (TB + 3):(c + 1) * (TB + 3)]
                    acc0 = convacc[:, 0:TB]
                    acc1 = convacc[:, TB:2 * TB]
                    v.wait_ge(s_S.h, conv_prev[0])
                    vi(v.tensor_copy(xr_ch[:, 0:3], halo[:, c * 3:(c + 1) * 3]))
                    vi(v.tensor_scalar(acc0, xr_ch[:, 0:TB], cwc[:, c * 4:c * 4 + 1],
                                       None, AluOpType.mult))
                    vi(v.scalar_tensor_tensor(acc1, xr_ch[:, 1:1 + TB],
                                              cwc[:, c * 4 + 1:c * 4 + 2], acc0,
                                              AluOpType.mult, AluOpType.add))
                    vi(v.scalar_tensor_tensor(acc0, xr_ch[:, 2:2 + TB],
                                              cwc[:, c * 4 + 2:c * 4 + 3], acc1,
                                              AluOpType.mult, AluOpType.add))
                    vi(v.scalar_tensor_tensor(acc1, xr_ch[:, 3:3 + TB],
                                              cwc[:, c * 4 + 3:c * 4 + 4], acc0,
                                              AluOpType.mult, AluOpType.add))
                    vi(v.tensor_copy(halo[:, c * 3:(c + 1) * 3], xr_ch[:, TB:TB + 3]))
                    s.wait_ge(s_V.h, s_V.v)
                    si(s.activation(out=xr_ch[:, 3:3 + TB], in_=acc1, func=AF.Silu,
                                    bias=cbc[:, c:c + 1]))
                    silu_g[(j, c)] = s_S.v
                    conv_prev[0] = s_S.v
            return f

        def fr_wx_wdt(j):
            def f():
                slx = slab_rot[0]
                slab_rot[0] = (slx + 1) % 2
                sy.wait_ge(s_T.h, slab_guard[slx])
                dma(sy, q_ws[slx],
                    out=wslab[:, slx * SLABW: slx * SLABW + NCH_I * 96]
                    .rearrange("p (k c) -> p k c", c=96),
                    in_=W_xT_bf.rearrange("(k p) c -> p k c", p=128))
                slab_cnt[slx] = q_ws[slx].v
                t.wait_ge(s_S.h, silu_g[(j, NCH_I - 1)])
                t.wait_ge(s_S.h, dt_g.get((j - 1, NCH_I - 1), 0))
                t.wait_ge(s_V.h, s_V.v)
                t.wait_ge(q_ws[slx].h, slab_cnt[slx])
                for kk in range(NCH_I):
                    ti(t.matmul(ps_c[0:96, :],
                                wslab[:, slx * SLABW + kk * 96: slx * SLABW + (kk + 1) * 96],
                                xs_c(kk),
                                start=(kk == 0), stop=(kk == NCH_I - 1)))
                slab_guard[slx] = s_T.v
                v.wait_ge(s_T.h, s_T.v)
                v.wait_ge(q_g.h, q_g.v)
                vi(v.tensor_copy(xdbl[:, :], ps_c[0:96, :]))
                dtr_g[j] = s_V.v
                par = j & 1
                g.wait_ge(s_V.h, s_V.v)
                g.wait_ge(q_g.h, q_g.v)
                dma(g, q_g, out=bbc[par][:, :], in_=xdbl[64:96, :])
                g.wait_ge(q_g.h, q_g.v)
                for si_ in range(D_STATE):
                    dma(g, q_b, out=brep[:, si_ * TB:(si_ + 1) * TB],
                        in_=bbc[par][si_:si_ + 1, :].broadcast_to((128, TB)))
                brep_q[j] = q_b.v
                for si_ in range(D_STATE):
                    dma(g, q_c, out=crep[:, si_ * TB:(si_ + 1) * TB],
                        in_=bbc[par][16 + si_:17 + si_, :].broadcast_to((128, TB)))
                crep_q[j] = q_c.v
                sl, _cnt = slab_load_dt()
                if debug:
                    s.wait_ge(q_g.h, q_g.v)
                t.wait_ge(s_V.h, s_V.v)
                t.wait_ge(q_ws[sl].h, _cnt)
                for m in range(NCH_I):
                    ps = PS[2 + (m & 1)]
                    xsq = xsqb if (m & 1) == 0 else xsqa
                    if m >= 2:
                        t.wait_ge(s_S.h, dt_g[(j, m - 2)])
                    ti(t.matmul(ps[:, :], wslab[0:64, sl * SLABW + m * 128:
                                                sl * SLABW + (m + 1) * 128],
                                xdbl[0:64, :], start=True, stop=True))
                    s.wait_ge(s_T.h, s_T.v)
                    si(s.activation(out=xsq[:], in_=ps[:, :], func=AF.Exp,
                                    bias=bdtc[:, m:m + 1]))
                    si(s.activation(out=dtt[:, m * TB:(m + 1) * TB], in_=xsq[:],
                                    func=AF.Ln, bias=1.0))
                    dt_g[(j, m)] = s_S.v
                slab_guard[sl] = s_T.v
            return f

        def bk_wout_mm(b, ms):
            def f():
                t0b = b * TB
                sy.wait_ge(s_V.h, s_V.v)
                for m in ms:
                    dma(sy, q_x, out=xre[:, (m & 1) * TB:((m & 1) + 1) * TB],
                        in_=xT[m * 128:(m + 1) * 128, t0b:t0b + TB])
                xre_q[ms[0]] = q_x.v
                t.wait_ge(s_V.h, gatdone_g[b])
                t.wait_ge(s_V.h, wv_g.get(b - 1, 0))
                def body(sl, m):
                    ps = PS[m & 1]
                    t.wait_ge(s_V.h, x1_g.get((b, m - 2), 0))
                    for kk in range(NCH_I):
                        ti(t.matmul(ps[:, :],
                                    wslab[:, sl * SLABW + kk * 128: sl * SLABW + (kk + 1) * 128],
                                    gat2[b & 1][:, kk * TB:(kk + 1) * TB],
                                    start=(kk == 0), stop=(kk == NCH_I - 1)))
                    slab_guard[sl] = s_T.v
                    woutmm_g[(b, m)] = s_T.v
                slab_seq(lambda m: slab_load(W_outT_bf, D_INNER, m), ms, body)
            return f

        def bk_wout_x1(b, ms):
            def f():
                if debug:
                    v.wait_ge(q_g.h, q_g.v)
                for m in ms:
                    ps = PS[m & 1]
                    v.wait_ge(s_T.h, woutmm_g[(b, m)])
                    v.wait_ge(q_x.h, xre_q[ms[0]])
                    vi(v.tensor_tensor(x1t[:, m * TB:(m + 1) * TB], ps[:, :],
                                       xre[:, (m & 1) * TB:((m & 1) + 1) * TB],
                                       AluOpType.add))
                    x1_g[(b, m)] = s_V.v
                if debug and ms[-1] == NCH_M - 1:
                    g.wait_ge(s_V.h, s_V.v)
                    for c in range(NCH_M):
                        dma(g, q_g, out=dbg["d_x1"][c * 128:(c + 1) * 128,
                                                    b * TB:(b + 1) * TB],
                            in_=x1t[:, c * TB:(c + 1) * TB])
            return f

        def bk_ln2_sq(b):
            def f():
                ln_stats_sq(x1t)
            return f

        def bk_ln2_fin(b):
            def f():
                ln_stats_fin()
            return f

        def bk_ln2_apply(b):
            def f():
                ln_apply(x1t, ln2t)
                lnap_g[("2", b)] = s_V.v
                if debug:
                    g.wait_ge(s_V.h, s_V.v)
                    for c in range(NCH_M):
                        dma(g, q_g, out=dbg["d_ln2"][c * 128:(c + 1) * 128,
                                                     b * TB:(b + 1) * TB],
                            in_=ln2t[:, c * TB:(c + 1) * TB])
            return f

        def bk_ffn_mix(b, cs):
            xft = gat2[b & 1].ap()[:, 0:NCH_M * TB]
            xkt = gat2[b & 1].ap()[:, NCH_M * TB:2 * NCH_M * TB]
            def f():
                for c in cs:
                    l2 = ln2t[:, c * TB:(c + 1) * TB]
                    vi(v.tensor_copy(ln2p[:, 0:1], carry[:, c:c + 1]))
                    vi(v.tensor_copy(ln2p[:, 1:TB], l2[:, 0:TB - 1]))
                    vi(v.tensor_copy(carry[:, c:c + 1], l2[:, TB - 1:TB]))
                    xf = xft[:, c * TB:(c + 1) * TB]
                    xk = xkt[:, c * TB:(c + 1) * TB]
                    vi(v.tensor_scalar(xf, l2, murc[:, c:c + 1], None, AluOpType.mult))
                    vi(v.scalar_tensor_tensor(ffntmp[:], ln2p[:], omrc[:, c:c + 1], xf,
                                              AluOpType.mult, AluOpType.add))
                    vi(v.tensor_copy(xf, ffntmp[:]))
                    vi(v.tensor_scalar(xk, l2, mukc[:, c:c + 1], None, AluOpType.mult))
                    vi(v.scalar_tensor_tensor(ffntmp[:], ln2p[:], omkc[:, c:c + 1], xk,
                                              AluOpType.mult, AluOpType.add))
                    vi(v.tensor_copy(xk, ffntmp[:]))
                mix_g[b] = s_V.v
            return f

        def bk_wrk(b, which, ms):
            wt = W_rT_bf if which == "r" else W_kT_bf
            off = 0 if which == "r" else NCH_M * TB
            def f():
                src = gat2[b & 1].ap()[:, off:off + NCH_M * TB]
                if debug:
                    s.wait_ge(q_g.h, q_g.v)
                t.wait_ge(s_V.h, mix_g[b])
                def body(sl, m):
                    ps = PS[m & 1]
                    if which == "r":
                        t.wait_ge(s_V.h, x1_g[(b, 6 + (m & 1))])
                        t.wait_ge(s_S.h, rk_g.get((b, "r", m - 2), 0))
                    else:
                        prev = rk_g.get((b, "k", m - 2), rk_g.get((b, "r", 6 + (m & 1)), 0))
                        t.wait_ge(s_S.h, prev)
                    for kk in range(NCH_M):
                        ti(t.matmul(ps[:, :],
                                    wslab[:, sl * SLABW + kk * 128: sl * SLABW + (kk + 1) * 128],
                                    src[:, kk * TB:(kk + 1) * TB],
                                    start=(kk == 0), stop=(kk == NCH_M - 1)))
                    slab_guard[sl] = s_T.v
                    s.wait_ge(s_T.h, s_T.v)
                    if which == "r":
                        si(s.activation(out=rt[:, m * TB:(m + 1) * TB], in_=ps[:, :],
                                        func=AF.Sigmoid))
                    else:
                        xsq = xsqa if (m & 1) == 0 else xsqb
                        si(s.activation(out=xsq[:], in_=ps[:, :], func=AF.Relu))
                        si(s.activation(out=kat_of(b)[:, m * TB:(m + 1) * TB],
                                        in_=xsq[:], func=AF.Square))
                    rk_g[(b, which, m)] = s_S.v
                slab_seq(lambda m: slab_load(wt, D_MODEL, m), ms, body)
            return f

        def bk_wv_mm(b, ms):
            def f():
                t.wait_ge(s_S.h, rk_g[(b, "k", NCH_M - 1)])
                def body(sl, m):
                    ps = PS[m & 1]
                    t.wait_ge(s_S.h, rk_g.get((b, "k", 6 + (m & 1)), 0))
                    t.wait_ge(s_V.h, wv_g.get((b, m - 2), 0))
                    for kk in range(NCH_M):
                        ti(t.matmul(ps[:, :],
                                    wslab[:, sl * SLABW + kk * 128: sl * SLABW + (kk + 1) * 128],
                                    kat_of(b)[:, kk * TB:(kk + 1) * TB],
                                    start=(kk == 0), stop=(kk == NCH_M - 1)))
                    slab_guard[sl] = s_T.v
                    wvm_g[(b, m)] = s_T.v
                slab_seq(lambda m: slab_load(W_vT_bf, D_MODEL, m), ms, body)
                if ms[-1] == NCH_M - 1:
                    wvmm_g[b] = s_T.v
            return f

        def bk_wv_x2(b, ms):
            t0b = b * TB
            def f():
                for m in ms:
                    v.wait_ge(s_T.h, wvm_g[(b, m)])
                    v.wait_ge(q_os[m & 1].h, out_cnt[m & 1])
                    x2b = x2t[:, 0:TB] if (m & 1) == 0 else ytmp[:, 0:TB]
                    vi(v.tensor_tensor(ffntmp[:], PS[m & 1][:, :],
                                       rt[:, m * TB:(m + 1) * TB], AluOpType.mult))
                    vi(v.tensor_tensor(x2b, ffntmp[:], x1t[:, m * TB:(m + 1) * TB],
                                       AluOpType.add))
                    wv_g[(b, m)] = s_V.v
                    sy.wait_ge(s_V.h, s_V.v)
                    dma(sy, q_os[m & 1], out=out[m * 128:(m + 1) * 128, t0b:t0b + TB],
                        in_=x2b)
                    out_cnt[m & 1] = q_os[m & 1].v
                if ms[-1] == NCH_M - 1:
                    wv_g[b] = s_V.v
            return f

        UB = [ube, ubo]
        HB = [hbe, hbo]

        def scan_section(k, c):
            c_abs = k * NCH_I + c
            dt_ch = dtt[:, c * TB:(c + 1) * TB]
            v.wait_ge(s_S.h, dt_g[(k, c)])
            v.wait_ge(s_S.h, silu_g[(k, c)])
            if c == 0:
                v.wait_ge(q_os[1].h, out_cnt[1])
                v.wait_ge(s_T.h, wvmm_g.get(k - 2, 0))
            vi(v.tensor_tensor(vch[:], dt_ch, xs_c(c), AluOpType.mult))
            vi(v.tensor_scalar(dxsb[:], xs_c(c), dc[:, c:c + 1], None, AluOpType.mult))
            vchfree[(k, c)] = s_V.v
            if c == 0:
                v.wait_ge(q_b.h, brep_q[k])
            vi(v.tensor_tensor(UB[0][:], vch[:], brep[:, 0:TB], AluOpType.mult))

            def tail(j):
                kj = c_abs * D_STATE + j
                v.wait_ge(s_T.h, mm_g.get(kj - 4, 0))
                if c == 0 and j == 0:
                    v.wait_ge(q_c.h, crep_q[k])
                chb = chb4[j & 3]
                vi(v.tensor_tensor(chb[:], HB[j & 1][:], crep[:, j * TB:(j + 1) * TB],
                                   AluOpType.mult))
                ch_g[kj] = s_V.v
                stx = stce if (j & 1) == 0 else stco
                stcol = stx[:, c * 8 + (j >> 1): c * 8 + (j >> 1) + 1]
                vi(v.tensor_scalar(stcol, HB[j & 1][:, TB - 1:TB], 1.0, None,
                                   AluOpType.mult))
                if j == 0:
                    t.wait_ge(s_V.h, ytmp_g.get((k, c - 1) if c else (k - 1, NCH_I - 1), 0))
                t.wait_ge(s_V.h, ch_g[kj])
                ti(t.matmul(ps_y0[:, :], eyeb[:], chb[:],
                            start=(j == 0), stop=(j == D_STATE - 1)))
                mm_g[kj] = s_T.v

            for si_ in range(D_STATE):
                kj = c_abs * D_STATE + si_
                ab = abuf[:, (si_ & 1) * TB:((si_ & 1) + 1) * TB]
                s.wait_ge(s_V.h, scan_g.get(kj - 2, 0))
                si(s.activation(out=ab, in_=dt_ch, func=AF.Exp,
                                scale=abc[:, si_:si_ + 1]))
                ab_cnt = s_S.v
                if si_ + 1 < D_STATE:
                    vi(v.tensor_tensor(UB[(si_ + 1) & 1][:], vch[:],
                                       brep[:, (si_ + 1) * TB:(si_ + 2) * TB],
                                       AluOpType.mult))
                stx = stce if (si_ & 1) == 0 else stco
                stcol = stx[:, c * 8 + (si_ >> 1): c * 8 + (si_ >> 1) + 1]
                v.wait_ge(s_S.h, ab_cnt)
                scan_g[kj] = s_V.v + 1
                vi(v.tensor_tensor_scan(HB[si_ & 1][:], ab, UB[si_ & 1][:], stcol,
                                        AluOpType.mult, AluOpType.add))
                if si_ > 0:
                    tail(si_ - 1)
            tail(D_STATE - 1)
            v.wait_ge(s_T.h, mm_g[c_abs * D_STATE + D_STATE - 1])
            v.wait_ge(s_S.h, zsilu_g[(k, c)])
            vi(v.tensor_tensor(ytmp[:], ps_y0[:, :], dxsb[:], AluOpType.add))
            ytmp_g[(k, c)] = s_V.v
            vi(v.tensor_tensor(gat2[k & 1][:, c * TB:(c + 1) * TB], ytmp[:],
                               szt[:, c * TB:(c + 1) * TB], AluOpType.mult))
            gate_g[(k, c)] = s_V.v
            if c == NCH_I - 1:
                gatdone_g[k] = s_V.v
            if debug:
                g.wait_ge(s_V.h, s_V.v)
                t0k = k * TB
                dma(g, q_g, out=dbg["d_xs"][c * 128:(c + 1) * 128, t0k:t0k + TB],
                    in_=xs_c(c))
                dma(g, q_g, out=dbg["d_dt"][c * 128:(c + 1) * 128, t0k:t0k + TB],
                    in_=dt_ch)
                dma(g, q_g, out=dbg["d_gy"][c * 128:(c + 1) * 128, t0k:t0k + TB],
                    in_=gat2[k & 1][:, c * TB:(c + 1) * TB])

        def front_agenda(j, k_cur, ag, tailp):
            ag[4].append(fr_load_xt(j))
            ag[5].append(fr_ln1_sq(j))
            ag[6].append(fr_ln1_fin(j))
            ag[7].append(fr_ln1_apply(j))
            ag[8].append(fr_win_xr_mm(j, [0, 1], k_cur))
            ag[9].append(fr_win_xr_ev(j, [0, 1], k_cur))
            ag[9].append(fr_win_xr_mm(j, [2, 3], k_cur))
            ag[10].append(fr_win_xr_ev(j, [2, 3], k_cur))
            ag[10].append(fr_win_xr_mm(j, [4, 5], k_cur))
            ag[11].append(fr_win_xr_ev(j, [4, 5], k_cur))
            ag[11].append(fr_win_xr_mm(j, [6, 7], k_cur))
            ag[12].append(fr_win_xr_ev(j, [6, 7], k_cur))
            ag[12].append(fr_win_xr_mm(j, [8, 9], k_cur))
            ag[13].append(fr_win_xr_ev(j, [8, 9], k_cur))
            ag[13].append(fr_win_xr_mm(j, [10, 11], k_cur))
            ag[14].append(fr_win_xr_ev(j, [10, 11], k_cur))
            ag[14].append(fr_win_xr_mm(j, [12, 13], k_cur))
            ag[15].append(fr_win_xr_ev(j, [12, 13], k_cur))
            ag[15].append(fr_win_xr_mm(j, [14], k_cur))
            ag[10].append(fr_conv(j, [0, 1, 2, 3]))
            ag[10].append(fr_win_z(j, [0, 1, 2, 3], k_cur))
            ag[12].append(fr_win_z(j, [4, 5, 6, 7], k_cur))
            ag[13].append(fr_conv(j, [4, 5, 6, 7, 8]))
            ag[15].append(fr_conv(j, [9, 10, 11, 12, 13]))
            ag[15].append(fr_win_z(j, [8, 9, 10, 11, 12, 13, 14], k_cur))
            tailp.append(fr_win_xr_ev(j, [14], k_cur))
            tailp.append(fr_win_xr_mm(j, [15], k_cur))
            tailp.append(fr_win_xr_ev(j, [15], k_cur))
            tailp.append(fr_conv(j, [14, 15]))
            tailp.append(fr_win_z(j, [15], k_cur))
            tailp.append(fr_wx_wdt(j))

        fr_load_xt(0)()
        fr_ln1_sq(0)()
        fr_ln1_fin(0)()
        fr_ln1_apply(0)()
        for m0 in range(0, NCH_I, 2):
            fr_win_xr_mm(0, [m0, m0 + 1], None)()
            fr_win_xr_ev(0, [m0, m0 + 1], None)()
        for c0 in range(0, NCH_I, 2):
            fr_conv(0, [c0, c0 + 1])()
        for j0 in range(0, NCH_I, 2):
            fr_win_z(0, [j0, j0 + 1], None)()
        fr_wx_wdt(0)()

        for k in range(NB):
            ag = {cc: [] for cc in range(NCH_I)}
            tailp = []
            b = k - 1
            if b >= 0:
                ag[0].append(bk_wout_mm(b, [0, 1]))
                ag[1].append(bk_wout_x1(b, [0, 1]))
                ag[1].append(bk_wout_mm(b, [2, 3]))
                ag[2].append(bk_wout_x1(b, [2, 3]))
                ag[2].append(bk_wout_mm(b, [4, 5]))
                ag[3].append(bk_wout_x1(b, [4, 5]))
                ag[3].append(bk_wout_mm(b, [6, 7]))
                ag[4].append(bk_wout_x1(b, [6, 7]))
                ag[4].append(bk_ln2_sq(b))
                ag[5].append(bk_ln2_fin(b))
                ag[6].append(bk_ln2_apply(b))
                ag[7].append(bk_ffn_mix(b, [0, 1, 2, 3, 4, 5, 6, 7]))
                ag[8].append(bk_wrk(b, "r", [0, 1, 2, 3]))
                ag[10].append(bk_wrk(b, "r", [4, 5, 6, 7]))
                ag[12].append(bk_wrk(b, "k", [0, 1, 2, 3]))
                ag[13].append(bk_wrk(b, "k", [4, 5, 6, 7]))
                ag[14].append(bk_wv_mm(b, [0, 1]))
                ag[15].append(bk_wv_x2(b, [0, 1]))
                ag[15].append(bk_wv_mm(b, [2, 3]))
                wvp = [bk_wv_x2(b, [2, 3]), bk_wv_mm(b, [4, 5]),
                       bk_wv_x2(b, [4, 5]), bk_wv_mm(b, [6, 7]),
                       bk_wv_x2(b, [6, 7])]
            else:
                wvp = []
            j = k + 1
            if j < NB:
                front_agenda(j, k, ag, tailp)
            if tailp and tailp[-1].__name__ == "f":
                pass
            wx = [fn for fn in tailp if fn.__qualname__.startswith("build_program.<locals>.fr_wx_wdt")]
            pre = [fn for fn in tailp if fn not in wx]
            tailp = pre + wvp + wx

            for c in range(NCH_I):
                scan_section(k, c)
                for fn in ag[c]:
                    fn()
            for fn in tailp:
                fn()

        b = NB - 1
        for m0 in range(0, NCH_M, 2):
            bk_wout_mm(b, [m0, m0 + 1])()
            bk_wout_x1(b, [m0, m0 + 1])()
        bk_ln2_sq(b)()
        bk_ln2_fin(b)()
        bk_ln2_apply(b)()
        bk_ffn_mix(b, list(range(NCH_M)))()
        for m0 in range(0, NCH_M, 2):
            bk_wrk(b, "r", [m0, m0 + 1])()
        for m0 in range(0, NCH_M, 2):
            bk_wrk(b, "k", [m0, m0 + 1])()
        for m0 in range(0, NCH_M, 2):
            bk_wv_mm(b, [m0, m0 + 1])()
            bk_wv_x2(b, [m0, m0 + 1])()

        g.wait_ge(q_o.h, q_o.v)
        g.wait_ge(q_o2.h, q_o2.v)
        if debug:
            g.wait_ge(q_g.h, q_g.v)

        @block.sync
        def _(e):
            sy.replay(e)

        @block.gpsimd
        def _(e):
            g.replay(e)

        @block.tensor
        def _(e):
            t.replay(e)

        @block.vector
        def _(e):
            v.replay(e)

        @block.scalar
        def _(e):
            s.replay(e)

    return nc


_PROG_CACHE = {}


def _get_prog(T, TB, debug=False):
    key = (T, TB, debug)
    if key not in _PROG_CACHE:
        _PROG_CACHE[key] = build_program(T, TB, debug)
    return _PROG_CACHE[key]


def make_in_map(b, T, L0_attn_x, L0_attn_W_in, L0_attn_conv_w, L0_attn_conv_b,
                L0_attn_A_log, L0_attn_D, L0_attn_W_x, L0_attn_W_dt, L0_attn_b_dt,
                L0_attn_W_out, L0_ffn_W_r, L0_ffn_W_k, L0_ffn_W_v, L0_ffn_mu_r,
                L0_ffn_mu_k):
    C = np.ascontiguousarray
    f = np.float32
    _bf16 = ml_dtypes.bfloat16
    return {
        "xT": C(np.asarray(L0_attn_x)[b, :T, :].T.astype(f)),
        "W_inT": C(np.asarray(L0_attn_W_in).T.astype(f)),
        "conv_wc": C(np.asarray(L0_attn_conv_w).reshape(NCH_I, 128, 4)
                     .transpose(1, 0, 2).reshape(128, NCH_I * 4).astype(f)),
        "conv_bc": C(np.asarray(L0_attn_conv_b).reshape(NCH_I, 128).T.astype(f)),
        "A_log_r": C(np.asarray(L0_attn_A_log).reshape(1, D_STATE).astype(f)),
        "D_c": C(np.asarray(L0_attn_D).reshape(NCH_I, 128).T.astype(f)),
        "W_xT": C(np.asarray(L0_attn_W_x).T.astype(f)),
        "W_dtT": C(np.asarray(L0_attn_W_dt).T.astype(f)),
        "b_dtc": C(np.asarray(L0_attn_b_dt).reshape(NCH_I, 128).T.astype(f)),
        "W_outT": C(np.asarray(L0_attn_W_out).T.astype(f)),
        "W_rT": C(np.asarray(L0_ffn_W_r).T.astype(f)),
        "W_kT": C(np.asarray(L0_ffn_W_k).T.astype(f)),
        "W_vT": C(np.asarray(L0_ffn_W_v).T.astype(f)),
        "mu_rc": C(np.asarray(L0_ffn_mu_r).reshape(NCH_M, 128).T.astype(f)),
        "mu_kc": C(np.asarray(L0_ffn_mu_k).reshape(NCH_M, 128).T.astype(f)),
        "eye_in": np.eye(128, dtype=_bf16),
    }


def kernel(**inputs):
    T = 8192
    nc = _get_prog(T, 512)
    in_maps = [make_in_map(c >> 1, T, **inputs) for c in range(8)]
    res = run_bass_kernel_spmd(nc, in_maps, core_ids=list(range(8)))
    B = 4
    outs = []
    for b in range(B):
        o = res.results[2 * b]["out"]
        outs.append(np.ascontiguousarray(o.T)[None])
    return np.concatenate(outs, axis=0)
